# version 11
# baseline (speedup 1.0000x reference)
import os
import numpy as np
import ml_dtypes
from contextlib import ExitStack

_NBLOCKS = int(os.environ.get("KBLOCKS", "3"))

import concourse.bass as bass
import concourse.mybir as mybir
from concourse.bass_utils import run_bass_kernel_spmd
from concourse.alu_op_type import AluOpType as OP

F32 = mybir.dt.float32
BF16 = mybir.dt.bfloat16
AF = mybir.ActivationFunctionType
AX = mybir.AxisListType

B, L, H = 2, 16384, 2048
IZ = 4096
NS = 128
NH, PH = 64, 64
KC = 4
XC = IZ + 2 * NS
XT = XC // 128
PROJ = 2 * (IZ + NS) + NH
SPAN = L // 4
T = 512
NTL = SPAN // T
CH = 128
NCH = SPAN // CH
CPT = T // CH
HK = H // 128
EPS = 1e-5
BIGNEG = 1e30

_DEBUG = False


class Prog:
    ENGS = ("tensor", "vector", "scalar", "gpsimd", "sync")

    def __init__(self, nc, tag):
        self.nc, self.tag = nc, tag
        self.q = {e: [] for e in self.ENGS}
        self.sems, self.cnt = {}, {}
        self.wm = {e: {} for e in self.ENGS}

    def _sem(self, name):
        key = f"{self.tag}_{name}"
        if key not in self.sems:
            self.sems[key] = self.nc.alloc_semaphore(key)
            self.cnt[key] = 0
        return key

    def op(self, eng, fname, *args, deps=(), inc=None, incamt=1, **kw):
        dl = []
        for (s, v) in deps:
            if v and v > 0:
                dl.append((self._sem(s), v))
        if inc is not None:
            inc = self._sem(inc)
            self.cnt[inc] += incamt
        self.q[eng].append((dl, fname, args, kw, inc, incamt))
        return self.cnt[inc] if inc else None

    def dma(self, eng, out, in_, inc, deps=(), fence=True, **kw):
        deps = list(deps)
        if fence and inc is not None:
            deps.append((inc, self.cnt.get(f"{self.tag}_{inc}", 0)))
        return self.op(eng, "dma_start", deps=deps, inc=inc, incamt=16,
                       out=out, in_=in_, **kw)

    def mark(self, name):
        return self.cnt.get(f"{self.tag}_{name}", 0)

    def drain(self):
        deps = [(k[len(self.tag) + 1:], v) for k, v in self.cnt.items()]
        self.op("sync", "nop", deps=deps)

    def emit(self):
        with self.nc.Block(self.tag) as block:
            for eng in self.ENGS:
                items = self.q[eng]
                if not items:
                    continue

                def body(E, items=items, eng=eng):
                    wm = self.wm[eng]
                    for dl, fname, args, kw, inc, incamt in items:
                        for (s, v) in dl:
                            if v > wm.get(s, 0):
                                E.wait_ge(self.sems[s], v)
                                wm[s] = v
                        inst = getattr(E, fname)(*args, **kw)
                        if inc:
                            inst.then_inc(self.sems[inc], incamt)
                getattr(block, eng)(body)


def b3(ap, a, b_):
    return ap.rearrange("p (a b) -> p a b", a=a)


def build_nc():
    nc = bass.Bass("TRN2", target_bir_lowering=False)
    g = {}

    def din(name, shape, dt=F32):
        g[name] = nc.dram_tensor(name, shape, dt, kind="ExternalInput").ap()
    din("x_span", [SPAN, H]); din("x_halo", [128, H])
    din("w_in", [PROJ, H]); din("conv_w", [XC, 1, KC]); din("conv_b", [XC])
    din("dt_bias", [NH]); din("a_log", [NH]); din("d_in", [NH]); din("norm_w", [IZ])
    din("w_out", [H, IZ]); din("w_ffn", [H, H]); din("ffn_scale", [H]); din("ffn_bias", [H])
    din("ident_bf", [128, 128], BF16); din("ident_f", [128, 128])
    din("mask_tril", [128, 128]); din("cmb", [9, 8])
    g["out_span"] = nc.dram_tensor("out_span", [SPAN, H], F32, kind="ExternalOutput").ap()

    sk = "ExternalOutput" if _DEBUG else "Internal"
    g["w_in_bf"] = nc.dram_tensor("w_in_bf", [PROJ, H], BF16).ap()
    g["w_out_bf"] = nc.dram_tensor("w_out_bf", [H, IZ], BF16).ap()
    g["w_ffn_bf"] = nc.dram_tensor("w_ffn_bf", [H, H], BF16).ap()
    g["xn_sp"] = nc.dram_tensor("xn_sp", [HK, 128, SPAN], BF16, kind=sk).ap()
    g["z_sp"] = nc.dram_tensor("z_sp", [NCH, 128, IZ], BF16, kind=sk).ap()
    g["g_sp"] = nc.dram_tensor("g_sp", [NCH, 128, H], BF16, kind=sk).ap()
    g["yp_sp"] = nc.dram_tensor("yp_sp", [NCH, 128, IZ], BF16, kind=sk).ap()
    g["acs_row"] = nc.dram_tensor("acs_row", [NCH, CH * NH], F32).ap()
    g["wx_t"] = nc.dram_tensor("wx_t", [9, HK, 128, 512], BF16).ap()
    g["wo_t"] = nc.dram_tensor("wo_t", [4, IZ // 128, 128, 512], BF16).ap()
    g["wz_t"] = nc.dram_tensor("wz_t", [8, HK, 128, 512], BF16).ap()
    g["wf_t"] = nc.dram_tensor("wf_t", [4, HK, 128, 512], BF16).ap()
    g["w_row"] = nc.dram_tensor("w_row", [8, NH], F32).ap()
    g["nwrow_d"] = nc.dram_tensor("nwrow_d", [IZ], F32).ap()
    g["ag_in"] = nc.dram_tensor("ag_in", [128, IZ + 64], F32).ap()
    g["ag_out"] = nc.dram_tensor("ag_out", [8 * 128, IZ + 64], F32).ap()
    if _DEBUG:
        g["dbg_e"] = nc.dram_tensor("dbg_e", [128, IZ + 64], F32, kind="ExternalOutput").ap()

    sb = nc.alloc_sbuf_tensor
    g["ident_bf_s"] = sb("ident_bf_s", [128, 128], BF16)
    g["ident_f_s"] = sb("ident_f_s", [128, 128], F32)
    g["mask_s"] = sb("mask_s", [128, 128], F32)
    g["ones_bf"] = sb("ones_bf", [1, 128], BF16)
    g["ones_f"] = sb("ones_f", [1, 128], F32)
    g["ones64"] = sb("ones64", [NH, 128], F32)
    g["epscol"] = sb("epscol", [128, 1], F32)
    g["onecol"] = sb("onecol", [128, 1], F32)
    g["dtb_s"] = sb("dtb_s", [NH, 1], F32)
    g["acol_s"] = sb("acol_s", [NH, 1], F32)
    g["cb_s"] = sb("cb_s", [128, XT], F32)
    g["cw_s"] = sb("cw_s", [128, KC * XT], F32)
    g["di_s"] = sb("di_s", [128, NH * 64], BF16)
    g["halo_fm"] = sb("halo_fm", [128, H], BF16)
    g["e_all"] = sb("e_all", [128, NH * PH], F32)
    g["cm_all"] = sb("cm_all", [128, 128 * NCH], BF16)
    g["acst_all"] = sb("acst_all", [128, NH * NCH], F32)
    g["aend_all"] = sb("aend_all", [NH, NCH], F32)

    _block1(nc, g)
    if _NBLOCKS >= 2:
        _block2(nc, g)
    if _NBLOCKS >= 3:
        _block3(nc, g)
    return nc


def _block1(nc, g):
    P = Prog(nc, "b1")
    es = ExitStack()
    QTR = SPAN // 4
    def sb(name, shape, dt):
        return es.enter_context(nc.sbuf_tensor(name, shape, dt))
    def ps(name, shape, dt):
        return es.enter_context(nc.psum_tensor(name, shape, dt))
    HALF = SPAN // 2

    xs = [sb("b1_xs0", [128, H], F32), sb("b1_xs1", [128, H], F32)]
    stat = sb("b1_stat", [128, 16], F32)
    xnt = [sb("b1_xnt0", [128, H], BF16), sb("b1_xnt1", [128, H], BF16)]
    xnfm = sb("b1_xnfm", [128, HK * QTR], BF16)
    wsl = [sb(f"b1_wsl{i}", [128, HK * 512], BF16) for i in range(3)]
    zst = [sb("b1_zst0", [128, 512], BF16), sb("b1_zst1", [128, 512], BF16)]
    wof = [sb("b1_wof0", [128, IZ], BF16)]
    wff = [sb("b1_wff0", [128, H], BF16), sb("b1_wff1", [128, H], BF16)]
    nwbc = sb("b1_nwbc", [128, IZ], BF16)
    fsc_s = sb("b1_fsc", [128, HK], F32)
    fbias_row = sb("b1_fbr", [1, H], BF16)
    tmp34 = sb("b1_tmp34", [XT, 512], F32)
    tmpk = sb("b1_tmpk", [XT, 128], F32)
    nw_t = sb("b1_nw_t", [IZ // 128, 128], F32)
    fs_t = sb("b1_fs_t", [HK, 128], F32)
    cwx = sb("b1_cwx", [XT, 128], F32)
    nwrow = sb("b1_nwrow", [1, 512], BF16)
    adcol = sb("b1_adcol", [NH, 2], F32)
    dbc = sb("b1_dbc", [128, NH], BF16)
    drow_f = sb("b1_drow", [1, NH], F32)

    ptA = ps("b1_ptA", [128, 2048], BF16)
    ptB = ps("b1_ptB", [128, 2048], BF16)
    pz = [ps("b1_pz0", [128, 512], F32), ps("b1_pz1", [128, 512], F32)]
    pm = ps("b1_pm", [128, 512], F32)

    n_c = 0
    def cdma(dst, src):
        nonlocal n_c
        n_c = P.dma("sync", dst, src, "dc")
        return n_c
    cdma(g["ident_bf_s"][:, :], g["ident_bf"])
    cdma(g["ident_f_s"][:, :], g["ident_f"])
    cdma(g["mask_s"][:, :], g["mask_tril"])
    cdma(g["dtb_s"][:, :], g["dt_bias"].unsqueeze(1))
    cdma(adcol[:, 0:1], g["a_log"].unsqueeze(1))
    cdma(adcol[:, 1:2], g["d_in"].unsqueeze(1))
    cdma(tmpk[:, :], g["conv_b"].rearrange("(t p) -> t p", p=128))
    cdma(tmp34[:, :], g["conv_w"].rearrange("(t p) o k -> t (p o k)", p=128))
    cdma(nw_t[:, :], g["norm_w"].rearrange("(t p) -> t p", p=128))
    cdma(fs_t[:, :], g["ffn_scale"].rearrange("(t p) -> t p", p=128))
    m_call = n_c

    m_ms = P.op("vector", "memset", g["ones_f"][:, :], 1.0, inc="v")
    P.op("vector", "memset", g["ones64"][:, :], 1.0, inc="v")
    P.op("vector", "memset", g["epscol"][:, :], EPS, inc="v")
    P.op("vector", "memset", g["onecol"][:, :], 1.0, inc="v")
    m_ones = P.op("vector", "memset", g["ones_bf"][:, :], 1.0, inc="v")

    m_cast_half = 0
    for i in range(4):
        r = PROJ // 4
        m_cast_win = P.dma("gpsimd", g["w_in_bf"][i * r:(i + 1) * r, :],
                           g["w_in"][i * r:(i + 1) * r, :], "dcw", fence=(i == 2))
        if i == 1:
            m_cast_half = m_cast_win

    mt = me = 0
    def small_tr(dst, src, idn, w):
        nonlocal mt, me
        mt = P.op("tensor", "transpose", pm[0:w, 0:src.shape[0]].bitcast(F32), src,
                  idn, deps=[("dc", m_call), ("s", me), ("v", m_ms)], inc="t")
        me = P.op("scalar", "copy", dst, pm[0:w, 0:src.shape[0]], deps=[("t", mt)], inc="s")
        return me
    small_tr(g["cb_s"][:, :], tmpk[:, :], g["ident_f_s"][0:XT, 0:XT], 128)
    for k in range(KC):
        m_cx = P.op("vector", "tensor_copy", cwx[:, :],
                    tmp34[:, :].rearrange("t (p k) -> t k p", k=KC)[:, k],
                    deps=[("dc", m_call), ("s", me)], inc="v")
        mt = P.op("tensor", "transpose", pm[0:128, 0:XT].bitcast(F32), cwx[:, :],
                  g["ident_f_s"][0:XT, 0:XT], deps=[("v", m_cx), ("s", me)], inc="t")
        me = P.op("scalar", "copy", g["cw_s"][:, k * XT:(k + 1) * XT], pm[0:128, 0:XT],
                  deps=[("t", mt)], inc="s")
    nw_s = sb("b1_nw_s", [128, IZ // 128], F32)
    small_tr(nw_s[:, :], nw_t[:, :], g["ident_f_s"][0:32, 0:32], 128)
    small_tr(fsc_s[:, :], fs_t[:, :], g["ident_f_s"][0:HK, 0:HK], 128)
    m_a = P.op("scalar", "activation", g["acol_s"][:, :], adcol[:, 0:1], AF.Exp,
               deps=[("dc", m_call), ("s", me)], inc="s")
    m_a = P.op("vector", "tensor_scalar", g["acol_s"][:, :], g["acol_s"][:, :], -1.0,
               None, OP.mult, deps=[("s", m_a)], inc="v")
    mt = P.op("tensor", "transpose", pm[0:1, 0:NH].bitcast(F32), adcol[:, 1:2],
              g["ident_f_s"][0:NH, 0:NH], deps=[("s", me), ("dc", m_call)], inc="t")
    me = P.op("scalar", "copy", drow_f[:, :], pm[0:1, 0:NH], deps=[("t", mt)], inc="s")
    mt = P.op("tensor", "matmul", pm[0:128, 64:128], g["ones_f"][0:1, :], drow_f[:, :],
              start=True, stop=True, deps=[("s", me), ("v", m_ms)], inc="t")
    me = P.op("scalar", "copy", dbc[:, :], pm[0:128, 64:128], deps=[("t", mt)], inc="s")
    for h in range(NH):
        P.op("vector", "tensor_tensor", g["di_s"][0:64, h * 64:(h + 1) * 64],
             g["ident_bf_s"][0:64, 0:64],
             dbc[0:64, h:h + 1].broadcast_to((64, 64)), OP.mult,
             deps=[("s", me)], inc="v")
        P.op("vector", "tensor_tensor", g["di_s"][64:128, h * 64:(h + 1) * 64],
             g["ident_bf_s"][64:128, 64:128],
             dbc[64:128, h:h + 1].broadcast_to((64, 64)), OP.mult,
             deps=[("s", me)], inc="v")
    m_di = P.mark("v")

    m_nf = P.dma("gpsimd", g["nwrow_d"].rearrange("(t p) -> t p", p=128), nw_t[:, :],
                 "dnw", deps=[("dc", m_call)])
    for j in range(IZ // 512):
        m_nr = P.dma("gpsimd", nwrow[:, :],
                     g["nwrow_d"][j * 512:(j + 1) * 512].unsqueeze(0), "dnw",
                     deps=[("t", mt)])
        mt = P.op("tensor", "matmul", pm[:, :], g["ones_bf"][0:1, :],
                  nwrow[:, :], start=True, stop=True,
                  deps=[("dnw", m_nr), ("s", me), ("v", m_ones)], inc="t")
        me = P.op("scalar", "copy", nwbc[:, j * 512:(j + 1) * 512], pm[:, :],
                  deps=[("t", mt)], inc="s")
    m_nwbc = me
    m_fbr_d2 = P.dma("gpsimd", fbias_row[:, :], g["ffn_bias"].unsqueeze(0), "dfb")

    m_use = [0]
    for i in range(HK):
        bi = 0
        m_d = P.dma("gpsimd", wof[bi][:, :], g["w_out"][i * 128:(i + 1) * 128, :], "dwo",
                    deps=[("dso", m_use[bi])])
        m_f = P.op("vector", "tensor_tensor", wof[bi][:, :], wof[bi][:, :], nwbc[:, :],
                   OP.mult, deps=[("dwo", m_d), ("s", m_nwbc)], inc="v")
        m_use[bi] = P.dma("scalar", g["w_out_bf"][i * 128:(i + 1) * 128, :], wof[bi][:, :],
                          "dso", deps=[("v", m_f)])
    m_use = [0, 0]
    for i in range(HK):
        bi = i % 2
        m_d = P.dma("gpsimd", wff[bi][:, :], g["w_ffn"][i * 128:(i + 1) * 128, :],
                    "dwf0" if bi == 0 else "dwf1",
                    deps=[("dsf", m_use[bi])])
        m_f = P.op("vector", "tensor_scalar", wff[bi][:, :], wff[bi][:, :],
                   fsc_s[:, i:i + 1], None, OP.mult,
                   deps=[("dwf0" if bi == 0 else "dwf1", m_d), ("s", me)], inc="v")
        m_use[bi] = P.dma("scalar", g["w_ffn_bf"][i * 128:(i + 1) * 128, :], wff[bi][:, :],
                          "dsf", deps=[("v", m_f)])
    m_wffn_sp = P.mark("dsf")

    bt = [sb(f"b1_bt{i}", [128, 2048], BF16) for i in range(4)]
    m_btw = [0, 0, 0, 0]
    bi_c = 0

    def bsems(bi):
        return f"dbt{bi}", f"dbw{bi}"

    def build4(src_name, r0, kcol, dst_list, dep_extra):
        nonlocal bi_c
        bi = bi_c % 4
        bi_c += 1
        dbs, dbw = bsems(bi)
        m_l = P.dma("sync", bt[bi][:, :],
                    g[src_name][r0:r0 + 2048, kcol * 128:(kcol + 1) * 128],
                    dbs, deps=dep_extra + [(dbw, m_btw[bi])], transpose=True)
        m_w = 0
        for i, dst in enumerate(dst_list):
            m_w = P.dma("gpsimd", dst, bt[bi][:, i * 512:(i + 1) * 512], dbw,
                        deps=[(dbs, m_l)], fence=(i == 0))
        m_btw[bi] = m_w

    for k in range(HK):
        build4("w_in_bf", 0, k, [g["wz_t"][og, k] for og in range(4)],
               [("dcw", m_cast_half)])
        build4("w_in_bf", 2048, k, [g["wz_t"][og, k] for og in range(4, 8)],
               [("dcw", m_cast_half)])
    m_wz_built = [P.mark(f"dbw{i}") for i in range(4)]
    m_wf_sp = P.mark("dsf")
    for k in range(HK):
        build4("w_ffn_bf", 0, k, [g["wf_t"][og, k] for og in range(4)],
               [("dsf", m_wf_sp)])
    m_wf_built = [P.mark(f"dbw{i}") for i in range(4)]
    for k in range(HK):
        build4("w_in_bf", IZ, k, [g["wx_t"][og, k] for og in range(4)],
               [("dcw", m_cast_win)])
        build4("w_in_bf", IZ + 2048, k, [g["wx_t"][og, k] for og in range(4, 8)],
               [("dcw", m_cast_win)])
    for k in range(HK):
        bi = bi_c % 4
        bi_c += 1
        dbs, dbw = bsems(bi)
        P.dma("sync", bt[bi][:, 0:256],
              g["w_in_bf"][2 * IZ:2 * IZ + 256, k * 128:(k + 1) * 128],
              dbs, deps=[("dcw", m_cast_win), (dbw, m_btw[bi])], transpose=True)
        m_l = P.dma("sync", bt[bi][:, 256:384],
                    g["w_in_bf"][PROJ - 128:PROJ, k * 128:(k + 1) * 128],
                    dbs, transpose=True, fence=False)
        m_btw[bi] = P.dma("gpsimd", g["wx_t"][8, k].rearrange("p c -> p c")[:, 0:384],
                          bt[bi][:, 0:384], dbw, deps=[(dbs, m_l)])
    m_wo_sp = P.mark("dso")
    for k in range(IZ // 128):
        build4("w_out_bf", 0, k, [g["wo_t"][cg, k] for cg in range(4)],
               [("dso", m_wo_sp)])

    m_xn_s = [0, 0]
    m_ev_v = [0, 0]
    m_trm = [0, 0]

    def ln_block(bi, src, dst_fm, extra_ev_deps=()):
        dxs = "dx0" if bi == 0 else "dx1"
        m_x = P.dma("scalar", xs[bi][:, :], src, dxs, deps=[("s", m_xn_s[bi])])
        o = bi * 8
        m_sum = P.op("vector", "tensor_reduce", stat[:, o:o + 1], xs[bi][:, :], AX.X,
                     OP.add, deps=[(dxs, m_x), ("s", m_xn_s[bi])], inc="v")
        m_sq = P.op("scalar", "activation", xnt[bi][:, :], xs[bi][:, :], AF.Square,
                    accum_out=stat[:, o + 1:o + 2],
                    deps=[(dxs, m_x), ("v", m_sum), ("t", m_trm[bi])], inc="s")
        m_mean = P.op("scalar", "mul", stat[:, o + 2:o + 3], stat[:, o:o + 1], 1.0 / H,
                      deps=[("s", m_sq)], inc="s")
        m_v1 = P.op("vector", "scalar_tensor_tensor", stat[:, o + 3:o + 4],
                    stat[:, o + 2:o + 3], 1.0, stat[:, o + 2:o + 3], OP.bypass, OP.mult,
                    deps=[("s", m_mean)], inc="v")
        m_v2 = P.op("vector", "scalar_tensor_tensor", stat[:, o + 4:o + 5],
                    stat[:, o + 1:o + 2], 1.0 / H, stat[:, o + 3:o + 4], OP.mult,
                    OP.subtract, deps=[("v", m_v1)], inc="v")
        m_ln = P.op("scalar", "activation", stat[:, o + 5:o + 6], stat[:, o + 4:o + 5],
                    AF.Ln, bias=g["epscol"][:, 0:1], deps=[("v", m_v2)], inc="s")
        m_rs = P.op("scalar", "activation", stat[:, o + 6:o + 7], stat[:, o + 5:o + 6],
                    AF.Exp, scale=-0.5, deps=[("s", m_ln)], inc="s")
        m_nm = P.op("vector", "scalar_tensor_tensor", stat[:, o + 7:o + 8],
                    stat[:, o + 2:o + 3], -1.0, stat[:, o + 6:o + 7], OP.mult, OP.mult,
                    deps=[("s", m_rs)], inc="v")
        m_xn = P.op("scalar", "activation", xnt[bi][:, :], xs[bi][:, :], AF.Identity,
                    bias=stat[:, o + 7:o + 8], scale=stat[:, o + 6:o + 7],
                    deps=[("v", m_nm), ("t", m_trm[bi])], inc="s")
        m_xn_s[bi] = m_xn
        pt = ptA if bi == 0 else ptB
        m_tr = 0
        for k in range(HK):
            m_tr = P.op("tensor", "transpose", pt[:, k * 128:(k + 1) * 128].bitcast(BF16),
                        xnt[bi][:, k * 128:(k + 1) * 128], g["ident_bf_s"][:, :],
                        deps=([("s", m_xn), ("v", m_ev_v[bi]), ("dc", 16 * 2)]
                              if k == 0 else []),
                        inc=("t" if k == HK - 1 else None))
        m_trm[bi] = m_tr
        deps = [("t", m_tr)] + list(extra_ev_deps)
        m_ev = P.op("vector", "tensor_copy", dst_fm, b3(pt[:, :], HK, 128),
                    deps=deps, inc="v")
        m_ev_v[bi] = m_ev
        return m_ev

    m_halo = ln_block(0, g["x_halo"], b3(g["halo_fm"][:, :], HK, 128))

    NCG_Z, NCG_F = IZ // 512, H // 512
    m_a2_t = 0
    m_xnsp_prev = 0
    m_wu = [0, 0]
    m_ev_s = [0, 0]
    m_zdma = [0, 0]
    for half in range(4):
        for j in range(8):
            tb = half * 8 + j
            dst = b3(xnfm[:, :], HK, QTR)[:, :, j * 128:(j + 1) * 128]
            ln_block(j % 2, g["x_span"][tb * 128:(tb + 1) * 128, :], dst,
                     extra_ev_deps=[("t", m_a2_t), ("dxn", m_xnsp_prev)])
        m_half_v = P.mark("v")
        for k in range(HK):
            m_xnsp_prev = P.dma("scalar", g["xn_sp"][k, :, half * QTR:(half + 1) * QTR],
                                xnfm[:, k * QTR:(k + 1) * QTR], "dxn",
                                deps=[("v", m_half_v)], fence=(k == 0))
        for cg in range(NCG_Z + NCG_F):
            bi = cg % 2
            is_ffn = cg >= NCG_Z
            cgl = cg - NCG_Z if is_ffn else cg
            wt = g["wf_t"] if is_ffn else g["wz_t"]
            dws = "dwA" if bi == 0 else "dwB"
            bdep = m_wf_built if is_ffn else m_wz_built
            m_wd = P.dma("scalar", wsl[bi][:, :].rearrange("p (k c) -> p k c", k=HK),
                         wt[cgl].rearrange("k p c -> p k c"), dws,
                         deps=[("t", m_wu[bi])] + [(f"dbw{i}", bdep[i]) for i in range(4)])
            for j in range(8):
                tb = half * 8 + j
                pb = j % 2
                for k in range(HK):
                    m_mm = P.op("tensor", "matmul", pz[pb][:, :],
                                xnfm[:, k * QTR + j * 128:k * QTR + (j + 1) * 128],
                                wsl[bi][:, k * 512:(k + 1) * 512],
                                start=(k == 0), stop=(not is_ffn and k == HK - 1),
                                deps=([("v", m_half_v), (dws, m_wd), ("s", m_ev_s[pb])]
                                      if k == 0 else []),
                                inc=("t" if (not is_ffn and k == HK - 1) else None))
                if is_ffn:
                    m_mm = P.op("tensor", "matmul", pz[pb][:, :], g["ones_bf"][0:1, :],
                                fbias_row[:, cgl * 512:(cgl + 1) * 512],
                                start=False, stop=True,
                                deps=[("dfb", m_fbr_d2), ("v", m_ones)], inc="t")
                dzp = "dzp0" if pb == 0 else "dzp1"
                if is_ffn:
                    m_e = P.op("scalar", "activation", zst[pb][:, :], pz[pb][:, :],
                               AF.Silu, deps=[("t", m_mm), (dzp, m_zdma[pb])], inc="s")
                    m_zdma[pb] = P.dma("scalar", g["g_sp"][tb, :, cgl * 512:(cgl + 1) * 512],
                                       zst[pb][:, :], dzp, deps=[("s", m_e)])
                else:
                    m_e = P.op("scalar", "activation", zst[pb][:, :], pz[pb][:, :],
                               AF.Silu, deps=[("t", m_mm), (dzp, m_zdma[pb])], inc="s")
                    m_zdma[pb] = P.dma("scalar", g["z_sp"][tb, :, cg * 512:(cg + 1) * 512],
                                       zst[pb][:, :], dzp, deps=[("s", m_e)])
                m_ev_s[pb] = m_e
            m_wu[bi] = P.mark("t")
        m_a2_t = P.mark("t")

    P.drain()
    P.emit()
    es.close()


def _block2(nc, g):
    P = Prog(nc, "b2")
    es = ExitStack()
    def sb(name, shape, dt):
        return es.enter_context(nc.sbuf_tensor(name, shape, dt))
    def ps(name, shape, dt):
        return es.enter_context(nc.psum_tensor(name, shape, dt))
    TP = T + 3

    xn_t = sb("b2_xnt", [128, HK * T], BF16)
    wsl = [sb("b2_wsl0", [128, HK * 512], BF16), sb("b2_wsl1", [128, HK * 512], BF16)]
    xbc = sb("b2_xbc", [128, XT * TP], BF16)
    dgw = [sb("b2_dgw0", [128, KC * 128], BF16), sb("b2_dgw1", [128, KC * 128], BF16)]
    dt_pre = sb("b2_dtpre", [NH, T], F32)
    sp1 = sb("b2_sp1", [NH, T], F32)
    dt_fm = sb("b2_dtfm", [NH, T], F32)
    abuf = sb("b2_abuf", [NH, CH], F32)
    acs_fm = sb("b2_acsfm", [NH, CH], F32)
    dcy = sb("b2_dcy", [NH, CH], F32)
    dsd = sb("b2_dsd", [128, CH], F32)
    dtsdt = sb("b2_dtsdt", [128, 4 * 128], F32)
    d_bc = sb("b2_dbc", [128, 4 * NH], F32)
    ddiag = sb("b2_ddiag", [NH, NH], F32)
    dexp = sb("b2_dexp", [NH, 1], F32)
    bcst = [sb(f"b2_bc{i}", [128, 1024], F32) for i in range(3)]
    m_hT = sb("b2_mh", [128, CH * NH], BF16)
    scm = sb("b2_scm", [128, 128], BF16)
    bmT = sb("b2_bmT", [128, NS], BF16)
    xsdtT = sb("b2_xsdtT", [128, IZ], BF16)
    xsdT = sb("b2_xsdT", [128, IZ], BF16)
    e_bf = sb("b2_ebf", [128, IZ], BF16)
    y_tok = sb("b2_ytok", [128, IZ], BF16)
    expacs = sb("b2_expacs", [128, 4 * NH], F32)

    pp = [ps("b2_pp0", [128, 512], F32), ps("b2_pp1", [128, 512], F32)]
    pcv = ps("b2_pcv", [128, 512], F32)
    ptr = [ps("b2_ptrA", [128, 1024], BF16), ps("b2_ptrB", [128, 1024], BF16)]
    psm = ps("b2_psm", [128, 512], F32)
    pyA = ps("b2_pyA", [128, 512], F32)
    pyB = ps("b2_pyB", [128, 512], F32)

    idb, idf, mask = g["ident_bf_s"], g["ident_f_s"], g["mask_s"]

    m_e0 = P.op("vector", "memset", g["e_all"][:, :], 0.0, inc="v")

    def xact(i):
        return xbc[:, i * TP + 3:i * TP + 3 + T]

    m_wu = [0, 0]

    def proj(rhs_fn, ncols, evac_fn, m_rhs, m_ev0, m_vguard=0):
        m_ev = m_ev0
        m_evb = [m_ev0, m_ev0]
        for ocg in range(9):
            bi = ocg % 2
            w = wsl[bi]
            dws = "dwA" if bi == 0 else "dwB"
            ncol = 512 if ocg < 8 else 384
            m_wd = P.dma("sync",
                         w[:, :].rearrange("p (k c) -> p k c", k=HK)[:, :, 0:ncol],
                         g["wx_t"][ocg].rearrange("k p c -> p k c")[:, :, 0:ncol],
                         dws, deps=[("t", m_wu[bi])])
            nblk = 4 if ocg < 8 else 3
            for ob in range(nblk):
                oc = ocg * 4 + ob
                pb = (ocg * 4 + ob) % 2
                is_dt = (ocg == 8 and ob == 2)
                rows = NH if is_dt else 128
                for k in range(HK):
                    if is_dt:
                        lhsT = w[:, k * 512 + 320:k * 512 + 384]
                    else:
                        lhsT = w[:, k * 512 + ob * 128:k * 512 + (ob + 1) * 128]
                    m_mm = P.op("tensor", "matmul", pp[pb][0:rows, 0:ncols], lhsT,
                                rhs_fn(k), start=(k == 0), stop=(k == HK - 1),
                                deps=([(dws, m_wd), ("dxt", m_rhs), ("s", m_evb[pb]),
                                       ("v", m_vguard)]
                                      if k == 0 else []),
                                inc=("t" if k == HK - 1 else None))
                m_ev = evac_fn(oc, pp[pb], m_mm)
                m_evb[pb] = m_ev
            m_wu[bi] = P.mark("t")
        return m_ev

    def ev_pro(oc, pz, m_mm):
        if oc < XT:
            return P.op("scalar", "copy", xbc[:, oc * TP:oc * TP + 3], pz[0:128, 0:3],
                        deps=[("t", m_mm)], inc="s")
        return P.op("scalar", "copy", dt_pre[:, 0:3], pz[0:NH, 0:3],
                    deps=[("t", m_mm)], inc="s")

    m_ev = proj(lambda k: g["halo_fm"][:, k * 128 + 125:k * 128 + 128], 3, ev_pro, 0, 0)

    m_states_t = 0
    m_chunk_v = P.mark("v")
    m_ydma = 0
    m_conv_s = [m_ev] * XT
    m_dtfree_v = 0
    m_trfree = [0, 0]
    m_mh_t = 0
    m_xsd_t = 0
    m_dgw_use = [0, 0]
    m_scm_prev = 0
    m_bev_prev = 0
    m_dd_t = 0
    m_dd_prev = 0
    m_acsT_prev = 0
    m_de_prev = 0
    m_st_sadd = 0
    m_sexp_slot = [0, 0, 0]
    m_fl_prev = 0
    m_yslot = [0] * 8
    m_fold_r = [0, 0, 0, 0]

    for t in range(NTL):
        m_xn = 0
        for k in range(HK):
            m_xn = P.dma("scalar", xn_t[:, k * T:(k + 1) * T],
                         g["xn_sp"][k, :, t * T:(t + 1) * T], "dxt",
                         deps=[("t", m_states_t)] if k == 0 else [],
                         fence=(k == 0))

        m_ssd_t = m_states_t
        m_ssd_v = m_chunk_v

        def ev_main(oc, pz, m_mm):
            if oc < XT:
                return P.op("scalar", "copy", xbc[:, oc * TP + 3:oc * TP + 3 + T],
                            pz[0:128, 0:T],
                            deps=[("t", m_mm), ("s", m_conv_s[oc]),
                                  ("t", m_ssd_t), ("v", m_ssd_v)],
                            inc="s")
            return P.op("scalar", "copy", dt_pre[:, :], pz[0:NH, 0:T],
                        deps=[("t", m_mm), ("v", m_dtfree_v)], inc="s")

        m_ev = proj(lambda k: xn_t[:, k * T:(k + 1) * T], T, ev_main, m_xn, m_ev,
                    m_vguard=m_ssd_v)
        m_proj_s = P.mark("s")

        m_cv_bank = [m_proj_s, m_proj_s]
        for i in range(XT):
            di = i % 2
            for k in range(KC):
                P.op("vector", "tensor_tensor",
                     dgw[di][:, k * 128:(k + 1) * 128], idb[:, :],
                     g["cw_s"][:, k * XT + i:k * XT + i + 1].broadcast_to((128, 128)),
                     OP.mult, deps=[("t", m_dgw_use[di])] if k == 0 else [], inc="v")
            m_dg = P.mark("v")
            for k in range(KC):
                m_cm = P.op("tensor", "matmul", pp[di][:, 0:T],
                            dgw[di][:, k * 128:(k + 1) * 128],
                            xbc[:, i * TP + k:i * TP + k + T],
                            start=(k == 0), stop=(k == KC - 1),
                            deps=([("s", m_proj_s), ("v", m_dg), ("s", m_cv_bank[di]),
                                   ("v", m_chunk_v)] if k == 0 else []),
                            inc=("t" if k == KC - 1 else None))
            m_dgw_use[di] = m_cm
            m_h = P.op("vector", "tensor_copy", xbc[:, i * TP:i * TP + 3],
                       xbc[:, i * TP + T:i * TP + T + 3], deps=[("t", m_cm)], inc="v")
            m_cv_bank[di] = P.op("scalar", "activation",
                                 xbc[:, i * TP + 3:i * TP + 3 + T],
                                 pp[di][:, 0:T], AF.Silu, bias=g["cb_s"][:, i:i + 1],
                                 deps=[("t", m_cm), ("v", m_h)], inc="s")
            m_conv_s[i] = m_cv_bank[di]
        m_conv_all = P.mark("s")

        m1 = P.op("vector", "tensor_scalar", dt_pre[:, :], dt_pre[:, :],
                  g["dtb_s"][:, 0:1], None, OP.add, deps=[("s", m_proj_s)], inc="v")
        m2 = P.op("vector", "scalar_tensor_tensor", sp1[:, :], dt_pre[:, :], -1.0,
                  dt_pre[:, :], OP.mult, OP.max, deps=[("v", m1)], inc="v")
        m3 = P.op("scalar", "activation", sp1[:, :], sp1[:, :], AF.Exp, scale=-1.0,
                  deps=[("v", m2)], inc="s")
        m4 = P.op("scalar", "activation", sp1[:, :], sp1[:, :], AF.Ln,
                  bias=g["onecol"][0:NH, 0:1], deps=[("s", m3)], inc="s")
        m5 = P.op("vector", "tensor_scalar", dt_fm[:, :], dt_pre[:, :], 0.0, None,
                  OP.max, deps=[("s", m4)], inc="v")
        m_dt = P.op("vector", "tensor_tensor", dt_fm[:, :], dt_fm[:, :], sp1[:, :],
                    OP.add, deps=[("v", m5)], inc="v")

        m_fl4 = [0, 0, 0, 0]
        me1l = [0, 0, 0, 0]
        me3l = 0
        m_d2p = 0
        for c in range(CPT):
            cc = t * CPT + c
            m_ah = P.op("vector", "tensor_scalar", abuf[:, :], dt_fm[:, c * CH:(c + 1) * CH],
                        g["acol_s"][:, 0:1], None, OP.mult,
                        deps=[("v", m_dt), ("v", m_ssd_v)], inc="v")
            m_acs = P.op("vector", "tensor_tensor_scan", acs_fm[:, :], abuf[:, :],
                         abuf[:, :], 0.0, OP.add, OP.bypass,
                         deps=[("v", m_ah), ("t", m_acsT_prev), ("s", m_de_prev),
                               ("dfl", m_fl_prev)],
                         inc="v")
            P.op("vector", "tensor_copy", g["aend_all"][:, cc:cc + 1],
                 acs_fm[:, CH - 1:CH], deps=[("v", m_acs)], inc="v")
            m_dcy = P.op("scalar", "activation", dcy[:, :], acs_fm[:, :], AF.Exp,
                         scale=-1.0, bias=acs_fm[:, CH - 1:CH],
                         deps=[("v", m_acs), ("s", me1l[max(0, c - 1)])], inc="s")
            m_d1 = P.op("vector", "tensor_copy", dsd[0:NH, :], dt_fm[:, c * CH:(c + 1) * CH],
                        deps=[("v", m_dt), ("s", me1l[max(0, c - 1)]),
                              ("t", m_states_t)], inc="v")
            m_d2 = P.op("vector", "tensor_tensor", dsd[NH:128, :],
                        dt_fm[:, c * CH:(c + 1) * CH],
                        dcy[:, :], OP.mult, deps=[("s", m_dcy), ("v", m_d1)], inc="v")
            m_de = P.op("scalar", "activation", dexp[:, :], acs_fm[:, CH - 1:CH],
                        AF.Exp, deps=[("v", m_acs), ("v", m_dd_prev)], inc="s")
            m_de_prev = m_de
            m_dd = P.op("vector", "tensor_scalar", ddiag[:, :], idf[0:NH, 0:NH],
                        dexp[:, 0:1], None, OP.mult, deps=[("s", m_de), ("t", m_dd_t)],
                        inc="v")
            m_dd_prev = m_dd
            mt1 = P.op("tensor", "transpose", psm[:, 128:256].bitcast(F32), dsd[:, :],
                       idf[:, :], deps=[("v", m_d2), ("v", m_scm_prev),
                                        ("v", m_ssd_v), ("s", me3l)], inc="t")
            me1l[c] = P.op("scalar", "copy", dtsdt[:, c * 128:(c + 1) * 128],
                           psm[:, 128:256], deps=[("t", mt1)], inc="s")
            mt2 = P.op("tensor", "transpose", psm[:, 256:320].bitcast(F32), acs_fm[:, :],
                       idf[0:NH, 0:NH], deps=[("v", m_acs), ("s", me1l[c])], inc="t")
            m_acsT_prev = mt2
            me2 = P.op("scalar", "copy", g["acst_all"][:, cc * NH:(cc + 1) * NH],
                       psm[:, 256:320], deps=[("t", mt2)], inc="s")
            mt3 = P.op("tensor", "matmul", psm[:, 320:384], g["ones64"][:, :], ddiag[:, :],
                       start=True, stop=True, deps=[("v", m_dd), ("s", me2)], inc="t")
            m_dd_t = mt3
            me3l = P.op("scalar", "copy", d_bc[:, c * NH:(c + 1) * NH],
                        psm[:, 320:384], deps=[("t", mt3)], inc="s")
            m_fl4[c] = P.dma("gpsimd", g["acs_row"][cc].rearrange("(h l) -> h l", h=NH),
                             acs_fm[:, :], "dfl", deps=[("v", m_acs)])
            m_fl_prev = m_fl4[c]
        cc0 = t * CPT
        m_ex = P.op("scalar", "activation", expacs[:, :],
                    g["acst_all"][:, cc0 * NH:(cc0 + 4) * NH], AF.Exp,
                    deps=[("s", me3l), ("v", m_ssd_v)], inc="s")

        m_scm_c = {}
        m_esnap_c = {}
        m_bev_c = {}
        m_mh_jc = {}
        m_exp_jc = {}

        def stage_pre(c):
            nonlocal m_scm_prev
            cc = t * CPT + c
            csl = slice(c * CH, (c + 1) * CH)
            msc = P.op("tensor", "matmul", psm[:, 0:128], xact(32)[:, csl],
                       xact(33)[:, csl], start=True, stop=True,
                       deps=[("s", m_conv_all), ("s", me3l), ("v", m_scm_prev),
                             ("v", m_st_sadd)], inc="t")
            m_scm_c[c] = P.op("vector", "tensor_tensor", scm[:, :], psm[:, 0:128],
                              mask[:, :], OP.mult,
                              deps=[("t", msc),
                                    ("v", m_mh_jc[c - 1][7] if c > 0 else 0)],
                              inc="v")
            m_scm_prev = m_scm_c[c]
            P.op("vector", "tensor_copy", g["cm_all"][:, cc * 128:(cc + 1) * 128],
                 xact(33)[:, csl], deps=[("s", m_conv_all)], inc="v")

        def stage_folds(c):
            nonlocal m_bev_prev
            csl = slice(c * CH, (c + 1) * CH)
            for rnd in range(4):
                pt = ptr[rnd % 2]
                for i in range(8):
                    m_tr = P.op("tensor", "transpose",
                                pt[:, i * 128:(i + 1) * 128].bitcast(BF16),
                                xact(rnd * 8 + i)[:, csl], idb[:, :],
                                deps=([("s", m_conv_all), ("v", m_trfree[rnd % 2]),
                                       ("s", m_bev_prev if rnd == 0 else 0),
                                       ("t", m_mh_t), ("t", m_xsd_t)] if i == 0 else []),
                                inc=("t" if i == 7 else None))
                h0 = rnd * 16
                src2 = pt[:, :].rearrange("p (a b c) -> p a b c", a=8, b=2)
                dtop = b3(dtsdt[:, c * 128 + h0:c * 128 + h0 + 16], 8, 2).unsqueeze(3) \
                    .broadcast_to((128, 8, 2, 64))
                sdop = b3(dtsdt[:, c * 128 + NH + h0:c * 128 + NH + h0 + 16], 8, 2) \
                    .unsqueeze(3).broadcast_to((128, 8, 2, 64))
                m_f1 = P.op("vector", "tensor_tensor",
                            xsdtT[:, rnd * 1024:(rnd + 1) * 1024]
                            .rearrange("p (a b c) -> p a b c", a=8, b=2),
                            src2, dtop, OP.mult,
                            deps=[("t", m_tr), ("s", me1l[c])], inc="v")
                m_f2 = P.op("vector", "tensor_tensor",
                            xsdT[:, rnd * 1024:(rnd + 1) * 1024]
                            .rearrange("p (a b c) -> p a b c", a=8, b=2),
                            src2, sdop, OP.mult, deps=[("v", m_f1)], inc="v")
                m_trfree[rnd % 2] = m_f2
                m_fold_r[rnd] = m_f1
            m_btr = P.op("tensor", "transpose", ptr[0][:, 0:128].bitcast(BF16),
                         xact(32)[:, csl], idb[:, :], deps=[("v", m_trfree[0])],
                         inc="t")
            m_bev_c[c] = P.op("scalar", "copy", bmT[:, :], ptr[0][:, 0:128],
                              deps=[("t", m_btr)], inc="s")
            m_bev_prev = m_bev_c[c]

        def stage_esnap(c):
            m_esnap_c[c] = P.op("vector", "tensor_copy", e_bf[:, :], g["e_all"][:, :],
                                deps=[("t", m_mh_t)], inc="v")

        def emit_mult(c, j):
            m_mh_jc[c][j] = P.op("vector", "tensor_tensor",
                                 b3(m_hT[:, j * 1024:(j + 1) * 1024], 8, 128),
                                 b3(m_hT[:, j * 1024:(j + 1) * 1024], 8, 128),
                                 scm[:, :].unsqueeze(1).broadcast_to((128, 8, 128)),
                                 OP.mult,
                                 deps=[("s", m_exp_jc[c][j]), ("v", m_scm_c[c])],
                                 inc="v")

        def piece(c, j):
            cc = t * CPT + c
            acsT_c = g["acst_all"][:, cc * NH:(cc + 1) * NH]
            pbc = (c * 8 + j) % 3
            dbs = f"dbc{pbc}"
            m_bc = P.dma("gpsimd", bcst[pbc][:, :],
                         g["acs_row"][cc][j * 1024:(j + 1) * 1024].unsqueeze(0)
                         .partition_broadcast(128), dbs,
                         deps=[("dfl", m_fl4[c]), ("s", m_sexp_slot[pbc])])
            m_sub = P.op("vector", "scalar_tensor_tensor",
                         b3(bcst[pbc][:, :], 8, 128), b3(bcst[pbc][:, :], 8, 128),
                         1.0, acsT_c[:, 8 * j:8 * j + 8].unsqueeze(2)
                         .broadcast_to((128, 8, 128)),
                         OP.bypass, OP.subtract,
                         deps=[(dbs, m_bc)], inc="v")
            m_min = P.op("vector", "tensor_scalar", bcst[pbc][:, :], bcst[pbc][:, :],
                         0.0, None, OP.min, deps=[("v", m_sub)], inc="v")
            m_exp_jc[c][j] = P.op("scalar", "activation",
                                  m_hT[:, j * 1024:(j + 1) * 1024], bcst[pbc][:, :],
                                  AF.Exp, deps=[("v", m_min), ("t", m_yslot[j])],
                                  inc="s")
            m_sexp_slot[pbc] = m_exp_jc[c][j]
            if j >= 1:
                emit_mult(c, j - 1)
            if j == 7:
                emit_mult(c, 7)

        def ygroup(c, gi, m_ev_pair):
            nonlocal m_mh_t
            cc = t * CPT + c
            csl = slice(c * CH, (c + 1) * CH)
            exslice = expacs[:, c * NH:(c + 1) * NH]
            pA, pB = (pyA, pyB) if gi % 2 == 0 else (pp[0], pp[1])
            for hh in range(8):
                h = gi * 8 + hh
                P.op("tensor", "matmul", pA[:, hh * 64:(hh + 1) * 64],
                     m_hT[:, h * 128:(h + 1) * 128],
                     xsdtT[:, h * 64:(h + 1) * 64], start=True, stop=False,
                     deps=([("v", m_mh_jc[c][gi]), ("v", m_esnap_c[c]),
                            ("v", m_ev_pair[gi % 2]), ("v", m_fold_r[gi // 2]),
                            ("s", m_conv_all)] if hh == 0 else []))
                P.op("tensor", "matmul", pA[:, hh * 64:(hh + 1) * 64],
                     xact(h // 2)[(h % 2) * 64:(h % 2) * 64 + 64, csl],
                     g["di_s"][(h % 2) * 64:(h % 2) * 64 + 64,
                               h * 64:(h + 1) * 64], start=False, stop=True)
                m_yg = P.op("tensor", "matmul", pB[:, hh * 64:(hh + 1) * 64],
                            xact(33)[:, csl], e_bf[:, h * 64:(h + 1) * 64],
                            start=True, stop=True,
                            inc=("t" if hh == 7 else None))
            m_mh_t = m_yg
            m_yslot[gi] = m_yg
            exop = exslice[:, gi * 8:(gi + 1) * 8].unsqueeze(2) \
                .broadcast_to((128, 8, 64))
            m_e1 = P.op("vector", "tensor_tensor",
                        b3(y_tok[:, gi * 512:(gi + 1) * 512], 8, 64),
                        b3(pB[:, :], 8, 64), exop, OP.mult,
                        deps=[("t", m_yg), ("s", m_ex), ("dyp", m_ydma)], inc="v")
            m_ev_pair[gi % 2] = P.op("vector", "tensor_tensor",
                                     y_tok[:, gi * 512:(gi + 1) * 512],
                                     y_tok[:, gi * 512:(gi + 1) * 512], pA[:, :],
                                     OP.add, deps=[("v", m_e1)], inc="v")

        def stage_states(c, m_ev_pair):
            nonlocal m_ydma, m_st_sadd, m_states_t, m_xsd_t, m_chunk_v
            cc = t * CPT + c
            m_ydma = P.dma("scalar", g["yp_sp"][cc], y_tok[:, :], "dyp",
                           deps=[("v", m_ev_pair[0]), ("v", m_ev_pair[1])])
            m_emul = P.op("vector", "tensor_tensor", b3(g["e_all"][:, :], NH, PH),
                          b3(g["e_all"][:, :], NH, PH),
                          d_bc[:, c * NH:(c + 1) * NH].unsqueeze(2)
                          .broadcast_to((128, NH, PH)), OP.mult,
                          deps=[("s", me3l), ("v", m_esnap_c[c])], inc="v")
            m_st_sadd = m_emul
            for gi in range(8):
                m_sm = P.op("tensor", "matmul", pcv[:, :], bmT[:, :],
                            xsdT[:, gi * 512:(gi + 1) * 512], start=True, stop=True,
                            deps=[("s", m_bev_c[c]), ("v", m_trfree[0]),
                                  ("v", m_trfree[1]),
                                  ("v", m_st_sadd), ("v", m_emul)],
                            inc="t")
                m_st_sadd = P.op("vector", "tensor_tensor",
                                 g["e_all"][:, gi * 512:(gi + 1) * 512],
                                 g["e_all"][:, gi * 512:(gi + 1) * 512], pcv[:, :],
                                 OP.add, deps=[("t", m_sm)], inc="v")
            m_states_t = P.mark("t")
            m_xsd_t = m_states_t
            m_chunk_v = P.mark("v")

        for c in range(CPT):
            m_mh_jc[c] = [0] * 8
            m_exp_jc[c] = [0] * 8
        stage_pre(0)
        stage_folds(0)
        stage_esnap(0)
        for j in range(8):
            piece(0, j)
        for c in range(CPT):
            if c < CPT - 1:
                stage_pre(c + 1)
            m_ev_pair = [0, 0]
            for gi in range(8):
                ygroup(c, gi, m_ev_pair)
                if c < CPT - 1:
                    piece(c + 1, gi)
            stage_states(c, m_ev_pair)
            if c < CPT - 1:
                stage_folds(c + 1)
                stage_esnap(c + 1)
        m_dtfree_v = P.mark("v")

    m_z0 = P.op("vector", "memset", bcst[0][:, 0:NH], 0.0, deps=[("v", P.mark("v"))],
                inc="v")
    P.dma("gpsimd", g["ag_in"][:, IZ:IZ + NH], bcst[0][:, 0:NH], "dex",
          deps=[("v", m_z0)])
    m_exp_e = P.dma("gpsimd", g["ag_in"][:, 0:IZ], g["e_all"][:, :], "dex",
                    deps=[("v", P.mark("v"))], fence=False)
    dred = sb("b2_dred", [NH, 1], F32)
    m_red = P.op("vector", "tensor_reduce", dred[:, :], g["aend_all"][:, :], AX.X,
                 OP.add, deps=[("v", P.mark("v"))], inc="v")
    m_exp_d = P.dma("gpsimd", g["ag_in"][127:128, IZ:IZ + NH].rearrange("o h -> h o"),
                    dred[:, :], "dex", deps=[("v", m_red)])
    if _DEBUG:
        P.dma("gpsimd", g["dbg_e"], g["ag_in"], "dex", deps=[("dex", P.mark("dex"))])
    P.drain()
    P.emit()
    es.close()


def _block3(nc, g):
    P = Prog(nc, "b3")
    es = ExitStack()
    def sb(name, shape, dt):
        return es.enter_context(nc.sbuf_tensor(name, shape, dt))
    def ps(name, shape, dt):
        return es.enter_context(nc.psum_tensor(name, shape, dt))

    wbc = sb("b3_wbc", [128, NH], F32)
    daug = sb("b3_daug", [8, NH], F32)
    cmb_s = sb("b3_cmb", [8, 8], F32)
    cmbb = sb("b3_cmbb", [1, 8], F32)
    wall = sb("b3_wall", [NH, 8], F32)
    zbuf = [sb("b3_zbuf0", [128, IZ], BF16), sb("b3_zbuf1", [128, IZ], BF16)]
    ypbuf = [sb("b3_ypbuf0", [128, IZ], BF16), sb("b3_ypbuf1", [128, IZ], BF16)]
    y32 = sb("b3_y32", [128, IZ], F32)
    yfn = sb("b3_yfn", [128, IZ], BF16)
    erbf = sb("b3_erbf", [128, IZ // 2], BF16)
    expacs2 = sb("b3_expacs2", [128, NH], F32)
    dexp2 = sb("b3_dexp2", [NH, 1], F32)
    ddiag2 = sb("b3_ddiag2", [NH, NH], F32)
    d_bc2 = sb("b3_dbc2", [128, NH], F32)
    rcol = sb("b3_rcol", [128, 4], F32)
    yffm = [sb(f"b3_yffm{i}", [128, IZ], BF16) for i in range(4)]
    gb = [sb(f"b3_gb{i}", [128, H], BF16) for i in range(4)]
    xr = [sb(f"b3_xr{i}", [128, H], F32) for i in range(4)]
    wot = [sb(f"b3_wot{i}", [128, 4 * 512], BF16) for i in range(3)]
    ost = [sb("b3_ost0", [128, 512], F32), sb("b3_ost1", [128, 512], F32)]

    po = [ps(f"b3_po{i}", [128, 512], F32) for i in range(4)]
    pyc = [ps("b3_pyc0", [128, 512], F32), ps("b3_pyc1", [128, 512], F32)]
    ptr2 = [ps("b3_ptr2A", [128, 1024], BF16), ps("b3_ptr2B", [128, 1024], BF16)]

    idb = g["ident_bf_s"]
    e_run = g["e_all"]

    m_cc = P.op("gpsimd", "collective_compute", "AllGather", OP.bypass,
                replica_groups=[list(range(8))],
                ins=[g["ag_in"]], outs=[g["ag_out"]], inc="cc")
    m_da = P.dma("sync", daug[0:8, :],
                 g["ag_out"].rearrange("(j p) c -> j p c", p=128)[:, 127, IZ:IZ + NH],
                 "dcc", deps=[("cc", m_cc)])
    m_cm = P.dma("sync", cmb_s[:, :], g["cmb"][0:8, :], "dcc")
    m_cb = P.dma("sync", cmbb[:, :], g["cmb"][8:9, :], "dcc")
    P.op("tensor", "matmul", pyc[0][0:NH, 0:8], daug[:, :], cmb_s[:, :],
         start=True, stop=False, deps=[("dcc", m_cb)])
    mt = P.op("tensor", "matmul", pyc[0][0:NH, 0:8], g["ones_f"][0:1, 0:NH],
              cmbb[:, :], start=False, stop=True, deps=[("dcc", m_cb)], inc="t")
    me = P.op("scalar", "copy", wall[:, :], pyc[0][0:NH, 0:8], deps=[("t", mt)], inc="s")
    me = P.op("scalar", "activation", wall[:, :], wall[:, :], AF.Exp,
              deps=[("s", me)], inc="s")
    mt = P.op("tensor", "transpose", pyc[1][0:8, 0:NH].bitcast(F32), wall[:, :],
              g["ident_f_s"][0:NH, 0:NH], deps=[("s", me)], inc="t")
    me = P.op("scalar", "copy", daug[0:8, :], pyc[1][0:8, 0:NH], deps=[("t", mt)], inc="s")
    m_wr = P.dma("gpsimd", g["w_row"], daug[0:8, :], "dwr", deps=[("s", me)])
    m_prev = 0
    for j in range(8):
        m_sj = P.dma("sync", y32[:, :].bitcast(F32),
                     g["ag_out"][j * 128:(j + 1) * 128, 0:IZ], "dsj",
                     deps=[("cc", m_cc), ("v", m_prev)])
        m_wb = P.dma("gpsimd", wbc[:, :],
                     g["w_row"][j].unsqueeze(0).partition_broadcast(128), "dwb",
                     deps=[("dwr", m_wr), ("v", m_prev)])
        m_t = P.op("vector", "tensor_tensor", b3(y32[:, :], NH, PH),
                   b3(y32[:, :], NH, PH),
                   wbc[:, :].unsqueeze(2).broadcast_to((128, NH, PH)), OP.mult,
                   deps=[("dsj", m_sj), ("dwb", m_wb)], inc="v")
        if j == 0:
            m_prev = P.op("vector", "tensor_copy", e_run[:, :], y32[:, :],
                          deps=[("v", m_t)], inc="v")
        else:
            m_prev = P.op("vector", "tensor_tensor", e_run[:, :], e_run[:, :],
                          y32[:, :], OP.add, deps=[("v", m_t)], inc="v")
    m_er = m_prev

    m_out = 0
    m_po_free = [0, 0, 0, 0]
    m_tr2free = [0, 0]
    m_yffm_t = [0, 0, 0, 0]
    m_gxfree = [(0, 0)] * 4
    m_wotu = [0, 0, 0]
    m_er_t = 0
    m_y32_free = 0
    m_pyc_v = [0, 0]
    m_pyc_s0 = 0
    m_dd2_t = 0
    m_dd2_s = 0
    m_eg_t = 0

    m_yf_prev = 0
    m_yv_prev = 0
    m_yf_pb = [0, 0]
    m_yv_pb = [0, 0]
    m_trlast = 0
    m_gd = [0, 0, 0, 0]
    m_xd = [0, 0, 0, 0]
    m_ost_d = [0, 0]
    for gg in range(8):
        for ci in range(4):
            cc = gg * 4 + ci
            zpb = cc % 2
            m_z = P.dma("sync", zbuf[zpb][:, :], g["z_sp"][cc],
                        "dz0" if zpb == 0 else "dz1", deps=[("v", m_yf_pb[zpb])])
            m_yp = P.dma("sync", ypbuf[zpb][:, :], g["yp_sp"][cc],
                         "dyp0" if zpb == 0 else "dyp1", deps=[("v", m_yv_pb[zpb])])
            m_gd[ci] = P.dma("sync", gb[ci][:, :], g["g_sp"][cc], "dgb",
                             deps=[("v", m_gxfree[ci][0])])
            m_xd[ci] = P.dma("sync", xr[ci][:, :],
                             g["x_span"][cc * 128:(cc + 1) * 128, :], "dxr",
                             deps=[("v", m_gxfree[ci][1])])
            if cc > 0:
                m_dre = P.op("scalar", "activation", dexp2[:, :],
                             g["aend_all"][:, cc - 1:cc], AF.Exp,
                             deps=[("s", m_dd2_s)], inc="s")
                m_dd = P.op("vector", "tensor_scalar", ddiag2[:, :],
                            g["ident_f_s"][0:NH, 0:NH], dexp2[:, 0:1], None, OP.mult,
                            deps=[("s", m_dre), ("t", m_dd2_t)], inc="v")
                mt = P.op("tensor", "matmul", pyc[0][:, 0:NH], g["ones64"][:, :],
                          ddiag2[:, :], start=True, stop=True,
                          deps=[("v", m_dd), ("v", m_pyc_v[0]), ("v", m_er),
                                ("s", m_pyc_s0)], inc="t")
                m_dd2_t = mt
                me = P.op("scalar", "copy", d_bc2[:, :], pyc[0][:, 0:NH],
                          deps=[("t", mt)], inc="s")
                m_dd2_s = me
                m_er = P.op("vector", "tensor_tensor", b3(e_run[:, :], NH, PH),
                            b3(e_run[:, :], NH, PH),
                            d_bc2[:, :].unsqueeze(2).broadcast_to((128, NH, PH)),
                            OP.mult, deps=[("s", me), ("t", m_er_t)], inc="v")
                m_pyc_s0 = me
            m_ex2 = P.op("scalar", "activation", expacs2[:, :],
                         g["acst_all"][:, cc * NH:(cc + 1) * NH], AF.Exp,
                         deps=[("v", m_yv_prev)], inc="s")
            m_yv = 0
            for gi in range(8):
                pb = gi % 2
                if gi % 4 == 0:
                    m_eg = P.op("vector", "tensor_copy", erbf[:, :],
                                e_run[:, (gi // 4) * 2048:(gi // 4 + 1) * 2048],
                                deps=[("v", m_er), ("t", m_er_t),
                                      ("t", m_eg_t)], inc="v")
                for hh in range(8):
                    h = gi * 8 + hh
                    m_yg = P.op("tensor", "matmul", pyc[pb][:, hh * 64:(hh + 1) * 64],
                                g["cm_all"][:, cc * 128:(cc + 1) * 128],
                                erbf[:, (h * 64) % 2048:(h * 64) % 2048 + 64],
                                start=True, stop=True,
                                deps=([("v", m_eg), ("v", m_pyc_v[pb]),
                                       ("s", m_pyc_s0 if pb == 0 else 0)]
                                      if hh == 0 else []),
                                inc=("t" if hh == 7 else None))
                if gi % 4 == 3:
                    m_eg_t = m_yg
                m_er_t = m_yg
                exop = expacs2[:, gi * 8:(gi + 1) * 8].unsqueeze(2) \
                    .broadcast_to((128, 8, 64))
                m_y1 = P.op("vector", "tensor_tensor",
                            b3(y32[:, gi * 512:(gi + 1) * 512], 8, 64),
                            b3(pyc[pb][:, :], 8, 64), exop, OP.mult,
                            deps=[("t", m_yg), ("s", m_ex2),
                                  ("s", m_y32_free if gi == 0 else 0)],
                            inc="v")
                m_yv = P.op("vector", "tensor_tensor", y32[:, gi * 512:(gi + 1) * 512],
                            y32[:, gi * 512:(gi + 1) * 512],
                            ypbuf[zpb][:, gi * 512:(gi + 1) * 512], OP.add,
                            deps=[("v", m_y1),
                                  ("dyp0" if zpb == 0 else "dyp1", m_yp)], inc="v")
                m_pyc_v[pb] = m_yv
            m_yf = P.op("vector", "tensor_tensor", y32[:, :], y32[:, :],
                        zbuf[zpb][:, :], OP.mult,
                        deps=[("v", m_yv), ("dz0" if zpb == 0 else "dz1", m_z)],
                        inc="v")
            m_yf_pb[zpb] = m_yf
            m_yv_pb[zpb] = m_yv
            m_ss = P.op("scalar", "activation", yfn[:, :], y32[:, :], AF.Square,
                        accum_out=rcol[:, 0:1],
                        deps=[("v", m_yf), ("t", m_trlast)], inc="s")
            m_r1 = P.op("scalar", "activation", rcol[:, 1:2], rcol[:, 0:1], AF.Ln,
                        scale=1.0 / IZ, bias=g["epscol"][:, 0:1],
                        deps=[("s", m_ss)], inc="s")
            m_r2 = P.op("scalar", "activation", rcol[:, 2:3], rcol[:, 1:2], AF.Exp,
                        scale=-0.5, deps=[("s", m_r1)], inc="s")
            m_yn = P.op("scalar", "activation", yfn[:, :], y32[:, :], AF.Identity,
                        scale=rcol[:, 2:3], deps=[("s", m_r2)], inc="s")
            m_y32_free = m_yn
            for rnd in range(4):
                pt = ptr2[rnd % 2]
                for i in range(8):
                    m_tr = P.op("tensor", "transpose",
                                pt[:, i * 128:(i + 1) * 128].bitcast(BF16),
                                yfn[:, (rnd * 8 + i) * 128:(rnd * 8 + i + 1) * 128],
                                idb[:, :],
                                deps=([("s", m_yn), ("v", m_tr2free[rnd % 2]),
                                       ("t", m_yffm_t[ci])] if i == 0 else []),
                                inc=("t" if i == 7 else None))
                m_trlast = m_tr
                m_tr2free[rnd % 2] = P.op(
                    "vector", "tensor_copy", yffm[ci][:, rnd * 1024:(rnd + 1) * 1024],
                    pt[:, :], deps=[("t", m_tr)], inc="v")
        m_yffm_v = P.mark("v")
        for cg in range(4):
            for kq in range(IZ // 512):
                bi = kq % 3
                dwo = f"dwo{bi}"
                m_wd = P.dma("sync", wot[bi][:, :].rearrange("p (k c) -> p k c", k=4),
                             g["wo_t"][cg, 4 * kq:4 * kq + 4].rearrange("k p c -> p k c"),
                             dwo, deps=[("t", m_wotu[bi])])
                for ksub in range(4):
                    k = kq * 4 + ksub
                    for ci in range(4):
                        dep = []
                        if ksub == 0 and ci == 0:
                            dep.append((dwo, m_wd))
                        if kq == 0 and ksub == 0:
                            dep += [("v", m_yffm_v), ("v", m_po_free[ci])]
                        m_lw = P.op("tensor", "matmul", po[ci][:, :],
                                    yffm[ci][:, k * 128:(k + 1) * 128],
                                    wot[bi][:, ksub * 512:(ksub + 1) * 512],
                                    start=(k == 0), stop=(k == IZ // 128 - 1),
                                    deps=dep,
                                    inc=("t" if ci == 3 and ksub == 3 else None))
                m_wotu[bi] = m_lw
            m_mm = P.mark("t")
            for ci in range(4):
                cc = gg * 4 + ci
                ob = ci % 2
                dos = "dout0" if ob == 0 else "dout1"
                m_t = P.op("vector", "tensor_tensor", ost[ob][:, :], po[ci][:, :],
                           gb[ci][:, cg * 512:(cg + 1) * 512], OP.mult,
                           deps=[("t", m_mm), ("dgb", m_gd[ci]), ("dxr", m_xd[ci]),
                                 (dos, m_ost_d[ob])], inc="v")
                m_o = P.op("vector", "tensor_tensor", ost[ob][:, :], ost[ob][:, :],
                           xr[ci][:, cg * 512:(cg + 1) * 512], OP.add,
                           deps=[("v", m_t)], inc="v")
                m_po_free[ci] = m_o
                m_ost_d[ob] = P.dma("scalar", g["out_span"][cc * 128:(cc + 1) * 128,
                                                           cg * 512:(cg + 1) * 512],
                                    ost[ob][:, :], dos, deps=[("v", m_o)])
        for ci in range(4):
            m_gxfree[ci] = (m_po_free[ci], m_po_free[ci])
            m_yffm_t[ci] = P.mark("t")

    P.drain()
    P.emit()
    es.close()


_NC_CACHE = {}


def kernel(L0_attn_hidden_states, L0_attn_W_in, L0_attn_conv_w, L0_attn_conv_b,
           L0_attn_dt_bias, L0_attn_A_log, L0_attn_D, L0_attn_norm_w, L0_attn_W_out,
           L0_ffn_weight, L0_ffn_scale_weight, L0_ffn_bias):
    x = np.ascontiguousarray(np.asarray(L0_attn_hidden_states, dtype=np.float32))
    ident_bf = np.eye(128, dtype=np.float32).astype(ml_dtypes.bfloat16)
    ident_f = np.eye(128, dtype=np.float32)
    mask = np.triu(np.ones((128, 128), dtype=np.float32))

    if "nc" not in _NC_CACHE:
        _NC_CACHE["nc"] = build_nc()
    nc = _NC_CACHE["nc"]

    def cmb_for(core):
        grp, sp = core // 4, core % 4
        m = np.zeros((9, 8), dtype=np.float32)
        for j in range(8):
            gj, sj = j // 4, j % 4
            if gj == grp and sj < sp:
                m[8, j] = 0.0
                for i in range(8):
                    gi, si = i // 4, i % 4
                    if gi == grp and sj < si < sp:
                        m[i, j] = 1.0
            else:
                m[8, j] = -BIGNEG
        return m

    in_maps = []
    for core in range(8):
        b_, sp = core // 4, core % 4
        s0 = sp * SPAN
        halo = (x[b_, s0 - 128:s0, :] if sp > 0
                else np.zeros((128, H), dtype=np.float32))
        in_maps.append({
            "x_span": np.ascontiguousarray(x[b_, s0:s0 + SPAN, :]),
            "x_halo": np.ascontiguousarray(halo),
            "w_in": L0_attn_W_in, "conv_w": L0_attn_conv_w, "conv_b": L0_attn_conv_b,
            "dt_bias": L0_attn_dt_bias, "a_log": L0_attn_A_log, "d_in": L0_attn_D,
            "norm_w": L0_attn_norm_w, "w_out": L0_attn_W_out,
            "w_ffn": L0_ffn_weight, "ffn_scale": L0_ffn_scale_weight,
            "ffn_bias": L0_ffn_bias,
            "ident_bf": ident_bf, "ident_f": ident_f, "mask_tril": mask,
            "cmb": cmb_for(core),
        })
    for m in in_maps:
        for k in list(m):
            m[k] = np.ascontiguousarray(np.asarray(m[k]))

    res = run_bass_kernel_spmd(nc, in_maps, core_ids=list(range(8)))
    out = np.empty((B, L, H), dtype=np.float32)
    for core in range(8):
        b_, sp = core // 4, core % 4
        out[b_, sp * SPAN:(sp + 1) * SPAN, :] = res.results[core]["out_span"]
    return out


# version 12
# speedup vs baseline: 1.1897x; 1.1897x over previous
import os
import numpy as np
import ml_dtypes
from contextlib import ExitStack

_NBLOCKS = int(os.environ.get("KBLOCKS", "3"))

import concourse.bass as bass
import concourse.mybir as mybir
from concourse.bass_utils import run_bass_kernel_spmd
from concourse.alu_op_type import AluOpType as OP

F32 = mybir.dt.float32
BF16 = mybir.dt.bfloat16
AF = mybir.ActivationFunctionType
AX = mybir.AxisListType

B, L, H = 2, 16384, 2048
IZ = 4096
NS = 128
NH, PH = 64, 64
KC = 4
XC = IZ + 2 * NS
XT = XC // 128
PROJ = 2 * (IZ + NS) + NH
SPAN = L // 4
T = 512
NTL = SPAN // T
CH = 128
NCH = SPAN // CH
CPT = T // CH
HK = H // 128
EPS = 1e-5
BIGNEG = 1e30

_DEBUG = False


class Prog:
    ENGS = ("tensor", "vector", "scalar", "gpsimd", "sync")

    def __init__(self, nc, tag):
        self.nc, self.tag = nc, tag
        self.q = {e: [] for e in self.ENGS}
        self.sems, self.cnt = {}, {}
        self.wm = {e: {} for e in self.ENGS}

    def _sem(self, name):
        key = f"{self.tag}_{name}"
        if key not in self.sems:
            self.sems[key] = self.nc.alloc_semaphore(key)
            self.cnt[key] = 0
        return key

    def op(self, eng, fname, *args, deps=(), inc=None, incamt=1, **kw):
        dl = []
        for (s, v) in deps:
            if v and v > 0:
                dl.append((self._sem(s), v))
        if inc is not None:
            inc = self._sem(inc)
            self.cnt[inc] += incamt
        self.q[eng].append((dl, fname, args, kw, inc, incamt))
        return self.cnt[inc] if inc else None

    def dma(self, eng, out, in_, inc, deps=(), fence=True, **kw):
        deps = list(deps)
        if fence and inc is not None:
            deps.append((inc, self.cnt.get(f"{self.tag}_{inc}", 0)))
        return self.op(eng, "dma_start", deps=deps, inc=inc, incamt=16,
                       out=out, in_=in_, **kw)

    def mark(self, name):
        return self.cnt.get(f"{self.tag}_{name}", 0)

    def drain(self):
        deps = [(k[len(self.tag) + 1:], v) for k, v in self.cnt.items()]
        self.op("sync", "nop", deps=deps)

    def emit(self):
        with self.nc.Block(self.tag) as block:
            for eng in self.ENGS:
                items = self.q[eng]
                if not items:
                    continue

                def body(E, items=items, eng=eng):
                    wm = self.wm[eng]
                    for dl, fname, args, kw, inc, incamt in items:
                        for (s, v) in dl:
                            if v > wm.get(s, 0):
                                E.wait_ge(self.sems[s], v)
                                wm[s] = v
                        inst = getattr(E, fname)(*args, **kw)
                        if inc:
                            inst.then_inc(self.sems[inc], incamt)
                getattr(block, eng)(body)


def b3(ap, a, b_):
    return ap.rearrange("p (a b) -> p a b", a=a)


def build_nc():
    nc = bass.Bass("TRN2", target_bir_lowering=False)
    g = {}

    def din(name, shape, dt=F32):
        g[name] = nc.dram_tensor(name, shape, dt, kind="ExternalInput").ap()
    din("x_span", [SPAN, H]); din("x_halo", [128, H])
    din("w_in", [PROJ, H]); din("conv_w", [XC, 1, KC]); din("conv_b", [XC])
    din("dt_bias", [NH]); din("a_log", [NH]); din("d_in", [NH]); din("norm_w", [IZ])
    din("w_out", [H, IZ]); din("w_ffn", [H, H]); din("ffn_scale", [H]); din("ffn_bias", [H])
    din("ident_bf", [128, 128], BF16); din("ident_f", [128, 128])
    din("mask_tril", [128, 128]); din("cmb", [9, 8])
    g["out_span"] = nc.dram_tensor("out_span", [SPAN, H], F32, kind="ExternalOutput").ap()

    sk = "ExternalOutput" if _DEBUG else "Internal"
    g["w_in_bf"] = nc.dram_tensor("w_in_bf", [PROJ, H], BF16).ap()
    g["w_out_bf"] = nc.dram_tensor("w_out_bf", [H, IZ], BF16).ap()
    g["w_ffn_bf"] = nc.dram_tensor("w_ffn_bf", [H, H], BF16).ap()
    g["xn_sp"] = nc.dram_tensor("xn_sp", [HK, 128, SPAN], BF16, kind=sk).ap()
    g["z_sp"] = nc.dram_tensor("z_sp", [NCH, 128, IZ], BF16, kind=sk).ap()
    g["g_sp"] = nc.dram_tensor("g_sp", [NCH, 128, H], BF16, kind=sk).ap()
    g["yp_sp"] = nc.dram_tensor("yp_sp", [NCH, 128, IZ], BF16, kind=sk).ap()
    g["acs_row"] = nc.dram_tensor("acs_row", [NCH, CH * NH], F32).ap()
    g["wx_t"] = nc.dram_tensor("wx_t", [9, HK, 128, 512], BF16).ap()
    g["wo_t"] = nc.dram_tensor("wo_t", [4, IZ // 128, 128, 512], BF16).ap()
    g["wz_t"] = nc.dram_tensor("wz_t", [8, HK, 128, 512], BF16).ap()
    g["wf_t"] = nc.dram_tensor("wf_t", [4, HK, 128, 512], BF16).ap()
    g["w_row"] = nc.dram_tensor("w_row", [8, NH], F32).ap()
    g["nwrow_d"] = nc.dram_tensor("nwrow_d", [IZ], F32).ap()
    g["ag_in"] = nc.dram_tensor("ag_in", [128, IZ + 64], F32).ap()
    g["ag_out"] = nc.dram_tensor("ag_out", [8 * 128, IZ + 64], F32).ap()
    if _DEBUG:
        g["dbg_e"] = nc.dram_tensor("dbg_e", [128, IZ + 64], F32, kind="ExternalOutput").ap()

    sb = nc.alloc_sbuf_tensor
    g["ident_bf_s"] = sb("ident_bf_s", [128, 128], BF16)
    g["ident_f_s"] = sb("ident_f_s", [128, 128], F32)
    g["mask_s"] = sb("mask_s", [128, 128], F32)
    g["ones_bf"] = sb("ones_bf", [1, 128], BF16)
    g["ones_f"] = sb("ones_f", [1, 128], F32)
    g["ones64"] = sb("ones64", [NH, 128], F32)
    g["epscol"] = sb("epscol", [128, 1], F32)
    g["onecol"] = sb("onecol", [128, 1], F32)
    g["dtb_s"] = sb("dtb_s", [NH, 1], F32)
    g["acol_s"] = sb("acol_s", [NH, 1], F32)
    g["cb_s"] = sb("cb_s", [128, XT], F32)
    g["cw_s"] = sb("cw_s", [128, KC * XT], F32)
    g["di_s"] = sb("di_s", [128, NH * 64], BF16)
    g["halo_fm"] = sb("halo_fm", [128, H], BF16)
    g["e_all"] = sb("e_all", [128, NH * PH], F32)
    g["cm_all"] = sb("cm_all", [128, 128 * NCH], BF16)
    g["acst_all"] = sb("acst_all", [128, NH * NCH], F32)
    g["aend_all"] = sb("aend_all", [NH, NCH], F32)

    _block1(nc, g)
    if _NBLOCKS >= 2:
        _block2(nc, g)
    if _NBLOCKS >= 3:
        _block3(nc, g)
    return nc


def _block1(nc, g):
    P = Prog(nc, "b1")
    es = ExitStack()
    QTR = SPAN // 4
    def sb(name, shape, dt):
        return es.enter_context(nc.sbuf_tensor(name, shape, dt))
    def ps(name, shape, dt):
        return es.enter_context(nc.psum_tensor(name, shape, dt))
    HALF = SPAN // 2

    xs = [sb("b1_xs0", [128, H], F32), sb("b1_xs1", [128, H], F32)]
    stat = sb("b1_stat", [128, 16], F32)
    xnt = [sb("b1_xnt0", [128, H], BF16), sb("b1_xnt1", [128, H], BF16)]
    xnfm = sb("b1_xnfm", [128, HK * QTR], BF16)
    wsl = [sb(f"b1_wsl{i}", [128, HK * 512], BF16) for i in range(3)]
    zst = [sb("b1_zst0", [128, 512], BF16), sb("b1_zst1", [128, 512], BF16)]
    wof = [sb("b1_wof0", [128, IZ], BF16)]
    wff = [sb("b1_wff0", [128, H], BF16), sb("b1_wff1", [128, H], BF16)]
    nwbc = sb("b1_nwbc", [128, IZ], BF16)
    fsc_s = sb("b1_fsc", [128, HK], F32)
    fbias_row = sb("b1_fbr", [1, H], BF16)
    tmp34 = sb("b1_tmp34", [XT, 512], F32)
    tmpk = sb("b1_tmpk", [XT, 128], F32)
    nw_t = sb("b1_nw_t", [IZ // 128, 128], F32)
    fs_t = sb("b1_fs_t", [HK, 128], F32)
    cwx = sb("b1_cwx", [XT, 128], F32)
    nwrow = sb("b1_nwrow", [1, 512], BF16)
    adcol = sb("b1_adcol", [NH, 2], F32)
    dbc = sb("b1_dbc", [128, NH], BF16)
    drow_f = sb("b1_drow", [1, NH], F32)

    ptA = ps("b1_ptA", [128, 2048], BF16)
    ptB = ps("b1_ptB", [128, 2048], BF16)
    pz = [ps("b1_pz0", [128, 512], F32), ps("b1_pz1", [128, 512], F32)]
    pm = ps("b1_pm", [128, 512], F32)

    n_c = 0
    def cdma(dst, src):
        nonlocal n_c
        n_c = P.dma("sync", dst, src, "dc")
        return n_c
    cdma(g["ident_bf_s"][:, :], g["ident_bf"])
    cdma(g["ident_f_s"][:, :], g["ident_f"])
    cdma(g["mask_s"][:, :], g["mask_tril"])
    cdma(g["dtb_s"][:, :], g["dt_bias"].unsqueeze(1))
    cdma(adcol[:, 0:1], g["a_log"].unsqueeze(1))
    cdma(adcol[:, 1:2], g["d_in"].unsqueeze(1))
    cdma(tmpk[:, :], g["conv_b"].rearrange("(t p) -> t p", p=128))
    cdma(tmp34[:, :], g["conv_w"].rearrange("(t p) o k -> t (p o k)", p=128))
    cdma(nw_t[:, :], g["norm_w"].rearrange("(t p) -> t p", p=128))
    cdma(fs_t[:, :], g["ffn_scale"].rearrange("(t p) -> t p", p=128))
    m_call = n_c

    m_ms = P.op("vector", "memset", g["ones_f"][:, :], 1.0, inc="v")
    P.op("vector", "memset", g["ones64"][:, :], 1.0, inc="v")
    P.op("vector", "memset", g["epscol"][:, :], EPS, inc="v")
    P.op("vector", "memset", g["onecol"][:, :], 1.0, inc="v")
    m_ones = P.op("vector", "memset", g["ones_bf"][:, :], 1.0, inc="v")

    m_cast_half = 0
    for i in range(4):
        r = PROJ // 4
        m_cast_win = P.dma("gpsimd", g["w_in_bf"][i * r:(i + 1) * r, :],
                           g["w_in"][i * r:(i + 1) * r, :], "dcw", fence=(i == 2))
        if i == 1:
            m_cast_half = m_cast_win

    mt = me = 0
    def small_tr(dst, src, idn, w):
        nonlocal mt, me
        mt = P.op("tensor", "transpose", pm[0:w, 0:src.shape[0]].bitcast(F32), src,
                  idn, deps=[("dc", m_call), ("s", me), ("v", m_ms)], inc="t")
        me = P.op("scalar", "copy", dst, pm[0:w, 0:src.shape[0]], deps=[("t", mt)], inc="s")
        return me
    small_tr(g["cb_s"][:, :], tmpk[:, :], g["ident_f_s"][0:XT, 0:XT], 128)
    for k in range(KC):
        m_cx = P.op("vector", "tensor_copy", cwx[:, :],
                    tmp34[:, :].rearrange("t (p k) -> t k p", k=KC)[:, k],
                    deps=[("dc", m_call), ("s", me)], inc="v")
        mt = P.op("tensor", "transpose", pm[0:128, 0:XT].bitcast(F32), cwx[:, :],
                  g["ident_f_s"][0:XT, 0:XT], deps=[("v", m_cx), ("s", me)], inc="t")
        me = P.op("scalar", "copy", g["cw_s"][:, k * XT:(k + 1) * XT], pm[0:128, 0:XT],
                  deps=[("t", mt)], inc="s")
    nw_s = sb("b1_nw_s", [128, IZ // 128], F32)
    small_tr(nw_s[:, :], nw_t[:, :], g["ident_f_s"][0:32, 0:32], 128)
    small_tr(fsc_s[:, :], fs_t[:, :], g["ident_f_s"][0:HK, 0:HK], 128)
    m_a = P.op("scalar", "activation", g["acol_s"][:, :], adcol[:, 0:1], AF.Exp,
               deps=[("dc", m_call), ("s", me)], inc="s")
    m_a = P.op("vector", "tensor_scalar", g["acol_s"][:, :], g["acol_s"][:, :], -1.0,
               None, OP.mult, deps=[("s", m_a)], inc="v")
    mt = P.op("tensor", "transpose", pm[0:1, 0:NH].bitcast(F32), adcol[:, 1:2],
              g["ident_f_s"][0:NH, 0:NH], deps=[("s", me), ("dc", m_call)], inc="t")
    me = P.op("scalar", "copy", drow_f[:, :], pm[0:1, 0:NH], deps=[("t", mt)], inc="s")
    mt = P.op("tensor", "matmul", pm[0:128, 64:128], g["ones_f"][0:1, :], drow_f[:, :],
              start=True, stop=True, deps=[("s", me), ("v", m_ms)], inc="t")
    me = P.op("scalar", "copy", dbc[:, :], pm[0:128, 64:128], deps=[("t", mt)], inc="s")
    for h in range(NH):
        P.op("vector", "tensor_tensor", g["di_s"][0:64, h * 64:(h + 1) * 64],
             g["ident_bf_s"][0:64, 0:64],
             dbc[0:64, h:h + 1].broadcast_to((64, 64)), OP.mult,
             deps=[("s", me)], inc="v")
        P.op("vector", "tensor_tensor", g["di_s"][64:128, h * 64:(h + 1) * 64],
             g["ident_bf_s"][64:128, 64:128],
             dbc[64:128, h:h + 1].broadcast_to((64, 64)), OP.mult,
             deps=[("s", me)], inc="v")
    m_di = P.mark("v")

    m_nf = P.dma("gpsimd", g["nwrow_d"].rearrange("(t p) -> t p", p=128), nw_t[:, :],
                 "dnw", deps=[("dc", m_call)])
    for j in range(IZ // 512):
        m_nr = P.dma("gpsimd", nwrow[:, :],
                     g["nwrow_d"][j * 512:(j + 1) * 512].unsqueeze(0), "dnw",
                     deps=[("t", mt)])
        mt = P.op("tensor", "matmul", pm[:, :], g["ones_bf"][0:1, :],
                  nwrow[:, :], start=True, stop=True,
                  deps=[("dnw", m_nr), ("s", me), ("v", m_ones)], inc="t")
        me = P.op("scalar", "copy", nwbc[:, j * 512:(j + 1) * 512], pm[:, :],
                  deps=[("t", mt)], inc="s")
    m_nwbc = me
    m_fbr_d2 = P.dma("gpsimd", fbias_row[:, :], g["ffn_bias"].unsqueeze(0), "dfb")

    m_use = [0]
    for i in range(HK):
        bi = 0
        m_d = P.dma("gpsimd", wof[bi][:, :], g["w_out"][i * 128:(i + 1) * 128, :], "dwo",
                    deps=[("dso", m_use[bi])])
        m_f = P.op("vector", "tensor_tensor", wof[bi][:, :], wof[bi][:, :], nwbc[:, :],
                   OP.mult, deps=[("dwo", m_d), ("s", m_nwbc)], inc="v")
        m_use[bi] = P.dma("scalar", g["w_out_bf"][i * 128:(i + 1) * 128, :], wof[bi][:, :],
                          "dso", deps=[("v", m_f)])
    m_use = [0, 0]
    for i in range(HK):
        bi = i % 2
        m_d = P.dma("gpsimd", wff[bi][:, :], g["w_ffn"][i * 128:(i + 1) * 128, :],
                    "dwf0" if bi == 0 else "dwf1",
                    deps=[("dsf", m_use[bi])])
        m_f = P.op("vector", "tensor_scalar", wff[bi][:, :], wff[bi][:, :],
                   fsc_s[:, i:i + 1], None, OP.mult,
                   deps=[("dwf0" if bi == 0 else "dwf1", m_d), ("s", me)], inc="v")
        m_use[bi] = P.dma("scalar", g["w_ffn_bf"][i * 128:(i + 1) * 128, :], wff[bi][:, :],
                          "dsf", deps=[("v", m_f)])
    m_wffn_sp = P.mark("dsf")

    bt = [sb(f"b1_bt{i}", [128, 2048], BF16) for i in range(4)]
    m_btw = [0, 0, 0, 0]
    bi_c = 0

    def bsems(bi):
        return f"dbt{bi}", f"dbw{bi}"

    def build4(src_name, r0, kcol, dst_list, dep_extra):
        nonlocal bi_c
        bi = bi_c % 4
        bi_c += 1
        dbs, dbw = bsems(bi)
        m_l = P.dma("sync", bt[bi][:, :],
                    g[src_name][r0:r0 + 2048, kcol * 128:(kcol + 1) * 128],
                    dbs, deps=dep_extra + [(dbw, m_btw[bi])], transpose=True)
        m_w = 0
        for i, dst in enumerate(dst_list):
            m_w = P.dma("gpsimd", dst, bt[bi][:, i * 512:(i + 1) * 512], dbw,
                        deps=[(dbs, m_l)], fence=(i == 0))
        m_btw[bi] = m_w

    for k in range(HK):
        build4("w_in_bf", 0, k, [g["wz_t"][og, k] for og in range(4)],
               [("dcw", m_cast_half)])
        build4("w_in_bf", 2048, k, [g["wz_t"][og, k] for og in range(4, 8)],
               [("dcw", m_cast_half)])
    m_wz_built = [P.mark(f"dbw{i}") for i in range(4)]
    m_wf_sp = P.mark("dsf")
    for k in range(HK):
        build4("w_ffn_bf", 0, k, [g["wf_t"][og, k] for og in range(4)],
               [("dsf", m_wf_sp)])
    m_wf_built = [P.mark(f"dbw{i}") for i in range(4)]
    for k in range(HK):
        build4("w_in_bf", IZ, k, [g["wx_t"][og, k] for og in range(4)],
               [("dcw", m_cast_win)])
        build4("w_in_bf", IZ + 2048, k, [g["wx_t"][og, k] for og in range(4, 8)],
               [("dcw", m_cast_win)])
    for k in range(HK):
        bi = bi_c % 4
        bi_c += 1
        dbs, dbw = bsems(bi)
        P.dma("sync", bt[bi][:, 0:256],
              g["w_in_bf"][2 * IZ:2 * IZ + 256, k * 128:(k + 1) * 128],
              dbs, deps=[("dcw", m_cast_win), (dbw, m_btw[bi])], transpose=True)
        m_l = P.dma("sync", bt[bi][:, 256:384],
                    g["w_in_bf"][PROJ - 128:PROJ, k * 128:(k + 1) * 128],
                    dbs, transpose=True, fence=False)
        m_btw[bi] = P.dma("gpsimd", g["wx_t"][8, k].rearrange("p c -> p c")[:, 0:384],
                          bt[bi][:, 0:384], dbw, deps=[(dbs, m_l)])
    m_wo_sp = P.mark("dso")
    for k in range(IZ // 128):
        build4("w_out_bf", 0, k, [g["wo_t"][cg, k] for cg in range(4)],
               [("dso", m_wo_sp)])

    m_xn_s = [0, 0]
    m_ev_v = [0, 0]
    m_trm = [0, 0]

    def ln_block(bi, src, dst_fm, extra_ev_deps=()):
        dxs = "dx0" if bi == 0 else "dx1"
        m_x = P.dma("scalar", xs[bi][:, :], src, dxs, deps=[("s", m_xn_s[bi])])
        o = bi * 8
        m_sum = P.op("vector", "tensor_reduce", stat[:, o:o + 1], xs[bi][:, :], AX.X,
                     OP.add, deps=[(dxs, m_x), ("s", m_xn_s[bi])], inc="v")
        m_sq = P.op("scalar", "activation", xnt[bi][:, :], xs[bi][:, :], AF.Square,
                    accum_out=stat[:, o + 1:o + 2],
                    deps=[(dxs, m_x), ("v", m_sum), ("t", m_trm[bi])], inc="s")
        m_mean = P.op("scalar", "mul", stat[:, o + 2:o + 3], stat[:, o:o + 1], 1.0 / H,
                      deps=[("s", m_sq)], inc="s")
        m_v1 = P.op("vector", "scalar_tensor_tensor", stat[:, o + 3:o + 4],
                    stat[:, o + 2:o + 3], 1.0, stat[:, o + 2:o + 3], OP.bypass, OP.mult,
                    deps=[("s", m_mean)], inc="v")
        m_v2 = P.op("vector", "scalar_tensor_tensor", stat[:, o + 4:o + 5],
                    stat[:, o + 1:o + 2], 1.0 / H, stat[:, o + 3:o + 4], OP.mult,
                    OP.subtract, deps=[("v", m_v1)], inc="v")
        m_ln = P.op("scalar", "activation", stat[:, o + 5:o + 6], stat[:, o + 4:o + 5],
                    AF.Ln, bias=g["epscol"][:, 0:1], deps=[("v", m_v2)], inc="s")
        m_rs = P.op("scalar", "activation", stat[:, o + 6:o + 7], stat[:, o + 5:o + 6],
                    AF.Exp, scale=-0.5, deps=[("s", m_ln)], inc="s")
        m_nm = P.op("vector", "scalar_tensor_tensor", stat[:, o + 7:o + 8],
                    stat[:, o + 2:o + 3], -1.0, stat[:, o + 6:o + 7], OP.mult, OP.mult,
                    deps=[("s", m_rs)], inc="v")
        m_xn = P.op("scalar", "activation", xnt[bi][:, :], xs[bi][:, :], AF.Identity,
                    bias=stat[:, o + 7:o + 8], scale=stat[:, o + 6:o + 7],
                    deps=[("v", m_nm), ("t", m_trm[bi])], inc="s")
        m_xn_s[bi] = m_xn
        pt = ptA if bi == 0 else ptB
        m_tr = 0
        for k in range(HK):
            m_tr = P.op("tensor", "transpose", pt[:, k * 128:(k + 1) * 128].bitcast(BF16),
                        xnt[bi][:, k * 128:(k + 1) * 128], g["ident_bf_s"][:, :],
                        deps=([("s", m_xn), ("v", m_ev_v[bi]), ("dc", 16 * 2)]
                              if k == 0 else []),
                        inc=("t" if k == HK - 1 else None))
        m_trm[bi] = m_tr
        deps = [("t", m_tr)] + list(extra_ev_deps)
        m_ev = P.op("vector", "tensor_copy", dst_fm, b3(pt[:, :], HK, 128),
                    deps=deps, inc="v")
        m_ev_v[bi] = m_ev
        return m_ev

    m_halo = ln_block(0, g["x_halo"], b3(g["halo_fm"][:, :], HK, 128))

    NCG_Z, NCG_F = IZ // 512, H // 512
    m_a2_t = 0
    m_xnsp_prev = 0
    m_wu = [0, 0]
    m_ev_s = [0, 0]
    m_zdma = [0, 0]
    for half in range(4):
        for j in range(8):
            tb = half * 8 + j
            dst = b3(xnfm[:, :], HK, QTR)[:, :, j * 128:(j + 1) * 128]
            ln_block(j % 2, g["x_span"][tb * 128:(tb + 1) * 128, :], dst,
                     extra_ev_deps=[("t", m_a2_t), ("dxn", m_xnsp_prev)])
        m_half_v = P.mark("v")
        for k in range(HK):
            m_xnsp_prev = P.dma("scalar", g["xn_sp"][k, :, half * QTR:(half + 1) * QTR],
                                xnfm[:, k * QTR:(k + 1) * QTR], "dxn",
                                deps=[("v", m_half_v)], fence=(k == 0))
        for cg in range(NCG_Z + NCG_F):
            bi = cg % 2
            is_ffn = cg >= NCG_Z
            cgl = cg - NCG_Z if is_ffn else cg
            wt = g["wf_t"] if is_ffn else g["wz_t"]
            dws = "dwA" if bi == 0 else "dwB"
            bdep = m_wf_built if is_ffn else m_wz_built
            m_wd = P.dma("scalar", wsl[bi][:, :].rearrange("p (k c) -> p k c", k=HK),
                         wt[cgl].rearrange("k p c -> p k c"), dws,
                         deps=[("t", m_wu[bi])] + [(f"dbw{i}", bdep[i]) for i in range(4)])
            for j in range(8):
                tb = half * 8 + j
                pb = j % 2
                for k in range(HK):
                    m_mm = P.op("tensor", "matmul", pz[pb][:, :],
                                xnfm[:, k * QTR + j * 128:k * QTR + (j + 1) * 128],
                                wsl[bi][:, k * 512:(k + 1) * 512],
                                start=(k == 0), stop=(not is_ffn and k == HK - 1),
                                deps=([("v", m_half_v), (dws, m_wd), ("s", m_ev_s[pb])]
                                      if k == 0 else []),
                                inc=("t" if (not is_ffn and k == HK - 1) else None))
                if is_ffn:
                    m_mm = P.op("tensor", "matmul", pz[pb][:, :], g["ones_bf"][0:1, :],
                                fbias_row[:, cgl * 512:(cgl + 1) * 512],
                                start=False, stop=True,
                                deps=[("dfb", m_fbr_d2), ("v", m_ones)], inc="t")
                dzp = "dzp0" if pb == 0 else "dzp1"
                if is_ffn:
                    m_e = P.op("scalar", "activation", zst[pb][:, :], pz[pb][:, :],
                               AF.Silu, deps=[("t", m_mm), (dzp, m_zdma[pb])], inc="s")
                    m_zdma[pb] = P.dma("scalar", g["g_sp"][tb, :, cgl * 512:(cgl + 1) * 512],
                                       zst[pb][:, :], dzp, deps=[("s", m_e)])
                else:
                    m_e = P.op("scalar", "activation", zst[pb][:, :], pz[pb][:, :],
                               AF.Silu, deps=[("t", m_mm), (dzp, m_zdma[pb])], inc="s")
                    m_zdma[pb] = P.dma("scalar", g["z_sp"][tb, :, cg * 512:(cg + 1) * 512],
                                       zst[pb][:, :], dzp, deps=[("s", m_e)])
                m_ev_s[pb] = m_e
            m_wu[bi] = P.mark("t")
        m_a2_t = P.mark("t")

    P.drain()
    P.emit()
    es.close()


def _block2(nc, g):
    P = Prog(nc, "b2")
    es = ExitStack()
    def sb(name, shape, dt):
        return es.enter_context(nc.sbuf_tensor(name, shape, dt))
    def ps(name, shape, dt):
        return es.enter_context(nc.psum_tensor(name, shape, dt))
    TP = T + 3

    xn_t = sb("b2_xnt", [128, HK * T], BF16)
    wsl = [sb("b2_wsl0", [128, HK * 512], BF16), sb("b2_wsl1", [128, HK * 512], BF16)]
    xbc = sb("b2_xbc", [128, XT * TP], BF16)
    dgw = [sb("b2_dgw0", [128, KC * 128], BF16), sb("b2_dgw1", [128, KC * 128], BF16)]
    dt_pre = sb("b2_dtpre", [NH, T], F32)
    sp1 = sb("b2_sp1", [NH, T], F32)
    dt_fm = sb("b2_dtfm", [NH, T], F32)
    abuf = sb("b2_abuf", [NH, CH], F32)
    acs_fm = sb("b2_acsfm", [NH, CH], F32)
    dcy = sb("b2_dcy", [NH, CH], F32)
    dsd = sb("b2_dsd", [128, CH], F32)
    dtsdt = sb("b2_dtsdt", [128, 4 * 128], F32)
    d_bc = sb("b2_dbc", [128, 4 * NH], F32)
    ddiag = sb("b2_ddiag", [NH, NH], F32)
    dexp = sb("b2_dexp", [NH, 1], F32)
    bcst = [sb(f"b2_bc{i}", [128, 1024], F32) for i in range(3)]
    m_hT = sb("b2_mh", [128, CH * NH], BF16)
    scm = sb("b2_scm", [128, 128], BF16)
    bmT = sb("b2_bmT", [128, NS], BF16)
    xsdtT = sb("b2_xsdtT", [128, IZ], BF16)
    xsdT = sb("b2_xsdT", [128, IZ], BF16)
    e_bf = sb("b2_ebf", [128, IZ], BF16)
    y_tok = sb("b2_ytok", [128, IZ], BF16)
    expacs = sb("b2_expacs", [128, 4 * NH], F32)

    pp = [ps("b2_pp0", [128, 512], F32), ps("b2_pp1", [128, 512], F32)]
    pcv = ps("b2_pcv", [128, 512], F32)
    ptr = [ps("b2_ptrA", [128, 1024], BF16), ps("b2_ptrB", [128, 1024], BF16)]
    psm = ps("b2_psm", [128, 512], F32)
    pyA = ps("b2_pyA", [128, 512], F32)
    pyB = ps("b2_pyB", [128, 512], F32)

    idb, idf, mask = g["ident_bf_s"], g["ident_f_s"], g["mask_s"]

    m_e0 = P.op("vector", "memset", g["e_all"][:, :], 0.0, inc="v")

    def xact(i):
        return xbc[:, i * TP + 3:i * TP + 3 + T]

    m_wu = [0, 0]

    def proj(rhs_fn, ncols, evac_fn, m_rhs, m_ev0, m_vguard=0):
        m_ev = m_ev0
        m_evb = [m_ev0, m_ev0]
        for ocg in range(9):
            bi = ocg % 2
            w = wsl[bi]
            dws = "dwA" if bi == 0 else "dwB"
            ncol = 512 if ocg < 8 else 384
            m_wd = P.dma("sync",
                         w[:, :].rearrange("p (k c) -> p k c", k=HK)[:, :, 0:ncol],
                         g["wx_t"][ocg].rearrange("k p c -> p k c")[:, :, 0:ncol],
                         dws, deps=[("t", m_wu[bi])])
            nblk = 4 if ocg < 8 else 3
            for ob in range(nblk):
                oc = ocg * 4 + ob
                pb = (ocg * 4 + ob) % 2
                is_dt = (ocg == 8 and ob == 2)
                rows = NH if is_dt else 128
                for k in range(HK):
                    if is_dt:
                        lhsT = w[:, k * 512 + 320:k * 512 + 384]
                    else:
                        lhsT = w[:, k * 512 + ob * 128:k * 512 + (ob + 1) * 128]
                    m_mm = P.op("tensor", "matmul", pp[pb][0:rows, 0:ncols], lhsT,
                                rhs_fn(k), start=(k == 0), stop=(k == HK - 1),
                                deps=([(dws, m_wd), ("dxt", m_rhs), ("s", m_evb[pb]),
                                       ("v", m_vguard)]
                                      if k == 0 else []),
                                inc=("t" if k == HK - 1 else None))
                m_ev = evac_fn(oc, pp[pb], m_mm)
                m_evb[pb] = m_ev
            m_wu[bi] = P.mark("t")
        return m_ev

    def ev_pro(oc, pz, m_mm):
        if oc < XT:
            return P.op("scalar", "copy", xbc[:, oc * TP:oc * TP + 3], pz[0:128, 0:3],
                        deps=[("t", m_mm)], inc="s")
        return P.op("scalar", "copy", dt_pre[:, 0:3], pz[0:NH, 0:3],
                    deps=[("t", m_mm)], inc="s")

    m_ev = proj(lambda k: g["halo_fm"][:, k * 128 + 125:k * 128 + 128], 3, ev_pro, 0, 0)

    m_states_t = 0
    m_chunk_v = P.mark("v")
    m_ydma = 0
    m_conv_s = [m_ev] * XT
    m_dtfree_v = 0
    m_trfree = [0, 0]
    m_mh_t = 0
    m_xsd_t = 0
    m_dgw_use = [0, 0]
    m_scm_prev = 0
    m_bev_prev = 0
    m_dd_t = 0
    m_dd_prev = 0
    m_acsT_prev = 0
    m_de_prev = 0
    m_st_sadd = 0
    m_sexp_slot = [0, 0, 0]
    m_fl_prev = 0
    m_yslot = [0] * 8
    m_fold_r = [0, 0, 0, 0]

    for t in range(NTL):
        m_xn = 0
        for k in range(HK):
            m_xn = P.dma("scalar", xn_t[:, k * T:(k + 1) * T],
                         g["xn_sp"][k, :, t * T:(t + 1) * T], "dxt",
                         deps=[("t", m_states_t)] if k == 0 else [],
                         fence=(k == 0))

        m_ssd_t = m_states_t
        m_ssd_v = m_chunk_v

        def ev_main(oc, pz, m_mm):
            if oc < XT:
                return P.op("scalar", "copy", xbc[:, oc * TP + 3:oc * TP + 3 + T],
                            pz[0:128, 0:T],
                            deps=[("t", m_mm), ("s", m_conv_s[oc]),
                                  ("t", m_ssd_t), ("v", m_ssd_v)],
                            inc="s")
            return P.op("scalar", "copy", dt_pre[:, :], pz[0:NH, 0:T],
                        deps=[("t", m_mm), ("v", m_dtfree_v)], inc="s")

        m_ev = proj(lambda k: xn_t[:, k * T:(k + 1) * T], T, ev_main, m_xn, m_ev,
                    m_vguard=m_ssd_v)
        m_proj_s = P.mark("s")

        m_cv_bank = [m_proj_s, m_proj_s]
        for i in range(XT):
            di = i % 2
            for k in range(KC):
                P.op("vector", "tensor_tensor",
                     dgw[di][:, k * 128:(k + 1) * 128], idb[:, :],
                     g["cw_s"][:, k * XT + i:k * XT + i + 1].broadcast_to((128, 128)),
                     OP.mult, deps=[("t", m_dgw_use[di])] if k == 0 else [], inc="v")
            m_dg = P.mark("v")
            for k in range(KC):
                m_cm = P.op("tensor", "matmul", pp[di][:, 0:T],
                            dgw[di][:, k * 128:(k + 1) * 128],
                            xbc[:, i * TP + k:i * TP + k + T],
                            start=(k == 0), stop=(k == KC - 1),
                            deps=([("s", m_proj_s), ("v", m_dg), ("s", m_cv_bank[di]),
                                   ("v", m_chunk_v)] if k == 0 else []),
                            inc=("t" if k == KC - 1 else None))
            m_dgw_use[di] = m_cm
            m_h = P.op("vector", "tensor_copy", xbc[:, i * TP:i * TP + 3],
                       xbc[:, i * TP + T:i * TP + T + 3], deps=[("t", m_cm)], inc="v")
            m_cv_bank[di] = P.op("scalar", "activation",
                                 xbc[:, i * TP + 3:i * TP + 3 + T],
                                 pp[di][:, 0:T], AF.Silu, bias=g["cb_s"][:, i:i + 1],
                                 deps=[("t", m_cm), ("v", m_h)], inc="s")
            m_conv_s[i] = m_cv_bank[di]
        m_conv_all = P.mark("s")

        m1 = P.op("vector", "tensor_scalar", dt_pre[:, :], dt_pre[:, :],
                  g["dtb_s"][:, 0:1], None, OP.add, deps=[("s", m_proj_s)], inc="v")
        m2 = P.op("vector", "scalar_tensor_tensor", sp1[:, :], dt_pre[:, :], -1.0,
                  dt_pre[:, :], OP.mult, OP.max, deps=[("v", m1)], inc="v")
        m3 = P.op("scalar", "activation", sp1[:, :], sp1[:, :], AF.Exp, scale=-1.0,
                  deps=[("v", m2)], inc="s")
        m4 = P.op("scalar", "activation", sp1[:, :], sp1[:, :], AF.Ln,
                  bias=g["onecol"][0:NH, 0:1], deps=[("s", m3)], inc="s")
        m5 = P.op("vector", "tensor_scalar", dt_fm[:, :], dt_pre[:, :], 0.0, None,
                  OP.max, deps=[("s", m4)], inc="v")
        m_dt = P.op("vector", "tensor_tensor", dt_fm[:, :], dt_fm[:, :], sp1[:, :],
                    OP.add, deps=[("v", m5)], inc="v")

        m_fl4 = [0, 0, 0, 0]
        me1l = [0, 0, 0, 0]
        me3l = 0
        m_d2p = 0
        for c in range(CPT):
            cc = t * CPT + c
            m_ah = P.op("vector", "tensor_scalar", abuf[:, :], dt_fm[:, c * CH:(c + 1) * CH],
                        g["acol_s"][:, 0:1], None, OP.mult,
                        deps=[("v", m_dt), ("v", m_ssd_v)], inc="v")
            m_acs = P.op("vector", "tensor_tensor_scan", acs_fm[:, :], abuf[:, :],
                         abuf[:, :], 0.0, OP.add, OP.bypass,
                         deps=[("v", m_ah), ("t", m_acsT_prev), ("s", m_de_prev),
                               ("dfl", m_fl_prev)],
                         inc="v")
            P.op("vector", "tensor_copy", g["aend_all"][:, cc:cc + 1],
                 acs_fm[:, CH - 1:CH], deps=[("v", m_acs)], inc="v")
            m_dcy = P.op("scalar", "activation", dcy[:, :], acs_fm[:, :], AF.Exp,
                         scale=-1.0, bias=acs_fm[:, CH - 1:CH],
                         deps=[("v", m_acs), ("s", me1l[max(0, c - 1)])], inc="s")
            m_d1 = P.op("vector", "tensor_copy", dsd[0:NH, :], dt_fm[:, c * CH:(c + 1) * CH],
                        deps=[("v", m_dt), ("s", me1l[max(0, c - 1)]),
                              ("t", m_states_t)], inc="v")
            m_d2 = P.op("vector", "tensor_tensor", dsd[NH:128, :],
                        dt_fm[:, c * CH:(c + 1) * CH],
                        dcy[:, :], OP.mult, deps=[("s", m_dcy), ("v", m_d1)], inc="v")
            m_de = P.op("scalar", "activation", dexp[:, :], acs_fm[:, CH - 1:CH],
                        AF.Exp, deps=[("v", m_acs), ("v", m_dd_prev)], inc="s")
            m_de_prev = m_de
            m_dd = P.op("vector", "tensor_scalar", ddiag[:, :], idf[0:NH, 0:NH],
                        dexp[:, 0:1], None, OP.mult, deps=[("s", m_de), ("t", m_dd_t)],
                        inc="v")
            m_dd_prev = m_dd
            mt1 = P.op("tensor", "transpose", psm[:, 128:256].bitcast(F32), dsd[:, :],
                       idf[:, :], deps=[("v", m_d2), ("v", m_scm_prev),
                                        ("v", m_ssd_v), ("s", me3l)], inc="t")
            me1l[c] = P.op("scalar", "copy", dtsdt[:, c * 128:(c + 1) * 128],
                           psm[:, 128:256], deps=[("t", mt1)], inc="s")
            mt2 = P.op("tensor", "transpose", psm[:, 256:320].bitcast(F32), acs_fm[:, :],
                       idf[0:NH, 0:NH], deps=[("v", m_acs), ("s", me1l[c])], inc="t")
            m_acsT_prev = mt2
            me2 = P.op("scalar", "copy", g["acst_all"][:, cc * NH:(cc + 1) * NH],
                       psm[:, 256:320], deps=[("t", mt2)], inc="s")
            mt3 = P.op("tensor", "matmul", psm[:, 320:384], g["ones64"][:, :], ddiag[:, :],
                       start=True, stop=True, deps=[("v", m_dd), ("s", me2)], inc="t")
            m_dd_t = mt3
            me3l = P.op("scalar", "copy", d_bc[:, c * NH:(c + 1) * NH],
                        psm[:, 320:384], deps=[("t", mt3)], inc="s")
            m_fl4[c] = P.dma("gpsimd", g["acs_row"][cc].rearrange("(h l) -> h l", h=NH),
                             acs_fm[:, :], "dfl", deps=[("v", m_acs)])
            m_fl_prev = m_fl4[c]
        cc0 = t * CPT
        m_ex = P.op("scalar", "activation", expacs[:, :],
                    g["acst_all"][:, cc0 * NH:(cc0 + 4) * NH], AF.Exp,
                    deps=[("s", me3l), ("v", m_ssd_v)], inc="s")

        m_scm_c = {}
        m_esnap_c = {}
        m_bev_c = {}
        m_mh_jc = {}
        m_exp_jc = {}

        def stage_pre(c):
            nonlocal m_scm_prev
            cc = t * CPT + c
            csl = slice(c * CH, (c + 1) * CH)
            msc = P.op("tensor", "matmul", psm[:, 0:128], xact(32)[:, csl],
                       xact(33)[:, csl], start=True, stop=True,
                       deps=[("s", m_conv_all), ("s", me3l), ("v", m_scm_prev),
                             ("v", m_st_sadd)], inc="t")
            m_scm_c[c] = P.op("vector", "tensor_tensor", scm[:, :], psm[:, 0:128],
                              mask[:, :], OP.mult,
                              deps=[("t", msc),
                                    ("v", m_mh_jc[c - 1][7] if c > 0 else 0)],
                              inc="v")
            m_scm_prev = m_scm_c[c]
            P.op("vector", "tensor_copy", g["cm_all"][:, cc * 128:(cc + 1) * 128],
                 xact(33)[:, csl], deps=[("s", m_conv_all)], inc="v")

        def stage_folds(c):
            nonlocal m_bev_prev
            csl = slice(c * CH, (c + 1) * CH)
            for rnd in range(4):
                pt = ptr[rnd % 2]
                for i in range(8):
                    m_tr = P.op("tensor", "transpose",
                                pt[:, i * 128:(i + 1) * 128].bitcast(BF16),
                                xact(rnd * 8 + i)[:, csl], idb[:, :],
                                deps=([("s", m_conv_all), ("v", m_trfree[rnd % 2]),
                                       ("s", m_bev_prev if rnd == 0 else 0),
                                       ("t", m_mh_t), ("t", m_xsd_t)] if i == 0 else []),
                                inc=("t" if i == 7 else None))
                h0 = rnd * 16
                src2 = pt[:, :].rearrange("p (a b c) -> p a b c", a=8, b=2)
                dtop = b3(dtsdt[:, c * 128 + h0:c * 128 + h0 + 16], 8, 2).unsqueeze(3) \
                    .broadcast_to((128, 8, 2, 64))
                sdop = b3(dtsdt[:, c * 128 + NH + h0:c * 128 + NH + h0 + 16], 8, 2) \
                    .unsqueeze(3).broadcast_to((128, 8, 2, 64))
                m_f1 = P.op("vector", "tensor_tensor",
                            xsdtT[:, rnd * 1024:(rnd + 1) * 1024]
                            .rearrange("p (a b c) -> p a b c", a=8, b=2),
                            src2, dtop, OP.mult,
                            deps=[("t", m_tr), ("s", me1l[c])], inc="v")
                m_f2 = P.op("vector", "tensor_tensor",
                            xsdT[:, rnd * 1024:(rnd + 1) * 1024]
                            .rearrange("p (a b c) -> p a b c", a=8, b=2),
                            src2, sdop, OP.mult, deps=[("v", m_f1)], inc="v")
                m_trfree[rnd % 2] = m_f2
                m_fold_r[rnd] = m_f1
            m_btr = P.op("tensor", "transpose", ptr[0][:, 0:128].bitcast(BF16),
                         xact(32)[:, csl], idb[:, :], deps=[("v", m_trfree[0])],
                         inc="t")
            m_bev_c[c] = P.op("scalar", "copy", bmT[:, :], ptr[0][:, 0:128],
                              deps=[("t", m_btr)], inc="s")
            m_bev_prev = m_bev_c[c]

        def stage_esnap(c):
            m_esnap_c[c] = P.op("vector", "tensor_copy", e_bf[:, :], g["e_all"][:, :],
                                deps=[("t", m_mh_t)], inc="v")

        def emit_mult(c, j):
            m_mh_jc[c][j] = P.op("vector", "tensor_tensor",
                                 b3(m_hT[:, j * 1024:(j + 1) * 1024], 8, 128),
                                 b3(m_hT[:, j * 1024:(j + 1) * 1024], 8, 128),
                                 scm[:, :].unsqueeze(1).broadcast_to((128, 8, 128)),
                                 OP.mult,
                                 deps=[("s", m_exp_jc[c][j]), ("v", m_scm_c[c])],
                                 inc="v")

        def piece(c, j):
            cc = t * CPT + c
            acsT_c = g["acst_all"][:, cc * NH:(cc + 1) * NH]
            pbc = (c * 8 + j) % 3
            dbs = f"dbc{pbc}"
            m_bc = P.dma("gpsimd", bcst[pbc][:, :],
                         g["acs_row"][cc][j * 1024:(j + 1) * 1024].unsqueeze(0)
                         .partition_broadcast(128), dbs,
                         deps=[("dfl", m_fl4[c]), ("s", m_sexp_slot[pbc])])
            m_sub = P.op("vector", "scalar_tensor_tensor",
                         b3(bcst[pbc][:, :], 8, 128), b3(bcst[pbc][:, :], 8, 128),
                         1.0, acsT_c[:, 8 * j:8 * j + 8].unsqueeze(2)
                         .broadcast_to((128, 8, 128)),
                         OP.bypass, OP.subtract,
                         deps=[(dbs, m_bc)], inc="v")
            m_min = P.op("vector", "tensor_scalar", bcst[pbc][:, :], bcst[pbc][:, :],
                         0.0, None, OP.min, deps=[("v", m_sub)], inc="v")
            m_exp_jc[c][j] = P.op("scalar", "activation",
                                  m_hT[:, j * 1024:(j + 1) * 1024], bcst[pbc][:, :],
                                  AF.Exp, deps=[("v", m_min), ("t", m_yslot[j])],
                                  inc="s")
            m_sexp_slot[pbc] = m_exp_jc[c][j]
            if j >= 1:
                emit_mult(c, j - 1)
            if j == 7:
                emit_mult(c, 7)

        def ygroup(c, gi, m_ev_pair):
            nonlocal m_mh_t
            cc = t * CPT + c
            csl = slice(c * CH, (c + 1) * CH)
            exslice = expacs[:, c * NH:(c + 1) * NH]
            pA, pB = (pyA, pyB) if gi % 2 == 0 else (pp[0], pp[1])
            for hh in range(8):
                h = gi * 8 + hh
                P.op("tensor", "matmul", pA[:, hh * 64:(hh + 1) * 64],
                     m_hT[:, h * 128:(h + 1) * 128],
                     xsdtT[:, h * 64:(h + 1) * 64], start=True, stop=False,
                     deps=([("v", m_mh_jc[c][gi]), ("v", m_esnap_c[c]),
                            ("v", m_ev_pair[gi % 2]), ("v", m_fold_r[gi // 2]),
                            ("s", m_conv_all)] if hh == 0 else []))
                P.op("tensor", "matmul", pA[:, hh * 64:(hh + 1) * 64],
                     xact(h // 2)[(h % 2) * 64:(h % 2) * 64 + 64, csl],
                     g["di_s"][(h % 2) * 64:(h % 2) * 64 + 64,
                               h * 64:(h + 1) * 64], start=False, stop=True)
                m_yg = P.op("tensor", "matmul", pB[:, hh * 64:(hh + 1) * 64],
                            xact(33)[:, csl], e_bf[:, h * 64:(h + 1) * 64],
                            start=True, stop=True,
                            inc=("t" if hh == 7 else None))
            m_mh_t = m_yg
            m_yslot[gi] = m_yg
            exop = exslice[:, gi * 8:(gi + 1) * 8].unsqueeze(2) \
                .broadcast_to((128, 8, 64))
            m_e1 = P.op("vector", "tensor_tensor",
                        b3(y_tok[:, gi * 512:(gi + 1) * 512], 8, 64),
                        b3(pB[:, :], 8, 64), exop, OP.mult,
                        deps=[("t", m_yg), ("s", m_ex), ("dyp", m_ydma)], inc="v")
            m_ev_pair[gi % 2] = P.op("vector", "tensor_tensor",
                                     y_tok[:, gi * 512:(gi + 1) * 512],
                                     y_tok[:, gi * 512:(gi + 1) * 512], pA[:, :],
                                     OP.add, deps=[("v", m_e1)], inc="v")

        def stage_states(c, m_ev_pair):
            nonlocal m_ydma, m_st_sadd, m_states_t, m_xsd_t, m_chunk_v
            cc = t * CPT + c
            m_ydma = P.dma("scalar", g["yp_sp"][cc], y_tok[:, :], "dyp",
                           deps=[("v", m_ev_pair[0]), ("v", m_ev_pair[1])])
            m_emul = P.op("vector", "tensor_tensor", b3(g["e_all"][:, :], NH, PH),
                          b3(g["e_all"][:, :], NH, PH),
                          d_bc[:, c * NH:(c + 1) * NH].unsqueeze(2)
                          .broadcast_to((128, NH, PH)), OP.mult,
                          deps=[("s", me3l), ("v", m_esnap_c[c])], inc="v")
            m_st_sadd = m_emul
            for gi in range(8):
                m_sm = P.op("tensor", "matmul", pcv[:, :], bmT[:, :],
                            xsdT[:, gi * 512:(gi + 1) * 512], start=True, stop=True,
                            deps=[("s", m_bev_c[c]), ("v", m_trfree[0]),
                                  ("v", m_trfree[1]),
                                  ("v", m_st_sadd), ("v", m_emul)],
                            inc="t")
                m_st_sadd = P.op("vector", "tensor_tensor",
                                 g["e_all"][:, gi * 512:(gi + 1) * 512],
                                 g["e_all"][:, gi * 512:(gi + 1) * 512], pcv[:, :],
                                 OP.add, deps=[("t", m_sm)], inc="v")
            m_states_t = P.mark("t")
            m_xsd_t = m_states_t
            m_chunk_v = P.mark("v")

        for c in range(CPT):
            m_mh_jc[c] = [0] * 8
            m_exp_jc[c] = [0] * 8
        stage_pre(0)
        stage_folds(0)
        stage_esnap(0)
        for j in range(8):
            piece(0, j)
        for c in range(CPT):
            if c < CPT - 1:
                stage_pre(c + 1)
            m_ev_pair = [0, 0]
            for gi in range(8):
                ygroup(c, gi, m_ev_pair)
                if c < CPT - 1:
                    piece(c + 1, gi)
            stage_states(c, m_ev_pair)
            if c < CPT - 1:
                stage_folds(c + 1)
                stage_esnap(c + 1)
        m_dtfree_v = P.mark("v")

    m_z0 = P.op("vector", "memset", bcst[0][:, 0:NH], 0.0, deps=[("v", P.mark("v"))],
                inc="v")
    P.dma("gpsimd", g["ag_in"][:, IZ:IZ + NH], bcst[0][:, 0:NH], "dex",
          deps=[("v", m_z0)])
    m_exp_e = P.dma("gpsimd", g["ag_in"][:, 0:IZ], g["e_all"][:, :], "dex",
                    deps=[("v", P.mark("v"))], fence=False)
    dred = sb("b2_dred", [NH, 1], F32)
    m_red = P.op("vector", "tensor_reduce", dred[:, :], g["aend_all"][:, :], AX.X,
                 OP.add, deps=[("v", P.mark("v"))], inc="v")
    m_exp_d = P.dma("gpsimd", g["ag_in"][127:128, IZ:IZ + NH].rearrange("o h -> h o"),
                    dred[:, :], "dex", deps=[("v", m_red)])
    if _DEBUG:
        P.dma("gpsimd", g["dbg_e"], g["ag_in"], "dex", deps=[("dex", P.mark("dex"))])
    P.drain()
    P.emit()
    es.close()


def _block3(nc, g):
    P = Prog(nc, "b3")
    es = ExitStack()
    def sb(name, shape, dt):
        return es.enter_context(nc.sbuf_tensor(name, shape, dt))
    def ps(name, shape, dt):
        return es.enter_context(nc.psum_tensor(name, shape, dt))

    wbc = sb("b3_wbc", [128, NH], F32)
    daug = sb("b3_daug", [8, NH], F32)
    cmb_s = sb("b3_cmb", [8, 8], F32)
    cmbb = sb("b3_cmbb", [1, 8], F32)
    wall = sb("b3_wall", [NH, 8], F32)
    zbuf = [sb("b3_zbuf0", [128, IZ], BF16), sb("b3_zbuf1", [128, IZ], BF16)]
    ypbuf = [sb("b3_ypbuf0", [128, IZ], BF16), sb("b3_ypbuf1", [128, IZ], BF16)]
    y32 = sb("b3_y32", [128, IZ], F32)
    yfn = sb("b3_yfn", [128, IZ], BF16)
    erbf = sb("b3_erbf", [128, IZ // 2], BF16)
    expacs2 = sb("b3_expacs2", [128, NH], F32)
    dexp2 = sb("b3_dexp2", [NH, 1], F32)
    ddiag2 = sb("b3_ddiag2", [NH, NH], F32)
    d_bc2 = sb("b3_dbc2", [128, NH], F32)
    rcol = sb("b3_rcol", [128, 4], F32)
    yffm = [sb(f"b3_yffm{i}", [128, IZ], BF16) for i in range(4)]
    gb = [sb(f"b3_gb{i}", [128, H], BF16) for i in range(4)]
    xr = [sb(f"b3_xr{i}", [128, H], F32) for i in range(4)]
    wot = [sb(f"b3_wot{i}", [128, 4 * 512], BF16) for i in range(3)]
    ost = [sb("b3_ost0", [128, 512], F32), sb("b3_ost1", [128, 512], F32)]

    po = [ps(f"b3_po{i}", [128, 512], F32) for i in range(4)]
    pyc = [ps("b3_pyc0", [128, 512], F32), ps("b3_pyc1", [128, 512], F32)]
    ptr2 = [ps("b3_ptr2A", [128, 1024], BF16), ps("b3_ptr2B", [128, 1024], BF16)]

    idb = g["ident_bf_s"]
    e_run = g["e_all"]

    m_cc = P.op("gpsimd", "collective_compute", "AllGather", OP.bypass,
                replica_groups=[list(range(8))],
                ins=[g["ag_in"]], outs=[g["ag_out"]], inc="cc")
    m_da = P.dma("sync", daug[0:8, :],
                 g["ag_out"].rearrange("(j p) c -> j p c", p=128)[:, 127, IZ:IZ + NH],
                 "dcc", deps=[("cc", m_cc)])
    m_cm = P.dma("sync", cmb_s[:, :], g["cmb"][0:8, :], "dcc")
    m_cb = P.dma("sync", cmbb[:, :], g["cmb"][8:9, :], "dcc")
    P.op("tensor", "matmul", pyc[0][0:NH, 0:8], daug[:, :], cmb_s[:, :],
         start=True, stop=False, deps=[("dcc", m_cb)])
    mt = P.op("tensor", "matmul", pyc[0][0:NH, 0:8], g["ones_f"][0:1, 0:NH],
              cmbb[:, :], start=False, stop=True, deps=[("dcc", m_cb)], inc="t")
    me = P.op("scalar", "copy", wall[:, :], pyc[0][0:NH, 0:8], deps=[("t", mt)], inc="s")
    me = P.op("scalar", "activation", wall[:, :], wall[:, :], AF.Exp,
              deps=[("s", me)], inc="s")
    mt = P.op("tensor", "transpose", pyc[1][0:8, 0:NH].bitcast(F32), wall[:, :],
              g["ident_f_s"][0:NH, 0:NH], deps=[("s", me)], inc="t")
    me = P.op("scalar", "copy", daug[0:8, :], pyc[1][0:8, 0:NH], deps=[("t", mt)], inc="s")
    m_wr = P.dma("gpsimd", g["w_row"], daug[0:8, :], "dwr", deps=[("s", me)])
    m_prev = 0
    for j in range(8):
        m_sj = P.dma("sync", y32[:, :].bitcast(F32),
                     g["ag_out"][j * 128:(j + 1) * 128, 0:IZ], "dsj",
                     deps=[("cc", m_cc), ("v", m_prev)])
        m_wb = P.dma("gpsimd", wbc[:, :],
                     g["w_row"][j].unsqueeze(0).partition_broadcast(128), "dwb",
                     deps=[("dwr", m_wr), ("v", m_prev)])
        m_t = P.op("vector", "tensor_tensor", b3(y32[:, :], NH, PH),
                   b3(y32[:, :], NH, PH),
                   wbc[:, :].unsqueeze(2).broadcast_to((128, NH, PH)), OP.mult,
                   deps=[("dsj", m_sj), ("dwb", m_wb)], inc="v")
        if j == 0:
            m_prev = P.op("vector", "tensor_copy", e_run[:, :], y32[:, :],
                          deps=[("v", m_t)], inc="v")
        else:
            m_prev = P.op("vector", "tensor_tensor", e_run[:, :], e_run[:, :],
                          y32[:, :], OP.add, deps=[("v", m_t)], inc="v")
    m_er = m_prev

    m_out = 0
    m_po_free = [0, 0, 0, 0]
    m_tr2free = [0, 0]
    m_yffm_t = [0, 0, 0, 0]
    m_gxfree = [(0, 0)] * 4
    m_wotu = [0, 0, 0]
    m_er_t = 0
    m_y32_free = 0
    m_pyc_v = [0, 0]
    m_pyc_s0 = 0
    m_dd2_t = 0
    m_dd2_s = 0
    m_eg_t = 0

    m_yf_prev = 0
    m_yv_prev = 0
    m_yf_pb = [0, 0]
    m_yv_pb = [0, 0]
    m_trlast = 0
    m_gd = [0] * 4
    m_xd = [0] * 4
    m_ost_d = [0, 0]
    m_yffm_vs = [0] * 4

    def prep_gen(cc):
        nonlocal m_dd2_s, m_dd2_t, m_er, m_pyc_s0, m_er_t, m_y32_free
        nonlocal m_trlast
        sl = cc % 4
        zpb = cc % 2
        m_z = P.dma("sync", zbuf[zpb][:, :], g["z_sp"][cc],
                    "dz0" if zpb == 0 else "dz1", deps=[("v", m_yf_pb[zpb])])
        m_yp = P.dma("sync", ypbuf[zpb][:, :], g["yp_sp"][cc],
                     "dyp0" if zpb == 0 else "dyp1", deps=[("v", m_yv_pb[zpb])])
        m_gd[sl] = P.dma("sync", gb[sl][:, :], g["g_sp"][cc], "dgb",
                         deps=[("v", m_gxfree[sl][0])])
        m_xd[sl] = P.dma("sync", xr[sl][:, :],
                         g["x_span"][cc * 128:(cc + 1) * 128, :], "dxr",
                         deps=[("v", m_gxfree[sl][1])])
        yield
        if cc > 0:
            m_dre = P.op("scalar", "activation", dexp2[:, :],
                         g["aend_all"][:, cc - 1:cc], AF.Exp,
                         deps=[("s", m_dd2_s)], inc="s")
            yield
            m_dd = P.op("vector", "tensor_scalar", ddiag2[:, :],
                        g["ident_f_s"][0:NH, 0:NH], dexp2[:, 0:1], None, OP.mult,
                        deps=[("s", m_dre), ("t", m_dd2_t)], inc="v")
            yield
            mt = P.op("tensor", "matmul", pyc[0][:, 0:NH], g["ones64"][:, :],
                      ddiag2[:, :], start=True, stop=True,
                      deps=[("v", m_dd), ("v", m_pyc_v[0]), ("v", m_er),
                            ("s", m_pyc_s0)], inc="t")
            m_dd2_t = mt
            me = P.op("scalar", "copy", d_bc2[:, :], pyc[0][:, 0:NH],
                      deps=[("t", mt)], inc="s")
            m_dd2_s = me
            yield
            m_er = P.op("vector", "tensor_tensor", b3(e_run[:, :], NH, PH),
                        b3(e_run[:, :], NH, PH),
                        d_bc2[:, :].unsqueeze(2).broadcast_to((128, NH, PH)),
                        OP.mult, deps=[("s", me), ("t", m_er_t)], inc="v")
            m_pyc_s0 = me
            yield
        m_ex2 = P.op("scalar", "activation", expacs2[:, :],
                     g["acst_all"][:, cc * NH:(cc + 1) * NH], AF.Exp,
                     deps=[("s", m_y32_free)], inc="s")
        yield
        m_yv = 0
        m_eg = [0, 0]
        for gi in range(8):
            pb = gi % 2
            if gi % 4 == 0:
                m_eg_l = P.op("vector", "tensor_copy", erbf[:, :],
                              e_run[:, (gi // 4) * 2048:(gi // 4 + 1) * 2048],
                              deps=[("v", m_er), ("t", m_er_t),
                                    ("t", m_eg_t[0])], inc="v")
                yield
            for hh in range(8):
                h = gi * 8 + hh
                m_yg = P.op("tensor", "matmul", pyc[pb][:, hh * 64:(hh + 1) * 64],
                            g["cm_all"][:, cc * 128:(cc + 1) * 128],
                            erbf[:, (h * 64) % 2048:(h * 64) % 2048 + 64],
                            start=True, stop=True,
                            deps=([("v", m_eg_l), ("v", m_pyc_v[pb]),
                                   ("s", m_pyc_s0 if pb == 0 else 0)]
                                  if hh == 0 else []),
                            inc=("t" if hh == 7 else None))
            if gi % 4 == 3:
                m_eg_t[0] = m_yg
            m_er_t = m_yg
            exop = expacs2[:, gi * 8:(gi + 1) * 8].unsqueeze(2) \
                .broadcast_to((128, 8, 64))
            m_y1 = P.op("vector", "tensor_tensor",
                        b3(y32[:, gi * 512:(gi + 1) * 512], 8, 64),
                        b3(pyc[pb][:, :], 8, 64), exop, OP.mult,
                        deps=[("t", m_yg), ("s", m_ex2),
                              ("s", m_y32_free if gi == 0 else 0)], inc="v")
            m_yv = P.op("vector", "tensor_tensor", y32[:, gi * 512:(gi + 1) * 512],
                        y32[:, gi * 512:(gi + 1) * 512],
                        ypbuf[zpb][:, gi * 512:(gi + 1) * 512], OP.add,
                        deps=[("v", m_y1),
                              ("dyp0" if zpb == 0 else "dyp1", m_yp)], inc="v")
            m_pyc_v[pb] = m_yv
            yield
        m_yf = P.op("vector", "tensor_tensor", y32[:, :], y32[:, :],
                    zbuf[zpb][:, :], OP.mult,
                    deps=[("v", m_yv), ("dz0" if zpb == 0 else "dz1", m_z)],
                    inc="v")
        m_yf_pb[zpb] = m_yf
        m_yv_pb[zpb] = m_yv
        yield
        m_ss = P.op("scalar", "activation", yfn[:, :], y32[:, :], AF.Square,
                    accum_out=rcol[:, 0:1],
                    deps=[("v", m_yf), ("t", m_trlast)], inc="s")
        m_r1 = P.op("scalar", "activation", rcol[:, 1:2], rcol[:, 0:1], AF.Ln,
                    scale=1.0 / IZ, bias=g["epscol"][:, 0:1],
                    deps=[("s", m_ss)], inc="s")
        m_r2 = P.op("scalar", "activation", rcol[:, 2:3], rcol[:, 1:2], AF.Exp,
                    scale=-0.5, deps=[("s", m_r1)], inc="s")
        m_yn = P.op("scalar", "activation", yfn[:, :], y32[:, :], AF.Identity,
                    scale=rcol[:, 2:3], deps=[("s", m_r2)], inc="s")
        m_y32_free = m_yn
        yield
        for rnd in range(4):
            pt = ptr2[rnd % 2]
            for i in range(8):
                m_tr = P.op("tensor", "transpose",
                            pt[:, i * 128:(i + 1) * 128].bitcast(BF16),
                            yfn[:, (rnd * 8 + i) * 128:(rnd * 8 + i + 1) * 128],
                            idb[:, :],
                            deps=([("s", m_yn), ("v", m_tr2free[rnd % 2]),
                                   ("t", m_yffm_t[sl])] if i == 0 else []),
                            inc=("t" if i == 7 else None))
            m_trlast = m_tr
            m_tr2free[rnd % 2] = P.op(
                "vector", "tensor_copy", yffm[sl][:, rnd * 1024:(rnd + 1) * 1024],
                pt[:, :], deps=[("t", m_tr)], inc="v")
            yield
        m_yffm_vs[sl] = m_tr2free[(4 - 1) % 2]

    def drain_gens(gens, n):
        cnt = 0
        while gens and cnt < n:
            try:
                next(gens[0])
                cnt += 1
            except StopIteration:
                gens.pop(0)

    m_eg_t = [0]
    gens = [prep_gen(0), prep_gen(1)]
    drain_gens(gens, 10000)
    for gg in range(16):
        s0 = (gg * 2) % 4
        p0_, p1_ = (0, 1) if gg % 2 == 0 else (2, 3)
        if gg < 15:
            gens = [prep_gen(gg * 2 + 2), prep_gen(gg * 2 + 3)]
        else:
            gens = []
        for cg in range(4):
            for kq in range(IZ // 512):
                bi = kq % 3
                dwo = f"dwo{bi}"
                m_wd = P.dma("sync", wot[bi][:, :].rearrange("p (k c) -> p k c", k=4),
                             g["wo_t"][cg, 4 * kq:4 * kq + 4].rearrange("k p c -> p k c"),
                             dwo, deps=[("t", m_wotu[bi])])
                for ksub in range(4):
                    k = kq * 4 + ksub
                    for cl in range(2):
                        sl = s0 + cl
                        pidx = (p0_, p1_)[cl]
                        dep = []
                        if ksub == 0 and cl == 0:
                            dep.append((dwo, m_wd))
                        if kq == 0 and ksub == 0:
                            dep += [("v", m_yffm_vs[sl]), ("v", m_po_free[pidx])]
                        m_lw = P.op("tensor", "matmul", po[pidx][:, :],
                                    yffm[sl][:, k * 128:(k + 1) * 128],
                                    wot[bi][:, ksub * 512:(ksub + 1) * 512],
                                    start=(k == 0), stop=(k == IZ // 128 - 1),
                                    deps=dep,
                                    inc=("t" if cl == 1 and ksub == 3 else None))
                m_wotu[bi] = m_lw
                drain_gens(gens, 2)
            m_mm = P.mark("t")
            drain_gens(gens, 4)
            for cl in range(2):
                sl = s0 + cl
                cc = gg * 2 + cl
                pidx = (p0_, p1_)[cl]
                ob = cl
                dos = "dout0" if ob == 0 else "dout1"
                m_t = P.op("vector", "tensor_tensor", ost[ob][:, :], po[pidx][:, :],
                           gb[sl][:, cg * 512:(cg + 1) * 512], OP.mult,
                           deps=[("t", m_mm), ("dgb", m_gd[sl]), ("dxr", m_xd[sl]),
                                 (dos, m_ost_d[ob])], inc="v")
                m_o = P.op("vector", "tensor_tensor", ost[ob][:, :], ost[ob][:, :],
                           xr[sl][:, cg * 512:(cg + 1) * 512], OP.add,
                           deps=[("v", m_t)], inc="v")
                m_po_free[pidx] = m_o
                m_ost_d[ob] = P.dma("scalar", g["out_span"][cc * 128:(cc + 1) * 128,
                                                           cg * 512:(cg + 1) * 512],
                                    ost[ob][:, :], dos, deps=[("v", m_o)])
        drain_gens(gens, 10000)
        for cl in range(2):
            sl = s0 + cl
            m_gxfree[sl] = (m_po_free[(p0_, p1_)[cl]], m_po_free[(p0_, p1_)[cl]])
            m_yffm_t[sl] = P.mark("t")

    P.drain()
    P.emit()
    es.close()


_NC_CACHE = {}


def kernel(L0_attn_hidden_states, L0_attn_W_in, L0_attn_conv_w, L0_attn_conv_b,
           L0_attn_dt_bias, L0_attn_A_log, L0_attn_D, L0_attn_norm_w, L0_attn_W_out,
           L0_ffn_weight, L0_ffn_scale_weight, L0_ffn_bias):
    x = np.ascontiguousarray(np.asarray(L0_attn_hidden_states, dtype=np.float32))
    ident_bf = np.eye(128, dtype=np.float32).astype(ml_dtypes.bfloat16)
    ident_f = np.eye(128, dtype=np.float32)
    mask = np.triu(np.ones((128, 128), dtype=np.float32))

    if "nc" not in _NC_CACHE:
        _NC_CACHE["nc"] = build_nc()
    nc = _NC_CACHE["nc"]

    def cmb_for(core):
        grp, sp = core // 4, core % 4
        m = np.zeros((9, 8), dtype=np.float32)
        for j in range(8):
            gj, sj = j // 4, j % 4
            if gj == grp and sj < sp:
                m[8, j] = 0.0
                for i in range(8):
                    gi, si = i // 4, i % 4
                    if gi == grp and sj < si < sp:
                        m[i, j] = 1.0
            else:
                m[8, j] = -BIGNEG
        return m

    in_maps = []
    for core in range(8):
        b_, sp = core // 4, core % 4
        s0 = sp * SPAN
        halo = (x[b_, s0 - 128:s0, :] if sp > 0
                else np.zeros((128, H), dtype=np.float32))
        in_maps.append({
            "x_span": np.ascontiguousarray(x[b_, s0:s0 + SPAN, :]),
            "x_halo": np.ascontiguousarray(halo),
            "w_in": L0_attn_W_in, "conv_w": L0_attn_conv_w, "conv_b": L0_attn_conv_b,
            "dt_bias": L0_attn_dt_bias, "a_log": L0_attn_A_log, "d_in": L0_attn_D,
            "norm_w": L0_attn_norm_w, "w_out": L0_attn_W_out,
            "w_ffn": L0_ffn_weight, "ffn_scale": L0_ffn_scale_weight,
            "ffn_bias": L0_ffn_bias,
            "ident_bf": ident_bf, "ident_f": ident_f, "mask_tril": mask,
            "cmb": cmb_for(core),
        })
    for m in in_maps:
        for k in list(m):
            m[k] = np.ascontiguousarray(np.asarray(m[k]))

    res = run_bass_kernel_spmd(nc, in_maps, core_ids=list(range(8)))
    out = np.empty((B, L, H), dtype=np.float32)
    for core in range(8):
        b_, sp = core // 4, core % 4
        out[b_, sp * SPAN:(sp + 1) * SPAN, :] = res.results[core]["out_span"]
    return out


# version 13 (final)
# speedup vs baseline: 1.2066x; 1.0142x over previous
import os
import numpy as np
import ml_dtypes
from contextlib import ExitStack

_NBLOCKS = int(os.environ.get("KBLOCKS", "3"))

import concourse.bass as bass
import concourse.mybir as mybir
from concourse.bass_utils import run_bass_kernel_spmd
from concourse.alu_op_type import AluOpType as OP

F32 = mybir.dt.float32
BF16 = mybir.dt.bfloat16
AF = mybir.ActivationFunctionType
AX = mybir.AxisListType

B, L, H = 2, 16384, 2048
IZ = 4096
NS = 128
NH, PH = 64, 64
KC = 4
XC = IZ + 2 * NS
XT = XC // 128
PROJ = 2 * (IZ + NS) + NH
SPAN = L // 4
T = 512
NTL = SPAN // T
CH = 128
NCH = SPAN // CH
CPT = T // CH
HK = H // 128
EPS = 1e-5
BIGNEG = 1e30

_DEBUG = False


class Prog:
    ENGS = ("tensor", "vector", "scalar", "gpsimd", "sync")

    def __init__(self, nc, tag):
        self.nc, self.tag = nc, tag
        self.q = {e: [] for e in self.ENGS}
        self.sems, self.cnt = {}, {}
        self.wm = {e: {} for e in self.ENGS}

    def _sem(self, name):
        key = f"{self.tag}_{name}"
        if key not in self.sems:
            self.sems[key] = self.nc.alloc_semaphore(key)
            self.cnt[key] = 0
        return key

    def op(self, eng, fname, *args, deps=(), inc=None, incamt=1, **kw):
        dl = []
        for (s, v) in deps:
            if v and v > 0:
                dl.append((self._sem(s), v))
        if inc is not None:
            inc = self._sem(inc)
            self.cnt[inc] += incamt
        self.q[eng].append((dl, fname, args, kw, inc, incamt))
        return self.cnt[inc] if inc else None

    def dma(self, eng, out, in_, inc, deps=(), fence=True, **kw):
        deps = list(deps)
        if fence and inc is not None:
            deps.append((inc, self.cnt.get(f"{self.tag}_{inc}", 0)))
        return self.op(eng, "dma_start", deps=deps, inc=inc, incamt=16,
                       out=out, in_=in_, **kw)

    def mark(self, name):
        return self.cnt.get(f"{self.tag}_{name}", 0)

    def drain(self):
        deps = [(k[len(self.tag) + 1:], v) for k, v in self.cnt.items()]
        self.op("sync", "nop", deps=deps)

    def emit(self):
        with self.nc.Block(self.tag) as block:
            for eng in self.ENGS:
                items = self.q[eng]
                if not items:
                    continue

                def body(E, items=items, eng=eng):
                    wm = self.wm[eng]
                    for dl, fname, args, kw, inc, incamt in items:
                        for (s, v) in dl:
                            if v > wm.get(s, 0):
                                E.wait_ge(self.sems[s], v)
                                wm[s] = v
                        inst = getattr(E, fname)(*args, **kw)
                        if inc:
                            inst.then_inc(self.sems[inc], incamt)
                getattr(block, eng)(body)


def b3(ap, a, b_):
    return ap.rearrange("p (a b) -> p a b", a=a)


def build_nc():
    nc = bass.Bass("TRN2", target_bir_lowering=False)
    g = {}

    def din(name, shape, dt=F32):
        g[name] = nc.dram_tensor(name, shape, dt, kind="ExternalInput").ap()
    din("x_span", [SPAN, H]); din("x_halo", [128, H])
    din("w_in", [PROJ, H]); din("conv_w", [XC, 1, KC]); din("conv_b", [XC])
    din("dt_bias", [NH]); din("a_log", [NH]); din("d_in", [NH]); din("norm_w", [IZ])
    din("w_out", [H, IZ]); din("w_ffn", [H, H]); din("ffn_scale", [H]); din("ffn_bias", [H])
    din("ident_bf", [128, 128], BF16); din("ident_f", [128, 128])
    din("mask_tril", [128, 128]); din("cmb", [9, 8])
    g["out_span"] = nc.dram_tensor("out_span", [SPAN, H], F32, kind="ExternalOutput").ap()

    sk = "ExternalOutput" if _DEBUG else "Internal"
    g["w_in_bf"] = nc.dram_tensor("w_in_bf", [PROJ, H], BF16).ap()
    g["w_out_bf"] = nc.dram_tensor("w_out_bf", [H, IZ], BF16).ap()
    g["w_ffn_bf"] = nc.dram_tensor("w_ffn_bf", [H, H], BF16).ap()
    g["xn_sp"] = nc.dram_tensor("xn_sp", [HK, 128, SPAN], BF16, kind=sk).ap()
    g["z_sp"] = nc.dram_tensor("z_sp", [NCH, 128, IZ], BF16, kind=sk).ap()
    g["g_sp"] = nc.dram_tensor("g_sp", [NCH, 128, H], BF16, kind=sk).ap()
    g["yp_sp"] = nc.dram_tensor("yp_sp", [NCH, 128, IZ], BF16, kind=sk).ap()
    g["acs_row"] = nc.dram_tensor("acs_row", [NCH, CH * NH], F32).ap()
    g["wx_t"] = nc.dram_tensor("wx_t", [9, HK, 128, 512], BF16).ap()
    g["wo_t"] = nc.dram_tensor("wo_t", [4, IZ // 128, 128, 512], BF16).ap()
    g["wz_t"] = nc.dram_tensor("wz_t", [8, HK, 128, 512], BF16).ap()
    g["wf_t"] = nc.dram_tensor("wf_t", [4, HK, 128, 512], BF16).ap()
    g["w_row"] = nc.dram_tensor("w_row", [8, NH], F32).ap()
    g["nwrow_d"] = nc.dram_tensor("nwrow_d", [IZ], F32).ap()
    g["ag_in"] = nc.dram_tensor("ag_in", [128, IZ + 64], F32).ap()
    g["ag_out"] = nc.dram_tensor("ag_out", [8 * 128, IZ + 64], F32).ap()
    if _DEBUG:
        g["dbg_e"] = nc.dram_tensor("dbg_e", [128, IZ + 64], F32, kind="ExternalOutput").ap()

    sb = nc.alloc_sbuf_tensor
    g["ident_bf_s"] = sb("ident_bf_s", [128, 128], BF16)
    g["ident_f_s"] = sb("ident_f_s", [128, 128], F32)
    g["mask_s"] = sb("mask_s", [128, 128], F32)
    g["ones_bf"] = sb("ones_bf", [1, 128], BF16)
    g["ones_f"] = sb("ones_f", [1, 128], F32)
    g["ones64"] = sb("ones64", [NH, 128], F32)
    g["epscol"] = sb("epscol", [128, 1], F32)
    g["onecol"] = sb("onecol", [128, 1], F32)
    g["dtb_s"] = sb("dtb_s", [NH, 1], F32)
    g["acol_s"] = sb("acol_s", [NH, 1], F32)
    g["cb_s"] = sb("cb_s", [128, XT], F32)
    g["cw_s"] = sb("cw_s", [128, KC * XT], F32)
    g["di_s"] = sb("di_s", [128, NH * 64], BF16)
    g["halo_fm"] = sb("halo_fm", [128, H], BF16)
    g["e_all"] = sb("e_all", [128, NH * PH], F32)
    g["cm_all"] = sb("cm_all", [128, 128 * NCH], BF16)
    g["acst_all"] = sb("acst_all", [128, NH * NCH], F32)
    g["aend_all"] = sb("aend_all", [NH, NCH], F32)

    _block1(nc, g)
    if _NBLOCKS >= 2:
        _block2(nc, g)
    if _NBLOCKS >= 3:
        _block3(nc, g)
    return nc


def _block1(nc, g):
    P = Prog(nc, "b1")
    es = ExitStack()
    QTR = SPAN // 4
    def sb(name, shape, dt):
        return es.enter_context(nc.sbuf_tensor(name, shape, dt))
    def ps(name, shape, dt):
        return es.enter_context(nc.psum_tensor(name, shape, dt))
    HALF = SPAN // 2

    xs = [sb("b1_xs0", [128, H], F32), sb("b1_xs1", [128, H], F32)]
    stat = sb("b1_stat", [128, 16], F32)
    xnt = [sb("b1_xnt0", [128, H], BF16), sb("b1_xnt1", [128, H], BF16)]
    xnfm = sb("b1_xnfm", [128, HK * QTR], BF16)
    wsl = [sb(f"b1_wsl{i}", [128, HK * 512], BF16) for i in range(3)]
    zst = [sb("b1_zst0", [128, 512], BF16), sb("b1_zst1", [128, 512], BF16)]
    wof = [sb("b1_wof0", [128, IZ], BF16)]
    wff = [sb("b1_wff0", [128, H], BF16), sb("b1_wff1", [128, H], BF16)]
    nwbc = sb("b1_nwbc", [128, IZ], BF16)
    fsc_s = sb("b1_fsc", [128, HK], F32)
    fbias_row = sb("b1_fbr", [1, H], BF16)
    tmp34 = sb("b1_tmp34", [XT, 512], F32)
    tmpk = sb("b1_tmpk", [XT, 128], F32)
    nw_t = sb("b1_nw_t", [IZ // 128, 128], F32)
    fs_t = sb("b1_fs_t", [HK, 128], F32)
    cwx = sb("b1_cwx", [XT, 128], F32)
    nwrow = sb("b1_nwrow", [1, 512], BF16)
    adcol = sb("b1_adcol", [NH, 2], F32)
    dbc = sb("b1_dbc", [128, NH], BF16)
    drow_f = sb("b1_drow", [1, NH], F32)

    ptA = ps("b1_ptA", [128, 2048], BF16)
    ptB = ps("b1_ptB", [128, 2048], BF16)
    pz = [ps("b1_pz0", [128, 512], F32), ps("b1_pz1", [128, 512], F32)]
    pm = ps("b1_pm", [128, 512], F32)

    n_c = 0
    def cdma(dst, src):
        nonlocal n_c
        n_c = P.dma("sync", dst, src, "dc")
        return n_c
    cdma(g["ident_bf_s"][:, :], g["ident_bf"])
    cdma(g["ident_f_s"][:, :], g["ident_f"])
    cdma(g["mask_s"][:, :], g["mask_tril"])
    cdma(g["dtb_s"][:, :], g["dt_bias"].unsqueeze(1))
    cdma(adcol[:, 0:1], g["a_log"].unsqueeze(1))
    cdma(adcol[:, 1:2], g["d_in"].unsqueeze(1))
    cdma(tmpk[:, :], g["conv_b"].rearrange("(t p) -> t p", p=128))
    cdma(tmp34[:, :], g["conv_w"].rearrange("(t p) o k -> t (p o k)", p=128))
    cdma(nw_t[:, :], g["norm_w"].rearrange("(t p) -> t p", p=128))
    cdma(fs_t[:, :], g["ffn_scale"].rearrange("(t p) -> t p", p=128))
    m_call = n_c

    m_ms = P.op("vector", "memset", g["ones_f"][:, :], 1.0, inc="v")
    P.op("vector", "memset", g["ones64"][:, :], 1.0, inc="v")
    P.op("vector", "memset", g["epscol"][:, :], EPS, inc="v")
    P.op("vector", "memset", g["onecol"][:, :], 1.0, inc="v")
    m_ones = P.op("vector", "memset", g["ones_bf"][:, :], 1.0, inc="v")

    m_cast_half = 0
    for i in range(4):
        r = PROJ // 4
        m_cast_win = P.dma("gpsimd", g["w_in_bf"][i * r:(i + 1) * r, :],
                           g["w_in"][i * r:(i + 1) * r, :], "dcw", fence=(i == 2))
        if i == 1:
            m_cast_half = m_cast_win

    mt = me = 0
    def small_tr(dst, src, idn, w):
        nonlocal mt, me
        mt = P.op("tensor", "transpose", pm[0:w, 0:src.shape[0]].bitcast(F32), src,
                  idn, deps=[("dc", m_call), ("s", me), ("v", m_ms)], inc="t")
        me = P.op("scalar", "copy", dst, pm[0:w, 0:src.shape[0]], deps=[("t", mt)], inc="s")
        return me
    small_tr(g["cb_s"][:, :], tmpk[:, :], g["ident_f_s"][0:XT, 0:XT], 128)
    for k in range(KC):
        m_cx = P.op("vector", "tensor_copy", cwx[:, :],
                    tmp34[:, :].rearrange("t (p k) -> t k p", k=KC)[:, k],
                    deps=[("dc", m_call), ("s", me)], inc="v")
        mt = P.op("tensor", "transpose", pm[0:128, 0:XT].bitcast(F32), cwx[:, :],
                  g["ident_f_s"][0:XT, 0:XT], deps=[("v", m_cx), ("s", me)], inc="t")
        me = P.op("scalar", "copy", g["cw_s"][:, k * XT:(k + 1) * XT], pm[0:128, 0:XT],
                  deps=[("t", mt)], inc="s")
    nw_s = sb("b1_nw_s", [128, IZ // 128], F32)
    small_tr(nw_s[:, :], nw_t[:, :], g["ident_f_s"][0:32, 0:32], 128)
    small_tr(fsc_s[:, :], fs_t[:, :], g["ident_f_s"][0:HK, 0:HK], 128)
    m_a = P.op("scalar", "activation", g["acol_s"][:, :], adcol[:, 0:1], AF.Exp,
               deps=[("dc", m_call), ("s", me)], inc="s")
    m_a = P.op("vector", "tensor_scalar", g["acol_s"][:, :], g["acol_s"][:, :], -1.0,
               None, OP.mult, deps=[("s", m_a)], inc="v")
    mt = P.op("tensor", "transpose", pm[0:1, 0:NH].bitcast(F32), adcol[:, 1:2],
              g["ident_f_s"][0:NH, 0:NH], deps=[("s", me), ("dc", m_call)], inc="t")
    me = P.op("scalar", "copy", drow_f[:, :], pm[0:1, 0:NH], deps=[("t", mt)], inc="s")
    mt = P.op("tensor", "matmul", pm[0:128, 64:128], g["ones_f"][0:1, :], drow_f[:, :],
              start=True, stop=True, deps=[("s", me), ("v", m_ms)], inc="t")
    me = P.op("scalar", "copy", dbc[:, :], pm[0:128, 64:128], deps=[("t", mt)], inc="s")
    for h in range(NH):
        P.op("vector", "tensor_tensor", g["di_s"][0:64, h * 64:(h + 1) * 64],
             g["ident_bf_s"][0:64, 0:64],
             dbc[0:64, h:h + 1].broadcast_to((64, 64)), OP.mult,
             deps=[("s", me)], inc="v")
        P.op("vector", "tensor_tensor", g["di_s"][64:128, h * 64:(h + 1) * 64],
             g["ident_bf_s"][64:128, 64:128],
             dbc[64:128, h:h + 1].broadcast_to((64, 64)), OP.mult,
             deps=[("s", me)], inc="v")
    m_di = P.mark("v")

    m_nf = P.dma("gpsimd", g["nwrow_d"].rearrange("(t p) -> t p", p=128), nw_t[:, :],
                 "dnw", deps=[("dc", m_call)])
    for j in range(IZ // 512):
        m_nr = P.dma("gpsimd", nwrow[:, :],
                     g["nwrow_d"][j * 512:(j + 1) * 512].unsqueeze(0), "dnw",
                     deps=[("t", mt)])
        mt = P.op("tensor", "matmul", pm[:, :], g["ones_bf"][0:1, :],
                  nwrow[:, :], start=True, stop=True,
                  deps=[("dnw", m_nr), ("s", me), ("v", m_ones)], inc="t")
        me = P.op("scalar", "copy", nwbc[:, j * 512:(j + 1) * 512], pm[:, :],
                  deps=[("t", mt)], inc="s")
    m_nwbc = me
    m_fbr_d2 = P.dma("gpsimd", fbias_row[:, :], g["ffn_bias"].unsqueeze(0), "dfb")

    m_use = [0]
    for i in range(HK):
        bi = 0
        m_d = P.dma("gpsimd", wof[bi][:, :], g["w_out"][i * 128:(i + 1) * 128, :], "dwo",
                    deps=[("dso", m_use[bi])])
        m_f = P.op("vector", "tensor_tensor", wof[bi][:, :], wof[bi][:, :], nwbc[:, :],
                   OP.mult, deps=[("dwo", m_d), ("s", m_nwbc)], inc="v")
        m_use[bi] = P.dma("scalar", g["w_out_bf"][i * 128:(i + 1) * 128, :], wof[bi][:, :],
                          "dso", deps=[("v", m_f)])
    m_use = [0, 0]
    for i in range(HK):
        bi = i % 2
        m_d = P.dma("gpsimd", wff[bi][:, :], g["w_ffn"][i * 128:(i + 1) * 128, :],
                    "dwf0" if bi == 0 else "dwf1",
                    deps=[("dsf", m_use[bi])])
        m_f = P.op("vector", "tensor_scalar", wff[bi][:, :], wff[bi][:, :],
                   fsc_s[:, i:i + 1], None, OP.mult,
                   deps=[("dwf0" if bi == 0 else "dwf1", m_d), ("s", me)], inc="v")
        m_use[bi] = P.dma("scalar", g["w_ffn_bf"][i * 128:(i + 1) * 128, :], wff[bi][:, :],
                          "dsf", deps=[("v", m_f)])
    m_wffn_sp = P.mark("dsf")

    bt = [sb(f"b1_bt{i}", [128, 2048], BF16) for i in range(4)]
    m_btw = [0, 0, 0, 0]
    bi_c = 0

    def bsems(bi):
        return f"dbt{bi}", f"dbw{bi}"

    def build4(src_name, r0, kcol, dst_list, dep_extra):
        nonlocal bi_c
        bi = bi_c % 4
        bi_c += 1
        dbs, dbw = bsems(bi)
        m_l = P.dma("sync", bt[bi][:, :],
                    g[src_name][r0:r0 + 2048, kcol * 128:(kcol + 1) * 128],
                    dbs, deps=dep_extra + [(dbw, m_btw[bi])], transpose=True)
        m_w = 0
        for i, dst in enumerate(dst_list):
            m_w = P.dma("gpsimd", dst, bt[bi][:, i * 512:(i + 1) * 512], dbw,
                        deps=[(dbs, m_l)], fence=(i == 0))
        m_btw[bi] = m_w

    for k in range(HK):
        build4("w_in_bf", 0, k, [g["wz_t"][og, k] for og in range(4)],
               [("dcw", m_cast_half)])
        build4("w_in_bf", 2048, k, [g["wz_t"][og, k] for og in range(4, 8)],
               [("dcw", m_cast_half)])
    m_wz_built = [P.mark(f"dbw{i}") for i in range(4)]
    m_wf_sp = P.mark("dsf")
    for k in range(HK):
        build4("w_ffn_bf", 0, k, [g["wf_t"][og, k] for og in range(4)],
               [("dsf", m_wf_sp)])
    m_wf_built = [P.mark(f"dbw{i}") for i in range(4)]
    for k in range(HK):
        build4("w_in_bf", IZ, k, [g["wx_t"][og, k] for og in range(4)],
               [("dcw", m_cast_win)])
        build4("w_in_bf", IZ + 2048, k, [g["wx_t"][og, k] for og in range(4, 8)],
               [("dcw", m_cast_win)])
    for k in range(HK):
        bi = bi_c % 4
        bi_c += 1
        dbs, dbw = bsems(bi)
        P.dma("sync", bt[bi][:, 0:256],
              g["w_in_bf"][2 * IZ:2 * IZ + 256, k * 128:(k + 1) * 128],
              dbs, deps=[("dcw", m_cast_win), (dbw, m_btw[bi])], transpose=True)
        m_l = P.dma("sync", bt[bi][:, 256:384],
                    g["w_in_bf"][PROJ - 128:PROJ, k * 128:(k + 1) * 128],
                    dbs, transpose=True, fence=False)
        m_btw[bi] = P.dma("gpsimd", g["wx_t"][8, k].rearrange("p c -> p c")[:, 0:384],
                          bt[bi][:, 0:384], dbw, deps=[(dbs, m_l)])
    m_wo_sp = P.mark("dso")
    for k in range(IZ // 128):
        build4("w_out_bf", 0, k, [g["wo_t"][cg, k] for cg in range(4)],
               [("dso", m_wo_sp)])

    m_xn_s = [0, 0]
    m_ev_v = [0, 0]
    m_trm = [0, 0]

    def ln_block(bi, src, dst_fm, extra_ev_deps=()):
        dxs = "dx0" if bi == 0 else "dx1"
        m_x = P.dma("scalar", xs[bi][:, :], src, dxs, deps=[("s", m_xn_s[bi])])
        o = bi * 8
        m_sum = P.op("vector", "tensor_reduce", stat[:, o:o + 1], xs[bi][:, :], AX.X,
                     OP.add, deps=[(dxs, m_x), ("s", m_xn_s[bi])], inc="v")
        m_sq = P.op("scalar", "activation", xnt[bi][:, :], xs[bi][:, :], AF.Square,
                    accum_out=stat[:, o + 1:o + 2],
                    deps=[(dxs, m_x), ("v", m_sum), ("t", m_trm[bi])], inc="s")
        m_mean = P.op("scalar", "mul", stat[:, o + 2:o + 3], stat[:, o:o + 1], 1.0 / H,
                      deps=[("s", m_sq)], inc="s")
        m_v1 = P.op("vector", "scalar_tensor_tensor", stat[:, o + 3:o + 4],
                    stat[:, o + 2:o + 3], 1.0, stat[:, o + 2:o + 3], OP.bypass, OP.mult,
                    deps=[("s", m_mean)], inc="v")
        m_v2 = P.op("vector", "scalar_tensor_tensor", stat[:, o + 4:o + 5],
                    stat[:, o + 1:o + 2], 1.0 / H, stat[:, o + 3:o + 4], OP.mult,
                    OP.subtract, deps=[("v", m_v1)], inc="v")
        m_ln = P.op("scalar", "activation", stat[:, o + 5:o + 6], stat[:, o + 4:o + 5],
                    AF.Ln, bias=g["epscol"][:, 0:1], deps=[("v", m_v2)], inc="s")
        m_rs = P.op("scalar", "activation", stat[:, o + 6:o + 7], stat[:, o + 5:o + 6],
                    AF.Exp, scale=-0.5, deps=[("s", m_ln)], inc="s")
        m_nm = P.op("vector", "scalar_tensor_tensor", stat[:, o + 7:o + 8],
                    stat[:, o + 2:o + 3], -1.0, stat[:, o + 6:o + 7], OP.mult, OP.mult,
                    deps=[("s", m_rs)], inc="v")
        m_xn = P.op("scalar", "activation", xnt[bi][:, :], xs[bi][:, :], AF.Identity,
                    bias=stat[:, o + 7:o + 8], scale=stat[:, o + 6:o + 7],
                    deps=[("v", m_nm), ("t", m_trm[bi])], inc="s")
        m_xn_s[bi] = m_xn
        pt = ptA if bi == 0 else ptB
        m_tr = 0
        for k in range(HK):
            m_tr = P.op("tensor", "transpose", pt[:, k * 128:(k + 1) * 128].bitcast(BF16),
                        xnt[bi][:, k * 128:(k + 1) * 128], g["ident_bf_s"][:, :],
                        deps=([("s", m_xn), ("v", m_ev_v[bi]), ("dc", 16 * 2)]
                              if k == 0 else []),
                        inc=("t" if k == HK - 1 else None))
        m_trm[bi] = m_tr
        deps = [("t", m_tr)] + list(extra_ev_deps)
        m_ev = P.op("vector", "tensor_copy", dst_fm, b3(pt[:, :], HK, 128),
                    deps=deps, inc="v")
        m_ev_v[bi] = m_ev
        return m_ev

    m_halo = ln_block(0, g["x_halo"], b3(g["halo_fm"][:, :], HK, 128))

    NCG_Z, NCG_F = IZ // 512, H // 512
    m_a2_t = 0
    m_xnsp_prev = 0
    m_wu = [0, 0]
    m_ev_s = [0, 0]
    m_zdma = [0, 0]
    for half in range(4):
        for j in range(8):
            tb = half * 8 + j
            dst = b3(xnfm[:, :], HK, QTR)[:, :, j * 128:(j + 1) * 128]
            ln_block(j % 2, g["x_span"][tb * 128:(tb + 1) * 128, :], dst,
                     extra_ev_deps=[("t", m_a2_t), ("dxn", m_xnsp_prev)])
        m_half_v = P.mark("v")
        for k in range(HK):
            m_xnsp_prev = P.dma("scalar", g["xn_sp"][k, :, half * QTR:(half + 1) * QTR],
                                xnfm[:, k * QTR:(k + 1) * QTR], "dxn",
                                deps=[("v", m_half_v)], fence=(k == 0))
        for cg in range(NCG_Z + NCG_F):
            bi = cg % 2
            is_ffn = cg >= NCG_Z
            cgl = cg - NCG_Z if is_ffn else cg
            wt = g["wf_t"] if is_ffn else g["wz_t"]
            dws = "dwA" if bi == 0 else "dwB"
            bdep = m_wf_built if is_ffn else m_wz_built
            m_wd = P.dma("scalar", wsl[bi][:, :].rearrange("p (k c) -> p k c", k=HK),
                         wt[cgl].rearrange("k p c -> p k c"), dws,
                         deps=[("t", m_wu[bi])] + [(f"dbw{i}", bdep[i]) for i in range(4)])
            for j in range(8):
                tb = half * 8 + j
                pb = j % 2
                for k in range(HK):
                    m_mm = P.op("tensor", "matmul", pz[pb][:, :],
                                xnfm[:, k * QTR + j * 128:k * QTR + (j + 1) * 128],
                                wsl[bi][:, k * 512:(k + 1) * 512],
                                start=(k == 0), stop=(not is_ffn and k == HK - 1),
                                deps=([("v", m_half_v), (dws, m_wd), ("s", m_ev_s[pb])]
                                      if k == 0 else []),
                                inc=("t" if (not is_ffn and k == HK - 1) else None))
                if is_ffn:
                    m_mm = P.op("tensor", "matmul", pz[pb][:, :], g["ones_bf"][0:1, :],
                                fbias_row[:, cgl * 512:(cgl + 1) * 512],
                                start=False, stop=True,
                                deps=[("dfb", m_fbr_d2), ("v", m_ones)], inc="t")
                dzp = "dzp0" if pb == 0 else "dzp1"
                if is_ffn:
                    m_e = P.op("scalar", "activation", zst[pb][:, :], pz[pb][:, :],
                               AF.Silu, deps=[("t", m_mm), (dzp, m_zdma[pb])], inc="s")
                    m_zdma[pb] = P.dma("scalar", g["g_sp"][tb, :, cgl * 512:(cgl + 1) * 512],
                                       zst[pb][:, :], dzp, deps=[("s", m_e)])
                else:
                    m_e = P.op("scalar", "activation", zst[pb][:, :], pz[pb][:, :],
                               AF.Silu, deps=[("t", m_mm), (dzp, m_zdma[pb])], inc="s")
                    m_zdma[pb] = P.dma("scalar", g["z_sp"][tb, :, cg * 512:(cg + 1) * 512],
                                       zst[pb][:, :], dzp, deps=[("s", m_e)])
                m_ev_s[pb] = m_e
            m_wu[bi] = P.mark("t")
        m_a2_t = P.mark("t")

    P.drain()
    P.emit()
    es.close()


def _block2(nc, g):
    P = Prog(nc, "b2")
    es = ExitStack()
    def sb(name, shape, dt):
        return es.enter_context(nc.sbuf_tensor(name, shape, dt))
    def ps(name, shape, dt):
        return es.enter_context(nc.psum_tensor(name, shape, dt))
    TP = T + 3

    xn_t = sb("b2_xnt", [128, HK * T], BF16)
    wsl = [sb("b2_wsl0", [128, HK * 512], BF16), sb("b2_wsl1", [128, HK * 512], BF16)]
    xbc = sb("b2_xbc", [128, XT * TP], BF16)
    dgw = [sb("b2_dgw0", [128, KC * 128], BF16), sb("b2_dgw1", [128, KC * 128], BF16)]
    dt_pre = sb("b2_dtpre", [NH, T], F32)
    sp1 = sb("b2_sp1", [NH, T], F32)
    dt_fm = sb("b2_dtfm", [NH, T], F32)
    abuf = sb("b2_abuf", [NH, CH], F32)
    acs_fm = sb("b2_acsfm", [NH, CH], F32)
    dcy = sb("b2_dcy", [NH, CH], F32)
    dsd = sb("b2_dsd", [128, CH], F32)
    dtsdt = sb("b2_dtsdt", [128, 4 * 128], F32)
    d_bc = sb("b2_dbc", [128, 4 * NH], F32)
    ddiag = sb("b2_ddiag", [NH, NH], F32)
    dexp = sb("b2_dexp", [NH, 1], F32)
    bcst = [sb(f"b2_bc{i}", [128, 1024], F32) for i in range(3)]
    m_hT = sb("b2_mh", [128, CH * NH], BF16)
    scm = sb("b2_scm", [128, 128], BF16)
    bmT = sb("b2_bmT", [128, NS], BF16)
    xsdtT = sb("b2_xsdtT", [128, IZ], BF16)
    xsdT = sb("b2_xsdT", [128, IZ], BF16)
    e_bf = sb("b2_ebf", [128, IZ], BF16)
    y_tok = sb("b2_ytok", [128, IZ], BF16)
    expacs = sb("b2_expacs", [128, 4 * NH], F32)

    pp = [ps("b2_pp0", [128, 512], F32), ps("b2_pp1", [128, 512], F32)]
    pcv = ps("b2_pcv", [128, 512], F32)
    ptr = [ps("b2_ptrA", [128, 1024], BF16), ps("b2_ptrB", [128, 1024], BF16)]
    psm = ps("b2_psm", [128, 512], F32)
    pyA = ps("b2_pyA", [128, 512], F32)
    pyB = ps("b2_pyB", [128, 512], F32)

    idb, idf, mask = g["ident_bf_s"], g["ident_f_s"], g["mask_s"]

    m_e0 = P.op("vector", "memset", g["e_all"][:, :], 0.0, inc="v")

    def xact(i):
        return xbc[:, i * TP + 3:i * TP + 3 + T]

    m_wu = [0, 0]

    def proj(rhs_fn, ncols, evac_fn, m_rhs, m_ev0, m_vguard=0):
        m_ev = m_ev0
        m_evb = [m_ev0, m_ev0]
        for ocg in range(9):
            bi = ocg % 2
            w = wsl[bi]
            dws = "dwA" if bi == 0 else "dwB"
            ncol = 512 if ocg < 8 else 384
            m_wd = P.dma("sync",
                         w[:, :].rearrange("p (k c) -> p k c", k=HK)[:, :, 0:ncol],
                         g["wx_t"][ocg].rearrange("k p c -> p k c")[:, :, 0:ncol],
                         dws, deps=[("t", m_wu[bi])])
            nblk = 4 if ocg < 8 else 3
            for ob in range(nblk):
                oc = ocg * 4 + ob
                pb = (ocg * 4 + ob) % 2
                is_dt = (ocg == 8 and ob == 2)
                rows = NH if is_dt else 128
                for k in range(HK):
                    if is_dt:
                        lhsT = w[:, k * 512 + 320:k * 512 + 384]
                    else:
                        lhsT = w[:, k * 512 + ob * 128:k * 512 + (ob + 1) * 128]
                    m_mm = P.op("tensor", "matmul", pp[pb][0:rows, 0:ncols], lhsT,
                                rhs_fn(k), start=(k == 0), stop=(k == HK - 1),
                                deps=([(dws, m_wd), ("dxt", m_rhs), ("s", m_evb[pb]),
                                       ("v", m_vguard)]
                                      if k == 0 else []),
                                inc=("t" if k == HK - 1 else None))
                m_ev = evac_fn(oc, pp[pb], m_mm)
                m_evb[pb] = m_ev
            m_wu[bi] = P.mark("t")
        return m_ev

    def ev_pro(oc, pz, m_mm):
        if oc < XT:
            return P.op("scalar", "copy", xbc[:, oc * TP:oc * TP + 3], pz[0:128, 0:3],
                        deps=[("t", m_mm)], inc="s")
        return P.op("scalar", "copy", dt_pre[:, 0:3], pz[0:NH, 0:3],
                    deps=[("t", m_mm)], inc="s")

    m_ev = proj(lambda k: g["halo_fm"][:, k * 128 + 125:k * 128 + 128], 3, ev_pro, 0, 0)

    m_states_t = 0
    m_chunk_v = P.mark("v")
    m_ydma = 0
    m_conv_s = [m_ev] * XT
    m_dtfree_v = 0
    m_trfree = [0, 0]
    m_mh_t = 0
    m_xsd_t = 0
    m_dgw_use = [0, 0]
    m_scm_prev = 0
    m_bev_prev = 0
    m_dd_t = 0
    m_dd_prev = 0
    m_acsT_prev = 0
    m_de_prev = 0
    m_st_sadd = 0
    m_sexp_slot = [0, 0, 0]
    m_fl_prev = 0
    m_yslot = [0] * 8
    m_fold_r = [0, 0, 0, 0]

    for t in range(NTL):
        m_xn = 0
        for k in range(HK):
            m_xn = P.dma("scalar", xn_t[:, k * T:(k + 1) * T],
                         g["xn_sp"][k, :, t * T:(t + 1) * T], "dxt",
                         deps=[("t", m_states_t)] if k == 0 else [],
                         fence=(k == 0))

        m_ssd_t = m_states_t
        m_ssd_v = m_chunk_v

        def ev_main(oc, pz, m_mm):
            if oc < XT:
                return P.op("scalar", "copy", xbc[:, oc * TP + 3:oc * TP + 3 + T],
                            pz[0:128, 0:T],
                            deps=[("t", m_mm), ("s", m_conv_s[oc]),
                                  ("t", m_ssd_t), ("v", m_ssd_v)],
                            inc="s")
            return P.op("scalar", "copy", dt_pre[:, :], pz[0:NH, 0:T],
                        deps=[("t", m_mm), ("v", m_dtfree_v)], inc="s")

        m_ev = proj(lambda k: xn_t[:, k * T:(k + 1) * T], T, ev_main, m_xn, m_ev,
                    m_vguard=m_ssd_v)
        m_proj_s = P.mark("s")

        m_cv_bank = [m_proj_s, m_proj_s]
        for i in range(XT):
            di = i % 2
            for k in range(KC):
                P.op("vector", "tensor_tensor",
                     dgw[di][:, k * 128:(k + 1) * 128], idb[:, :],
                     g["cw_s"][:, k * XT + i:k * XT + i + 1].broadcast_to((128, 128)),
                     OP.mult, deps=[("t", m_dgw_use[di])] if k == 0 else [], inc="v")
            m_dg = P.mark("v")
            for k in range(KC):
                m_cm = P.op("tensor", "matmul", pp[di][:, 0:T],
                            dgw[di][:, k * 128:(k + 1) * 128],
                            xbc[:, i * TP + k:i * TP + k + T],
                            start=(k == 0), stop=(k == KC - 1),
                            deps=([("s", m_proj_s), ("v", m_dg), ("s", m_cv_bank[di]),
                                   ("v", m_chunk_v)] if k == 0 else []),
                            inc=("t" if k == KC - 1 else None))
            m_dgw_use[di] = m_cm
            m_h = P.op("vector", "tensor_copy", xbc[:, i * TP:i * TP + 3],
                       xbc[:, i * TP + T:i * TP + T + 3], deps=[("t", m_cm)], inc="v")
            m_cv_bank[di] = P.op("scalar", "activation",
                                 xbc[:, i * TP + 3:i * TP + 3 + T],
                                 pp[di][:, 0:T], AF.Silu, bias=g["cb_s"][:, i:i + 1],
                                 deps=[("t", m_cm), ("v", m_h)], inc="s")
            m_conv_s[i] = m_cv_bank[di]
        m_conv_all = P.mark("s")

        m1 = P.op("vector", "tensor_scalar", dt_pre[:, :], dt_pre[:, :],
                  g["dtb_s"][:, 0:1], None, OP.add, deps=[("s", m_proj_s)], inc="v")
        m2 = P.op("vector", "scalar_tensor_tensor", sp1[:, :], dt_pre[:, :], -1.0,
                  dt_pre[:, :], OP.mult, OP.max, deps=[("v", m1)], inc="v")
        m3 = P.op("scalar", "activation", sp1[:, :], sp1[:, :], AF.Exp, scale=-1.0,
                  deps=[("v", m2)], inc="s")
        m4 = P.op("scalar", "activation", sp1[:, :], sp1[:, :], AF.Ln,
                  bias=g["onecol"][0:NH, 0:1], deps=[("s", m3)], inc="s")
        m5 = P.op("vector", "tensor_scalar", dt_fm[:, :], dt_pre[:, :], 0.0, None,
                  OP.max, deps=[("s", m4)], inc="v")
        m_dt = P.op("vector", "tensor_tensor", dt_fm[:, :], dt_fm[:, :], sp1[:, :],
                    OP.add, deps=[("v", m5)], inc="v")

        m_fl4 = [0, 0, 0, 0]
        me1l = [0, 0, 0, 0]
        me3l = 0
        m_d2p = 0
        for c in range(CPT):
            cc = t * CPT + c
            m_ah = P.op("vector", "tensor_scalar", abuf[:, :], dt_fm[:, c * CH:(c + 1) * CH],
                        g["acol_s"][:, 0:1], None, OP.mult,
                        deps=[("v", m_dt), ("v", m_ssd_v)], inc="v")
            m_acs = P.op("vector", "tensor_tensor_scan", acs_fm[:, :], abuf[:, :],
                         abuf[:, :], 0.0, OP.add, OP.bypass,
                         deps=[("v", m_ah), ("t", m_acsT_prev), ("s", m_de_prev),
                               ("dfl", m_fl_prev)],
                         inc="v")
            P.op("vector", "tensor_copy", g["aend_all"][:, cc:cc + 1],
                 acs_fm[:, CH - 1:CH], deps=[("v", m_acs)], inc="v")
            m_dcy = P.op("scalar", "activation", dcy[:, :], acs_fm[:, :], AF.Exp,
                         scale=-1.0, bias=acs_fm[:, CH - 1:CH],
                         deps=[("v", m_acs), ("s", me1l[max(0, c - 1)])], inc="s")
            m_d1 = P.op("vector", "tensor_copy", dsd[0:NH, :], dt_fm[:, c * CH:(c + 1) * CH],
                        deps=[("v", m_dt), ("s", me1l[max(0, c - 1)]),
                              ("t", m_states_t)], inc="v")
            m_d2 = P.op("vector", "tensor_tensor", dsd[NH:128, :],
                        dt_fm[:, c * CH:(c + 1) * CH],
                        dcy[:, :], OP.mult, deps=[("s", m_dcy), ("v", m_d1)], inc="v")
            m_de = P.op("scalar", "activation", dexp[:, :], acs_fm[:, CH - 1:CH],
                        AF.Exp, deps=[("v", m_acs), ("v", m_dd_prev)], inc="s")
            m_de_prev = m_de
            m_dd = P.op("vector", "tensor_scalar", ddiag[:, :], idf[0:NH, 0:NH],
                        dexp[:, 0:1], None, OP.mult, deps=[("s", m_de), ("t", m_dd_t)],
                        inc="v")
            m_dd_prev = m_dd
            mt1 = P.op("tensor", "transpose", psm[:, 128:256].bitcast(F32), dsd[:, :],
                       idf[:, :], deps=[("v", m_d2), ("v", m_scm_prev),
                                        ("v", m_ssd_v), ("s", me3l)], inc="t")
            me1l[c] = P.op("scalar", "copy", dtsdt[:, c * 128:(c + 1) * 128],
                           psm[:, 128:256], deps=[("t", mt1)], inc="s")
            mt2 = P.op("tensor", "transpose", psm[:, 256:320].bitcast(F32), acs_fm[:, :],
                       idf[0:NH, 0:NH], deps=[("v", m_acs), ("s", me1l[c])], inc="t")
            m_acsT_prev = mt2
            me2 = P.op("scalar", "copy", g["acst_all"][:, cc * NH:(cc + 1) * NH],
                       psm[:, 256:320], deps=[("t", mt2)], inc="s")
            mt3 = P.op("tensor", "matmul", psm[:, 320:384], g["ones64"][:, :], ddiag[:, :],
                       start=True, stop=True, deps=[("v", m_dd), ("s", me2)], inc="t")
            m_dd_t = mt3
            me3l = P.op("scalar", "copy", d_bc[:, c * NH:(c + 1) * NH],
                        psm[:, 320:384], deps=[("t", mt3)], inc="s")
            m_fl4[c] = P.dma("gpsimd", g["acs_row"][cc].rearrange("(h l) -> h l", h=NH),
                             acs_fm[:, :], "dfl", deps=[("v", m_acs)])
            m_fl_prev = m_fl4[c]
        cc0 = t * CPT
        m_ex = P.op("scalar", "activation", expacs[:, :],
                    g["acst_all"][:, cc0 * NH:(cc0 + 4) * NH], AF.Exp,
                    deps=[("s", me3l), ("v", m_ssd_v)], inc="s")

        m_scm_c = {}
        m_esnap_c = {}
        m_bev_c = {}
        m_mh_jc = {}
        m_exp_jc = {}

        def stage_pre(c):
            nonlocal m_scm_prev
            cc = t * CPT + c
            csl = slice(c * CH, (c + 1) * CH)
            msc = P.op("tensor", "matmul", psm[:, 0:128], xact(32)[:, csl],
                       xact(33)[:, csl], start=True, stop=True,
                       deps=[("s", m_conv_all), ("s", me3l), ("v", m_scm_prev),
                             ("v", m_st_sadd)], inc="t")
            m_scm_c[c] = P.op("vector", "tensor_tensor", scm[:, :], psm[:, 0:128],
                              mask[:, :], OP.mult,
                              deps=[("t", msc),
                                    ("v", m_mh_jc[c - 1][7] if c > 0 else 0)],
                              inc="v")
            m_scm_prev = m_scm_c[c]
            P.op("vector", "tensor_copy", g["cm_all"][:, cc * 128:(cc + 1) * 128],
                 xact(33)[:, csl], deps=[("s", m_conv_all)], inc="v")

        def stage_folds(c):
            nonlocal m_bev_prev
            csl = slice(c * CH, (c + 1) * CH)
            for rnd in range(4):
                pt = ptr[rnd % 2]
                for i in range(8):
                    m_tr = P.op("tensor", "transpose",
                                pt[:, i * 128:(i + 1) * 128].bitcast(BF16),
                                xact(rnd * 8 + i)[:, csl], idb[:, :],
                                deps=([("s", m_conv_all), ("v", m_trfree[rnd % 2]),
                                       ("s", m_bev_prev if rnd == 0 else 0),
                                       ("t", m_mh_t), ("t", m_xsd_t)] if i == 0 else []),
                                inc=("t" if i == 7 else None))
                h0 = rnd * 16
                src2 = pt[:, :].rearrange("p (a b c) -> p a b c", a=8, b=2)
                dtop = b3(dtsdt[:, c * 128 + h0:c * 128 + h0 + 16], 8, 2).unsqueeze(3) \
                    .broadcast_to((128, 8, 2, 64))
                sdop = b3(dtsdt[:, c * 128 + NH + h0:c * 128 + NH + h0 + 16], 8, 2) \
                    .unsqueeze(3).broadcast_to((128, 8, 2, 64))
                m_f1 = P.op("vector", "tensor_tensor",
                            xsdtT[:, rnd * 1024:(rnd + 1) * 1024]
                            .rearrange("p (a b c) -> p a b c", a=8, b=2),
                            src2, dtop, OP.mult,
                            deps=[("t", m_tr), ("s", me1l[c])], inc="v")
                m_f2 = P.op("vector", "tensor_tensor",
                            xsdT[:, rnd * 1024:(rnd + 1) * 1024]
                            .rearrange("p (a b c) -> p a b c", a=8, b=2),
                            src2, sdop, OP.mult, deps=[("v", m_f1)], inc="v")
                m_trfree[rnd % 2] = m_f2
                m_fold_r[rnd] = m_f1
            m_btr = P.op("tensor", "transpose", ptr[0][:, 0:128].bitcast(BF16),
                         xact(32)[:, csl], idb[:, :], deps=[("v", m_trfree[0])],
                         inc="t")
            m_bev_c[c] = P.op("scalar", "copy", bmT[:, :], ptr[0][:, 0:128],
                              deps=[("t", m_btr)], inc="s")
            m_bev_prev = m_bev_c[c]

        def stage_esnap(c):
            m_esnap_c[c] = P.op("vector", "tensor_copy", e_bf[:, :], g["e_all"][:, :],
                                deps=[("t", m_mh_t)], inc="v")

        def emit_mult(c, j):
            m_mh_jc[c][j] = P.op("vector", "tensor_tensor",
                                 b3(m_hT[:, j * 1024:(j + 1) * 1024], 8, 128),
                                 b3(m_hT[:, j * 1024:(j + 1) * 1024], 8, 128),
                                 scm[:, :].unsqueeze(1).broadcast_to((128, 8, 128)),
                                 OP.mult,
                                 deps=[("s", m_exp_jc[c][j]), ("v", m_scm_c[c])],
                                 inc="v")

        def piece(c, j):
            cc = t * CPT + c
            acsT_c = g["acst_all"][:, cc * NH:(cc + 1) * NH]
            pbc = (c * 8 + j) % 3
            dbs = f"dbc{pbc}"
            m_bc = P.dma("gpsimd", bcst[pbc][:, :],
                         g["acs_row"][cc][j * 1024:(j + 1) * 1024].unsqueeze(0)
                         .partition_broadcast(128), dbs,
                         deps=[("dfl", m_fl4[c]), ("s", m_sexp_slot[pbc])])
            m_sub = P.op("vector", "scalar_tensor_tensor",
                         b3(bcst[pbc][:, :], 8, 128), b3(bcst[pbc][:, :], 8, 128),
                         -1.0, acsT_c[:, 8 * j:8 * j + 8].unsqueeze(2)
                         .broadcast_to((128, 8, 128)),
                         OP.mult, OP.add,
                         deps=[(dbs, m_bc)], inc="v")
            m_rl = P.op("scalar", "activation", bcst[pbc][:, :], bcst[pbc][:, :],
                        AF.Relu, deps=[("v", m_sub)], inc="s")
            m_exp_jc[c][j] = P.op("scalar", "activation",
                                  m_hT[:, j * 1024:(j + 1) * 1024], bcst[pbc][:, :],
                                  AF.Exp, scale=-1.0,
                                  deps=[("s", m_rl), ("t", m_yslot[j])],
                                  inc="s")
            m_sexp_slot[pbc] = m_exp_jc[c][j]
            if j >= 1:
                emit_mult(c, j - 1)
            if j == 7:
                emit_mult(c, 7)

        def ygroup(c, gi, m_ev_pair):
            nonlocal m_mh_t
            cc = t * CPT + c
            csl = slice(c * CH, (c + 1) * CH)
            exslice = expacs[:, c * NH:(c + 1) * NH]
            pA, pB = (pyA, pyB) if gi % 2 == 0 else (pp[0], pp[1])
            for hh in range(8):
                h = gi * 8 + hh
                P.op("tensor", "matmul", pA[:, hh * 64:(hh + 1) * 64],
                     m_hT[:, h * 128:(h + 1) * 128],
                     xsdtT[:, h * 64:(h + 1) * 64], start=True, stop=False,
                     deps=([("v", m_mh_jc[c][gi]), ("v", m_esnap_c[c]),
                            ("v", m_ev_pair[gi % 2]), ("v", m_fold_r[gi // 2]),
                            ("s", m_conv_all)] if hh == 0 else []))
                P.op("tensor", "matmul", pA[:, hh * 64:(hh + 1) * 64],
                     xact(h // 2)[(h % 2) * 64:(h % 2) * 64 + 64, csl],
                     g["di_s"][(h % 2) * 64:(h % 2) * 64 + 64,
                               h * 64:(h + 1) * 64], start=False, stop=True)
                m_yg = P.op("tensor", "matmul", pB[:, hh * 64:(hh + 1) * 64],
                            xact(33)[:, csl], e_bf[:, h * 64:(h + 1) * 64],
                            start=True, stop=True,
                            inc=("t" if hh == 7 else None))
            m_mh_t = m_yg
            m_yslot[gi] = m_yg
            exop = exslice[:, gi * 8:(gi + 1) * 8].unsqueeze(2) \
                .broadcast_to((128, 8, 64))
            m_e1 = P.op("vector", "tensor_tensor",
                        b3(y_tok[:, gi * 512:(gi + 1) * 512], 8, 64),
                        b3(pB[:, :], 8, 64), exop, OP.mult,
                        deps=[("t", m_yg), ("s", m_ex), ("dyp", m_ydma)], inc="v")
            m_ev_pair[gi % 2] = P.op("vector", "tensor_tensor",
                                     y_tok[:, gi * 512:(gi + 1) * 512],
                                     y_tok[:, gi * 512:(gi + 1) * 512], pA[:, :],
                                     OP.add, deps=[("v", m_e1)], inc="v")

        def stage_states(c, m_ev_pair):
            nonlocal m_ydma, m_st_sadd, m_states_t, m_xsd_t, m_chunk_v
            cc = t * CPT + c
            m_ydma = P.dma("scalar", g["yp_sp"][cc], y_tok[:, :], "dyp",
                           deps=[("v", m_ev_pair[0]), ("v", m_ev_pair[1])])
            m_emul = P.op("vector", "tensor_tensor", b3(g["e_all"][:, :], NH, PH),
                          b3(g["e_all"][:, :], NH, PH),
                          d_bc[:, c * NH:(c + 1) * NH].unsqueeze(2)
                          .broadcast_to((128, NH, PH)), OP.mult,
                          deps=[("s", me3l), ("v", m_esnap_c[c])], inc="v")
            m_st_sadd = m_emul
            for gi in range(8):
                m_sm = P.op("tensor", "matmul", pcv[:, :], bmT[:, :],
                            xsdT[:, gi * 512:(gi + 1) * 512], start=True, stop=True,
                            deps=[("s", m_bev_c[c]), ("v", m_trfree[0]),
                                  ("v", m_trfree[1]),
                                  ("v", m_st_sadd), ("v", m_emul)],
                            inc="t")
                m_st_sadd = P.op("vector", "tensor_tensor",
                                 g["e_all"][:, gi * 512:(gi + 1) * 512],
                                 g["e_all"][:, gi * 512:(gi + 1) * 512], pcv[:, :],
                                 OP.add, deps=[("t", m_sm)], inc="v")
            m_states_t = P.mark("t")
            m_xsd_t = m_states_t
            m_chunk_v = P.mark("v")

        for c in range(CPT):
            m_mh_jc[c] = [0] * 8
            m_exp_jc[c] = [0] * 8
        stage_pre(0)
        stage_folds(0)
        stage_esnap(0)
        for j in range(8):
            piece(0, j)
        for c in range(CPT):
            if c < CPT - 1:
                stage_pre(c + 1)
            m_ev_pair = [0, 0]
            for gi in range(8):
                ygroup(c, gi, m_ev_pair)
                if c < CPT - 1:
                    piece(c + 1, gi)
            stage_states(c, m_ev_pair)
            if c < CPT - 1:
                stage_folds(c + 1)
                stage_esnap(c + 1)
        m_dtfree_v = P.mark("v")

    m_z0 = P.op("vector", "memset", bcst[0][:, 0:NH], 0.0, deps=[("v", P.mark("v"))],
                inc="v")
    P.dma("gpsimd", g["ag_in"][:, IZ:IZ + NH], bcst[0][:, 0:NH], "dex",
          deps=[("v", m_z0)])
    m_exp_e = P.dma("gpsimd", g["ag_in"][:, 0:IZ], g["e_all"][:, :], "dex",
                    deps=[("v", P.mark("v"))], fence=False)
    dred = sb("b2_dred", [NH, 1], F32)
    m_red = P.op("vector", "tensor_reduce", dred[:, :], g["aend_all"][:, :], AX.X,
                 OP.add, deps=[("v", P.mark("v"))], inc="v")
    m_exp_d = P.dma("gpsimd", g["ag_in"][127:128, IZ:IZ + NH].rearrange("o h -> h o"),
                    dred[:, :], "dex", deps=[("v", m_red)])
    if _DEBUG:
        P.dma("gpsimd", g["dbg_e"], g["ag_in"], "dex", deps=[("dex", P.mark("dex"))])
    P.drain()
    P.emit()
    es.close()


def _block3(nc, g):
    P = Prog(nc, "b3")
    es = ExitStack()
    def sb(name, shape, dt):
        return es.enter_context(nc.sbuf_tensor(name, shape, dt))
    def ps(name, shape, dt):
        return es.enter_context(nc.psum_tensor(name, shape, dt))

    wbc = sb("b3_wbc", [128, NH], F32)
    daug = sb("b3_daug", [8, NH], F32)
    cmb_s = sb("b3_cmb", [8, 8], F32)
    cmbb = sb("b3_cmbb", [1, 8], F32)
    wall = sb("b3_wall", [NH, 8], F32)
    zbuf = [sb("b3_zbuf0", [128, IZ], BF16), sb("b3_zbuf1", [128, IZ], BF16)]
    ypbuf = [sb("b3_ypbuf0", [128, IZ], BF16), sb("b3_ypbuf1", [128, IZ], BF16)]
    y32 = sb("b3_y32", [128, IZ], F32)
    yfn = sb("b3_yfn", [128, IZ], BF16)
    erbf = sb("b3_erbf", [128, IZ // 2], BF16)
    expacs2 = sb("b3_expacs2", [128, NH], F32)
    dexp2 = sb("b3_dexp2", [NH, 1], F32)
    ddiag2 = sb("b3_ddiag2", [NH, NH], F32)
    d_bc2 = sb("b3_dbc2", [128, NH], F32)
    rcol = sb("b3_rcol", [128, 4], F32)
    yffm = [sb(f"b3_yffm{i}", [128, IZ], BF16) for i in range(4)]
    gb = [sb(f"b3_gb{i}", [128, H], BF16) for i in range(4)]
    xr = [sb(f"b3_xr{i}", [128, H], F32) for i in range(4)]
    wot = [sb(f"b3_wot{i}", [128, 4 * 512], BF16) for i in range(3)]
    ost = [sb("b3_ost0", [128, 512], F32), sb("b3_ost1", [128, 512], F32)]

    po = [ps(f"b3_po{i}", [128, 512], F32) for i in range(4)]
    pyc = [ps("b3_pyc0", [128, 512], F32), ps("b3_pyc1", [128, 512], F32)]
    ptr2 = [ps("b3_ptr2A", [128, 1024], BF16), ps("b3_ptr2B", [128, 1024], BF16)]

    idb = g["ident_bf_s"]
    e_run = g["e_all"]

    m_cc = P.op("gpsimd", "collective_compute", "AllGather", OP.bypass,
                replica_groups=[list(range(8))],
                ins=[g["ag_in"]], outs=[g["ag_out"]], inc="cc")
    m_da = P.dma("sync", daug[0:8, :],
                 g["ag_out"].rearrange("(j p) c -> j p c", p=128)[:, 127, IZ:IZ + NH],
                 "dcc", deps=[("cc", m_cc)])
    m_cm = P.dma("sync", cmb_s[:, :], g["cmb"][0:8, :], "dcc")
    m_cb = P.dma("sync", cmbb[:, :], g["cmb"][8:9, :], "dcc")
    P.op("tensor", "matmul", pyc[0][0:NH, 0:8], daug[:, :], cmb_s[:, :],
         start=True, stop=False, deps=[("dcc", m_cb)])
    mt = P.op("tensor", "matmul", pyc[0][0:NH, 0:8], g["ones_f"][0:1, 0:NH],
              cmbb[:, :], start=False, stop=True, deps=[("dcc", m_cb)], inc="t")
    me = P.op("scalar", "copy", wall[:, :], pyc[0][0:NH, 0:8], deps=[("t", mt)], inc="s")
    me = P.op("scalar", "activation", wall[:, :], wall[:, :], AF.Exp,
              deps=[("s", me)], inc="s")
    mt = P.op("tensor", "transpose", pyc[1][0:8, 0:NH].bitcast(F32), wall[:, :],
              g["ident_f_s"][0:NH, 0:NH], deps=[("s", me)], inc="t")
    me = P.op("scalar", "copy", daug[0:8, :], pyc[1][0:8, 0:NH], deps=[("t", mt)], inc="s")
    m_wr = P.dma("gpsimd", g["w_row"], daug[0:8, :], "dwr", deps=[("s", me)])
    m_prev = 0
    for j in range(8):
        m_sj = P.dma("sync", y32[:, :].bitcast(F32),
                     g["ag_out"][j * 128:(j + 1) * 128, 0:IZ], "dsj",
                     deps=[("cc", m_cc), ("v", m_prev)])
        m_wb = P.dma("gpsimd", wbc[:, :],
                     g["w_row"][j].unsqueeze(0).partition_broadcast(128), "dwb",
                     deps=[("dwr", m_wr), ("v", m_prev)])
        m_t = P.op("vector", "tensor_tensor", b3(y32[:, :], NH, PH),
                   b3(y32[:, :], NH, PH),
                   wbc[:, :].unsqueeze(2).broadcast_to((128, NH, PH)), OP.mult,
                   deps=[("dsj", m_sj), ("dwb", m_wb)], inc="v")
        if j == 0:
            m_prev = P.op("vector", "tensor_copy", e_run[:, :], y32[:, :],
                          deps=[("v", m_t)], inc="v")
        else:
            m_prev = P.op("vector", "tensor_tensor", e_run[:, :], e_run[:, :],
                          y32[:, :], OP.add, deps=[("v", m_t)], inc="v")
    m_er = m_prev

    m_out = 0
    m_po_free = [0, 0, 0, 0]
    m_tr2free = [0, 0]
    m_yffm_t = [0, 0, 0, 0]
    m_gxfree = [(0, 0)] * 4
    m_wotu = [0, 0, 0]
    m_er_t = 0
    m_y32_free = 0
    m_pyc_v = [0, 0]
    m_pyc_s0 = 0
    m_dd2_t = 0
    m_dd2_s = 0
    m_eg_t = 0

    m_yf_prev = 0
    m_yv_prev = 0
    m_yf_pb = [0, 0]
    m_yv_pb = [0, 0]
    m_trlast = 0
    m_gd = [0] * 4
    m_xd = [0] * 4
    m_ost_d = [0, 0]
    m_yffm_vs = [0] * 4

    def prep_gen(cc):
        nonlocal m_dd2_s, m_dd2_t, m_er, m_pyc_s0, m_er_t, m_y32_free
        nonlocal m_trlast
        sl = cc % 4
        zpb = cc % 2
        m_z = P.dma("sync", zbuf[zpb][:, :], g["z_sp"][cc],
                    "dz0" if zpb == 0 else "dz1", deps=[("v", m_yf_pb[zpb])])
        m_yp = P.dma("sync", ypbuf[zpb][:, :], g["yp_sp"][cc],
                     "dyp0" if zpb == 0 else "dyp1", deps=[("v", m_yv_pb[zpb])])
        m_gd[sl] = P.dma("sync", gb[sl][:, :], g["g_sp"][cc], "dgb",
                         deps=[("v", m_gxfree[sl][0])])
        m_xd[sl] = P.dma("sync", xr[sl][:, :],
                         g["x_span"][cc * 128:(cc + 1) * 128, :], "dxr",
                         deps=[("v", m_gxfree[sl][1])])
        yield
        if cc > 0:
            m_dre = P.op("scalar", "activation", dexp2[:, :],
                         g["aend_all"][:, cc - 1:cc], AF.Exp,
                         deps=[("s", m_dd2_s)], inc="s")
            yield
            m_dd = P.op("vector", "tensor_scalar", ddiag2[:, :],
                        g["ident_f_s"][0:NH, 0:NH], dexp2[:, 0:1], None, OP.mult,
                        deps=[("s", m_dre), ("t", m_dd2_t)], inc="v")
            yield
            mt = P.op("tensor", "matmul", pyc[0][:, 0:NH], g["ones64"][:, :],
                      ddiag2[:, :], start=True, stop=True,
                      deps=[("v", m_dd), ("v", m_pyc_v[0]), ("v", m_er),
                            ("s", m_pyc_s0)], inc="t")
            m_dd2_t = mt
            me = P.op("scalar", "copy", d_bc2[:, :], pyc[0][:, 0:NH],
                      deps=[("t", mt)], inc="s")
            m_dd2_s = me
            yield
            m_er = P.op("vector", "tensor_tensor", b3(e_run[:, :], NH, PH),
                        b3(e_run[:, :], NH, PH),
                        d_bc2[:, :].unsqueeze(2).broadcast_to((128, NH, PH)),
                        OP.mult, deps=[("s", me), ("t", m_er_t)], inc="v")
            m_pyc_s0 = me
            yield
        m_ex2 = P.op("scalar", "activation", expacs2[:, :],
                     g["acst_all"][:, cc * NH:(cc + 1) * NH], AF.Exp,
                     deps=[("s", m_y32_free)], inc="s")
        yield
        m_yv = 0
        m_eg = [0, 0]
        for gi in range(8):
            pb = gi % 2
            if gi % 4 == 0:
                m_eg_l = P.op("vector", "tensor_copy", erbf[:, :],
                              e_run[:, (gi // 4) * 2048:(gi // 4 + 1) * 2048],
                              deps=[("v", m_er), ("t", m_er_t),
                                    ("t", m_eg_t[0])], inc="v")
                yield
            for hh in range(8):
                h = gi * 8 + hh
                m_yg = P.op("tensor", "matmul", pyc[pb][:, hh * 64:(hh + 1) * 64],
                            g["cm_all"][:, cc * 128:(cc + 1) * 128],
                            erbf[:, (h * 64) % 2048:(h * 64) % 2048 + 64],
                            start=True, stop=True,
                            deps=([("v", m_eg_l), ("v", m_pyc_v[pb]),
                                   ("s", m_pyc_s0 if pb == 0 else 0)]
                                  if hh == 0 else []),
                            inc=("t" if hh == 7 else None))
            if gi % 4 == 3:
                m_eg_t[0] = m_yg
            m_er_t = m_yg
            exop = expacs2[:, gi * 8:(gi + 1) * 8].unsqueeze(2) \
                .broadcast_to((128, 8, 64))
            m_y1 = P.op("vector", "tensor_tensor",
                        b3(y32[:, gi * 512:(gi + 1) * 512], 8, 64),
                        b3(pyc[pb][:, :], 8, 64), exop, OP.mult,
                        deps=[("t", m_yg), ("s", m_ex2),
                              ("s", m_y32_free if gi == 0 else 0)], inc="v")
            m_yv = P.op("vector", "tensor_tensor", y32[:, gi * 512:(gi + 1) * 512],
                        y32[:, gi * 512:(gi + 1) * 512],
                        ypbuf[zpb][:, gi * 512:(gi + 1) * 512], OP.add,
                        deps=[("v", m_y1),
                              ("dyp0" if zpb == 0 else "dyp1", m_yp)], inc="v")
            m_pyc_v[pb] = m_yv
            yield
        m_yf = P.op("vector", "tensor_tensor", y32[:, :], y32[:, :],
                    zbuf[zpb][:, :], OP.mult,
                    deps=[("v", m_yv), ("dz0" if zpb == 0 else "dz1", m_z)],
                    inc="v")
        m_yf_pb[zpb] = m_yf
        m_yv_pb[zpb] = m_yv
        yield
        m_ss = P.op("scalar", "activation", yfn[:, :], y32[:, :], AF.Square,
                    accum_out=rcol[:, 0:1],
                    deps=[("v", m_yf), ("t", m_trlast)], inc="s")
        m_r1 = P.op("scalar", "activation", rcol[:, 1:2], rcol[:, 0:1], AF.Ln,
                    scale=1.0 / IZ, bias=g["epscol"][:, 0:1],
                    deps=[("s", m_ss)], inc="s")
        m_r2 = P.op("scalar", "activation", rcol[:, 2:3], rcol[:, 1:2], AF.Exp,
                    scale=-0.5, deps=[("s", m_r1)], inc="s")
        m_yn = P.op("scalar", "activation", yfn[:, :], y32[:, :], AF.Identity,
                    scale=rcol[:, 2:3], deps=[("s", m_r2)], inc="s")
        m_y32_free = m_yn
        yield
        for rnd in range(4):
            pt = ptr2[rnd % 2]
            for i in range(8):
                m_tr = P.op("tensor", "transpose",
                            pt[:, i * 128:(i + 1) * 128].bitcast(BF16),
                            yfn[:, (rnd * 8 + i) * 128:(rnd * 8 + i + 1) * 128],
                            idb[:, :],
                            deps=([("s", m_yn), ("v", m_tr2free[rnd % 2]),
                                   ("t", m_yffm_t[sl])] if i == 0 else []),
                            inc=("t" if i == 7 else None))
            m_trlast = m_tr
            m_tr2free[rnd % 2] = P.op(
                "vector", "tensor_copy", yffm[sl][:, rnd * 1024:(rnd + 1) * 1024],
                pt[:, :], deps=[("t", m_tr)], inc="v")
            yield
        m_yffm_vs[sl] = m_tr2free[(4 - 1) % 2]

    def drain_gens(gens, n):
        cnt = 0
        while gens and cnt < n:
            try:
                next(gens[0])
                cnt += 1
            except StopIteration:
                gens.pop(0)

    m_eg_t = [0]
    gens = [prep_gen(0), prep_gen(1)]
    drain_gens(gens, 10000)
    for gg in range(16):
        s0 = (gg * 2) % 4
        p0_, p1_ = (0, 1) if gg % 2 == 0 else (2, 3)
        if gg < 15:
            gens = [prep_gen(gg * 2 + 2), prep_gen(gg * 2 + 3)]
        else:
            gens = []
        for cg in range(4):
            for kq in range(IZ // 512):
                bi = kq % 3
                dwo = f"dwo{bi}"
                m_wd = P.dma("sync", wot[bi][:, :].rearrange("p (k c) -> p k c", k=4),
                             g["wo_t"][cg, 4 * kq:4 * kq + 4].rearrange("k p c -> p k c"),
                             dwo, deps=[("t", m_wotu[bi])])
                for ksub in range(4):
                    k = kq * 4 + ksub
                    for cl in range(2):
                        sl = s0 + cl
                        pidx = (p0_, p1_)[cl]
                        dep = []
                        if ksub == 0 and cl == 0:
                            dep.append((dwo, m_wd))
                        if kq == 0 and ksub == 0:
                            dep += [("v", m_yffm_vs[sl]), ("v", m_po_free[pidx])]
                        m_lw = P.op("tensor", "matmul", po[pidx][:, :],
                                    yffm[sl][:, k * 128:(k + 1) * 128],
                                    wot[bi][:, ksub * 512:(ksub + 1) * 512],
                                    start=(k == 0), stop=(k == IZ // 128 - 1),
                                    deps=dep,
                                    inc=("t" if cl == 1 and ksub == 3 else None))
                m_wotu[bi] = m_lw
                drain_gens(gens, 2)
            m_mm = P.mark("t")
            drain_gens(gens, 4)
            for cl in range(2):
                sl = s0 + cl
                cc = gg * 2 + cl
                pidx = (p0_, p1_)[cl]
                ob = cl
                dos = "dout0" if ob == 0 else "dout1"
                m_t = P.op("vector", "tensor_tensor", ost[ob][:, :], po[pidx][:, :],
                           gb[sl][:, cg * 512:(cg + 1) * 512], OP.mult,
                           deps=[("t", m_mm), ("dgb", m_gd[sl]), ("dxr", m_xd[sl]),
                                 (dos, m_ost_d[ob])], inc="v")
                m_o = P.op("vector", "tensor_tensor", ost[ob][:, :], ost[ob][:, :],
                           xr[sl][:, cg * 512:(cg + 1) * 512], OP.add,
                           deps=[("v", m_t)], inc="v")
                m_po_free[pidx] = m_o
                m_ost_d[ob] = P.dma("scalar", g["out_span"][cc * 128:(cc + 1) * 128,
                                                           cg * 512:(cg + 1) * 512],
                                    ost[ob][:, :], dos, deps=[("v", m_o)])
        drain_gens(gens, 10000)
        for cl in range(2):
            sl = s0 + cl
            m_gxfree[sl] = (m_po_free[(p0_, p1_)[cl]], m_po_free[(p0_, p1_)[cl]])
            m_yffm_t[sl] = P.mark("t")

    P.drain()
    P.emit()
    es.close()


_NC_CACHE = {}


def kernel(L0_attn_hidden_states, L0_attn_W_in, L0_attn_conv_w, L0_attn_conv_b,
           L0_attn_dt_bias, L0_attn_A_log, L0_attn_D, L0_attn_norm_w, L0_attn_W_out,
           L0_ffn_weight, L0_ffn_scale_weight, L0_ffn_bias):
    x = np.ascontiguousarray(np.asarray(L0_attn_hidden_states, dtype=np.float32))
    ident_bf = np.eye(128, dtype=np.float32).astype(ml_dtypes.bfloat16)
    ident_f = np.eye(128, dtype=np.float32)
    mask = np.triu(np.ones((128, 128), dtype=np.float32))

    if "nc" not in _NC_CACHE:
        _NC_CACHE["nc"] = build_nc()
    nc = _NC_CACHE["nc"]

    def cmb_for(core):
        grp, sp = core // 4, core % 4
        m = np.zeros((9, 8), dtype=np.float32)
        for j in range(8):
            gj, sj = j // 4, j % 4
            if gj == grp and sj < sp:
                m[8, j] = 0.0
                for i in range(8):
                    gi, si = i // 4, i % 4
                    if gi == grp and sj < si < sp:
                        m[i, j] = 1.0
            else:
                m[8, j] = -BIGNEG
        return m

    in_maps = []
    for core in range(8):
        b_, sp = core // 4, core % 4
        s0 = sp * SPAN
        halo = (x[b_, s0 - 128:s0, :] if sp > 0
                else np.zeros((128, H), dtype=np.float32))
        in_maps.append({
            "x_span": np.ascontiguousarray(x[b_, s0:s0 + SPAN, :]),
            "x_halo": np.ascontiguousarray(halo),
            "w_in": L0_attn_W_in, "conv_w": L0_attn_conv_w, "conv_b": L0_attn_conv_b,
            "dt_bias": L0_attn_dt_bias, "a_log": L0_attn_A_log, "d_in": L0_attn_D,
            "norm_w": L0_attn_norm_w, "w_out": L0_attn_W_out,
            "w_ffn": L0_ffn_weight, "ffn_scale": L0_ffn_scale_weight,
            "ffn_bias": L0_ffn_bias,
            "ident_bf": ident_bf, "ident_f": ident_f, "mask_tril": mask,
            "cmb": cmb_for(core),
        })
    for m in in_maps:
        for k in list(m):
            m[k] = np.ascontiguousarray(np.asarray(m[k]))

    res = run_bass_kernel_spmd(nc, in_maps, core_ids=list(range(8)))
    out = np.empty((B, L, H), dtype=np.float32)
    for core in range(8):
        b_, sp = core // 4, core % 4
        out[b_, sp * SPAN:(sp + 1) * SPAN, :] = res.results[core]["out_span"]
    return out
